# version 23
# baseline (speedup 1.0000x reference)
import numpy as np
import ml_dtypes
import concourse.bass as bass
import concourse.mybir as mybir
from concourse.bass_utils import run_bass_kernel_spmd
from concourse.bass_core_rust import IndirectOffsetOnAxis

F32 = mybir.dt.float32
BF16 = mybir.dt.bfloat16
AF = mybir.ActivationFunctionType
ALU = mybir.AluOpType
BF = ml_dtypes.bfloat16

P = 128
CH = 128
NKH, NVH, DK, DV = 8, 16, 64, 64
D = 1024
FF = 4096
NE = 8

SEM_OF_ENG = {"T": 0, "V": 1, "S": 2, "P": 3}
DMA_SEM = {"Y": 4, "S": 5, "P": 6}
NSEM = 7
ENGS = ("T", "V", "S", "P", "Y")


class Res:
    __slots__ = ("w", "r")

    def __init__(self):
        self.w = None
        self.r = {}


class Op:
    __slots__ = ("sem", "val")

    def __init__(self, sem, val):
        self.sem = sem
        self.val = val


class Prog:
    def __init__(self):
        self.q = {e: [] for e in ENGS}
        self.cnt = [0] * NSEM
        self.wm = {e: [0] * NSEM for e in ENGS}

    def emit(self, eng, fn, reads=(), writes=(), dma=False, group_mid=False):
        need = {}

        def need_op(op):
            if op is None:
                return
            if eng == "T" and op.sem == SEM_OF_ENG["T"]:
                return
            v = self.cnt[op.sem] if op.sem >= 4 else op.val
            if need.get(op.sem, 0) < v:
                need[op.sem] = v

        for r in reads:
            need_op(r.w)
        if not group_mid:
            for w in writes:
                need_op(w.w)
                for op in w.r.values():
                    need_op(op)
        wm = self.wm[eng]
        waits = [(s, v) for s, v in sorted(need.items()) if wm[s] < v]
        for s, v in waits:
            wm[s] = v
        if dma:
            sem, amt = DMA_SEM[eng], 16
        else:
            sem, amt = SEM_OF_ENG[eng], 1
        self.cnt[sem] += amt
        op = Op(sem, self.cnt[sem])
        self.q[eng].append((waits, fn, sem, amt))
        for r in reads:
            r.r[eng] = op
        for w in writes:
            w.w = op
            w.r = {}
        return op

    def barrier(self):
        snap = list(self.cnt)
        for eng in ("T", "V", "S", "P", "Y"):
            wm = self.wm[eng]
            waits = [(s, v) for s, v in enumerate(snap) if v > 0 and wm[s] < v]
            for s, v in waits:
                wm[s] = v
            if eng == "Y":
                self.q[eng].append((waits, lambda e: e.nop(), 4, 0))
            else:
                sem = SEM_OF_ENG[eng]
                self.cnt[sem] += 1
                self.q[eng].append((waits, lambda e: e.nop(), sem, 1))


def build_program(S):
    nchunks = S // CH
    MW = min(512, S)
    nmb = S // MW
    nc = bass.Bass("TRN2", target_bir_lowering=False)
    pr = Prog()

    d_x = nc.dram_tensor("xT", [D, S], F32, kind="ExternalInput")
    d_wqkv = nc.dram_tensor("wqkvT", [16 * 8, P, P], F32, kind="ExternalInput")
    d_wz = nc.dram_tensor("wzT", [8 * 8, P, P], F32, kind="ExternalInput")
    d_wout = nc.dram_tensor("woutT", [8, P, D], F32, kind="ExternalInput")
    d_wb = nc.dram_tensor("wbT", [8, P, NVH], F32, kind="ExternalInput")
    d_wa = nc.dram_tensor("waT", [8, P, NVH], F32, kind="ExternalInput")
    d_conv = nc.dram_tensor("convw", [P, 64], F32, kind="ExternalInput")
    d_dt = nc.dram_tensor("dtb", [NVH, 1], F32, kind="ExternalInput")
    d_alog = nc.dram_tensor("alog", [NVH, 1], F32, kind="ExternalInput")
    d_nw = nc.dram_tensor("normw", [P, 1], F32, kind="ExternalInput")
    d_gate = nc.dram_tensor("gatew", [8, P, NE], F32, kind="ExternalInput")
    d_w1 = nc.dram_tensor("w1p", [NE * 32, P, 8 * P], F32, kind="ExternalInput")
    d_b1 = nc.dram_tensor("b1p", [NE, P, 32], F32, kind="ExternalInput")
    d_w2 = nc.dram_tensor("w2p", [NE * 32, P, D], F32, kind="ExternalInput")
    d_b2 = nc.dram_tensor("b2p", [P, 64], F32, kind="ExternalInput")
    d_identf = nc.dram_tensor("identf", [P, P], F32, kind="ExternalInput")
    d_ident16 = nc.dram_tensor("ident16", [16, 16], F32, kind="ExternalInput")
    d_maskL = nc.dram_tensor("maskL", [P, P], F32, kind="ExternalInput")
    d_maskUs = nc.dram_tensor("maskUs", [P, P], F32, kind="ExternalInput")
    d_maskUi = nc.dram_tensor("maskUi", [P, P], F32, kind="ExternalInput")
    d_hm2 = nc.dram_tensor("hm2", [P, 2], F32, kind="ExternalInput")
    d_ex2 = nc.dram_tensor("ex2", [2, P], F32, kind="ExternalInput")
    d_ones = nc.dram_tensor("onesin", [P, P], F32, kind="ExternalInput")
    d_cvals = nc.dram_tensor("cvals", [P, 8], F32, kind="ExternalInput")
    d_onescol = nc.dram_tensor("onescol", [P, 1], F32, kind="ExternalInput")
    d_iotac = nc.dram_tensor("iotac", [P, 1], F32, kind="ExternalInput")
    d_out = nc.dram_tensor("outT", [D, S], F32, kind="ExternalOutput")
    d_wqkv_bf = nc.dram_tensor("wqkv_bfi", [16 * 8, P, P], BF16, kind="Internal")
    d_wz_bf = nc.dram_tensor("wz_bfi", [8 * 8, P, P], BF16, kind="Internal")
    d_w1bf = nc.dram_tensor("w1bfi", [NE * 32, P, 8 * P], BF16, kind="Internal")
    d_w2bf = nc.dram_tensor("w2bfi", [NE * 32, P, D], BF16, kind="Internal")

    ctx = []

    def sbt(name, shape, dtype):
        t = nc.sbuf_tensor(name, shape, dtype)
        h = t.__enter__()
        ctx.append(t)
        return h

    def psb(name, shape, dtype):
        t = nc.psum_tensor(name, shape, dtype)
        h = t.__enter__()
        ctx.append(t)
        return h

    AR_COLS = 28744
    arena = sbt("arena", [P, AR_COLS], BF16)
    _off = [0]

    def acarve(n):
        o = _off[0]
        _off[0] = o + n
        assert _off[0] <= AR_COLS, f"arena overflow {_off[0]}"
        return arena[:, o:o + n]

    qkv = acarve(16 * 2 * CH)
    szr = acarve(8 * 2 * CH)
    xbfr = acarve(8 * 2 * CH)
    wqbuf = acarve(2 * 8 * P)
    wzbuf = acarve(2 * 8 * P)
    kcm = acarve(2 * 512)
    vcm = acarve(2 * 1024)
    sqt = acarve(1024)
    hnbf = acarve(1024)
    mixsb = acarve(2 * CH)
    mixtail = acarve(16 * 4)
    Upow = acarve(4 * 2 * P)
    Wpow = acarve(4 * 7 * P)
    Xch = acarve(4 * 2 * DV)
    Rp = acarve(4 * DV)
    Dtl = acarve(4 * DV)
    gam1 = acarve(4 * P)
    gam2 = acarve(4 * P)
    gam3 = acarve(4 * P)
    nkkL = acarve(2 * P)
    nkkU = acarve(2 * P)
    kqI = acarve(2 * P)
    Sbf = acarve(NVH * DV)
    obuf = acarve(2 * 1024)
    scan_end = _off[0]
    srepbf = acarve(8)
    excl_end = _off[0]
    _off[0] = 0
    hmid = acarve(8 * S)
    w1buf = acarve(2 * 8 * P)
    w2buf = acarve(8 * D)
    ybft = acarve(MW)
    ln2sq = acarve(MW)
    moe_end = _off[0]
    assert moe_end <= AR_COLS
    assert max(excl_end, moe_end) <= AR_COLS, (excl_end, moe_end)

    X = sbt("Xs", [P, 8 * S], F32)
    x1bf = sbt("x1bfs", [P, 8 * S], BF16)
    wout = sbt("wouts", [P, 8 * D], BF16)
    convw = sbt("convws", [P, 64], F32)
    convbuf = sbt("convbufs", [P, 2 * CH], F32)
    convtmp = sbt("convtmps", [P, CH], F32)
    thbuf = sbt("thbufs", [P, 2 * CH], F32)
    ubuf = sbt("ubufs", [NVH, 2 * CH], F32)
    gbuf = sbt("gbufs", [NVH, 2 * CH], F32)
    hbuf = sbt("hbufs", [NVH, 2 * CH], F32)
    gl1p = sbt("gl1ps", [NVH, CH], F32)
    gsA = sbt("gsAs", [NVH, CH], F32)
    gsB = sbt("gsBs", [NVH, CH], F32)
    gsC = sbt("gsCs", [NVH, CH], F32)
    dtcol = sbt("dtcols", [NVH, 1], F32)
    negea = sbt("negeas", [NVH, 1], F32)
    alogc = sbt("alogcs", [NVH, 1], F32)
    gt = sbt("gts", [P, 2 * 16], F32)
    ht = sbt("hts", [P, 2 * 16], F32)
    negbt = sbt("negbts", [P, 2 * 16], F32)
    expgt = sbt("expgts", [P, 2 * 16], F32)
    creps = sbt("crepss", [P, 4 * P], F32)
    tmpc = sbt("tmpcs", [P, 8 * P], F32)
    bccol = sbt("bccols", [DK, 4], F32)
    scol = sbt("scols", [P, 4], F32)
    t3 = sbt("t3s", [P, 4 * DV], F32)
    Sf = sbt("Sfs", [DK, NVH * DV], F32)
    htile = sbt("htiles", [P, 8 * P], F32)
    t4 = htile[:, 0:MW] if MW <= 8 * P else None
    assert MW <= 8 * P
    rsq = sbt("rsqs", [2, 512], F32)
    rstd2b = sbt("rstd2bs", [2, 512], BF16)
    rowsb = sbt("rowsbs", [1, 4 * MW], F32)
    m_row = rowsb[0:1, 2 * MW:3 * MW]
    v_row = rowsb[0:1, 3 * MW:4 * MW]
    r_row = rowsb[0:1, 0:2 * MW]
    identbf = sbt("identbfs", [P, P], BF16)
    identf = sbt("identfs", [P, P], F32)
    ident16 = sbt("ident16s", [16, 16], F32)
    maskL = sbt("maskLsb", [P, P], BF16)
    maskUs = sbt("maskUssb", [P, P], BF16)
    maskUi = sbt("maskUisb", [P, P], BF16)
    hm2 = sbt("hm2sb", [P, 2], BF16)
    ex2 = sbt("ex2sb", [2, P], BF16)
    onesbf = sbt("onesbfs", [P, 1], BF16)
    onesf1 = sbt("onesf1s", [1, P], F32)
    nwcol = sbt("nwcols", [P, 1], F32)
    cvals = sbt("cvalss", [P, 8], F32)
    wbT = sbt("wbTsb", [P, 8 * NVH], BF16)
    waT = sbt("waTsb", [P, 8 * NVH], BF16)
    gatesb = sbt("gatesbs", [P, 8 * NE], BF16)
    srep = sbt("sreps", [P, 8], F32)
    lcol = sbt("lcols", [NE, 1], F32)
    lrow = sbt("lrows", [1, NE], F32)
    prow = sbt("prows", [1, NE], F32)
    p2row = sbt("p2rows", [1, NE], F32)
    crow = sbt("crows", [1, NE], F32)
    sc1 = sbt("sc1s", [1, 16], F32)
    cvec_c = sbt("cvec_cs", [NE, 1], F32)
    cbc = sbt("cbcs", [P, NE], F32)
    iotac = sbt("iotacs", [P, 1], F32)
    ebase = sbt("ebases", [P, 2], F32)
    idxf = sbt("idxfs", [P, 2], F32)
    idxi = sbt("idxis", [P, 8], mybir.dt.int32)
    cbs = sbt("cbss", [P, 2], F32)
    b1g = sbt("b1gs", [P, 2 * 32], F32)
    iorow = sbt("iorows", [1, NE], F32)
    idxt = sbt("idxts", [P, 8], F32)
    b2comb = sbt("b2combs", [P, 8], F32)
    b1sb = sbt("b1sbs", [P, NE * 32], F32)
    b2sb = sbt("b2sbs", [P, 64], F32)
    outst = tmpc

    pbk = [psb(f"pbk{i}", [P, 512], F32) for i in range(7)]
    pbf = psb("pbkbf", [P, 1024], BF16)
    RB = [Res() for _ in range(8)]

    rX = [[Res() for _ in range(nchunks)] for _ in range(8)]
    rqkv = [[Res(), Res()] for _ in range(16)]
    rszr = [[Res(), Res()] for _ in range(8)]
    rxbfr = [[Res(), Res()] for _ in range(8)]
    rwq = [Res(), Res()]
    rwz = [Res(), Res()]
    rmix = [Res(), Res()]
    rconvb = [Res(), Res()]
    rcvt = Res()
    rth = [Res(), Res()]
    rmt = [Res() for _ in range(16)]
    rwout = Res()
    rconsts = Res()
    rmask = Res()
    rub = [Res(), Res()]
    rgb = [Res(), Res()]
    rhb = [Res(), Res()]
    rl1p = Res()
    rsA = Res()
    rsB = Res()
    rsC = Res()
    rgt = [Res(), Res()]
    rht = [Res(), Res()]
    rnegbt = [Res(), Res()]
    rexpgt = [Res(), Res()]
    rbcg = Res()
    rbch = Res()
    rgrowd = [Res(), Res()]
    rhrowd = [Res(), Res()]
    rtmpc = [[Res(), Res()] for _ in range(4)]
    rgam = [[Res(), Res(), Res()] for _ in range(4)]
    rgam3 = [Res() for _ in range(4)]
    rkkm = [[Res(), Res(), Res()] for _ in range(2)]
    rUp = [[Res(), Res()] for _ in range(4)]
    rWp = [[Res() for _ in range(7)] for _ in range(4)]
    rXch = [[Res(), Res()] for _ in range(4)]
    rRp = [Res() for _ in range(4)]
    rDtl = [Res() for _ in range(4)]
    rt3 = [Res() for _ in range(4)]
    rt4 = Res()
    rbccol = [Res() for _ in range(4)]
    rscol = [Res() for _ in range(4)]
    rbcg2 = [Res(), Res()]
    rbch2 = [Res(), Res()]
    rkcm = [Res(), Res()]
    rvcm = [Res(), Res()]
    robuf = [Res(), Res()]
    rSf = [Res() for _ in range(NVH)]
    rSbf = [Res() for _ in range(NVH)]
    rhtile = Res()
    rsqt = Res()
    rrsq = Res()
    rrstd = Res()
    rhn = Res()
    rrows = Res()
    rrouter = Res()
    rw1 = [Res(), Res()]
    ridx = [Res() for _ in range(8)]
    rw2 = [Res() for _ in range(8)]
    rhmid = [[Res() for _ in range(4)] for _ in range(8)]
    rybft = Res()
    rln2sq = Res()
    routst = [Res(), Res()]
    rstage = Res()
    rwqd = Res()
    rwzd = Res()
    rx1bf = [[Res() for _ in range(nchunks)] for _ in range(8)]

    def dma(eng, dst, src, w, rr=()):
        return pr.emit(eng, lambda e, d=dst, s=src: e.dma_start(d, s),
                       reads=tuple(rr), writes=(w,), dma=True)

    def mm(bres, out, lhs, rhs, start, stop, reads):
        pr.emit("T", lambda e, o=out, l=lhs, r=rhs, s=start, q=stop: e.matmul(o, l, r, start=s, stop=q),
                reads=tuple(reads), writes=(bres,), group_mid=not start)

    def trans(bres, out, in_, ident, reads):
        pr.emit("T", lambda e, o=out, i=in_, d=ident: e.transpose(o, i, d),
                reads=tuple(reads), writes=(bres,))

    def act(eng, out, in_, func, bias, scale, reads, writes):
        pr.emit(eng, lambda e, o=out, i=in_, f=func, b=bias, s=scale: e.activation(o, i, f, bias=b, scale=s),
                reads=tuple(reads) + (rcval,), writes=tuple(writes))

    def cp(eng, out, in_, reads, writes):
        if eng == "S":
            pr.emit(eng, lambda e, o=out, i=in_: e.copy(o, i), reads=tuple(reads), writes=tuple(writes))
        else:
            pr.emit(eng, lambda e, o=out, i=in_: e.tensor_copy(o, i), reads=tuple(reads), writes=tuple(writes))

    def ts1(eng, out, in0, s1, op0, reads, writes):
        pr.emit(eng, lambda e, o=out, i=in0, a=s1, x=op0: e.tensor_single_scalar(o, i, a, x),
                reads=tuple(reads), writes=tuple(writes))

    def ts2(eng, out, in0, s1, s2, op0, op1, reads, writes):
        pr.emit(eng, lambda e, o=out, i=in0, a=s1, b=s2, x=op0, y=op1: e.tensor_scalar(o, i, a, b, x, y),
                reads=tuple(reads), writes=tuple(writes))

    def tt(eng, out, in0, in1, op, reads, writes):
        pr.emit(eng, lambda e, o=out, a=in0, b=in1, x=op: e.tensor_tensor(o, a, b, x),
                reads=tuple(reads), writes=tuple(writes))

    def stt(eng, out, in0, s, in1, op0, op1, reads, writes):
        pr.emit(eng, lambda e, o=out, a=in0, c=s, b=in1, x=op0, y=op1: e.scalar_tensor_tensor(o, a, c, b, x, y),
                reads=tuple(reads), writes=tuple(writes))

    rcval = Res()
    CONST_VALS = [1.0, 1e-6, 1e-5, 0.0, 0.0, 0.0, 0.0, 0.0]
    dma("Y", cvals[:], d_cvals.ap(), rcval)
    for i, v in enumerate(CONST_VALS[:3]):
        nc.const_aps.register(F32, v, cvals[:, i:i + 1])
    dma("Y", ident16[:], d_ident16.ap(), rconsts)
    dma("Y", convw[:], d_conv.ap(), rconsts)
    dma("Y", dtcol[:], d_dt.ap(), rconsts)
    dma("Y", alogc[:], d_alog.ap(), rconsts)
    dma("Y", nwcol[:], d_nw.ap(), rconsts)
    dma("Y", b1sb[:].rearrange("p (e c) -> p e c", c=32), d_b1.ap().rearrange("e p c -> p e c"), rconsts)
    dma("Y", b2sb[:], d_b2.ap(), rconsts)
    dma("Y", onesf1[:], d_ones.ap()[0:1, :], rconsts)
    dma("Y", iotac[:], d_iotac.ap(), rconsts)
    dma("P", identbf[:], d_identf.ap(), rmask)
    dma("Y", identf[:], d_identf.ap(), rconsts)
    dma("P", maskL[:], d_maskL.ap(), rmask)
    dma("P", maskUs[:], d_maskUs.ap(), rmask)
    dma("P", maskUi[:], d_maskUi.ap(), rmask)
    dma("P", hm2[:], d_hm2.ap(), rmask)
    dma("P", ex2[:], d_ex2.ap(), rmask)
    dma("P", onesbf[:], d_onescol.ap(), rmask)
    dma("P", wbT[:].rearrange("p (k c) -> p k c", c=NVH), d_wb.ap().rearrange("k p c -> p k c"), rconsts)
    dma("P", waT[:].rearrange("p (k c) -> p k c", c=NVH), d_wa.ap().rearrange("k p c -> p k c"), rconsts)
    dma("P", gatesb[:].rearrange("p (k c) -> p k c", c=NE), d_gate.ap().rearrange("k p c -> p k c"), rconsts)
    dma("P", wout[:].rearrange("p (k c) -> p k c", c=D), d_wout.ap().rearrange("k p c -> p k c"), rwout)
    ts1("V", wout[:], wout[:], nwcol[:], ALU.mult, (rconsts, rwout), (rwout,))
    act("S", negea[:], alogc[:], AF.Exp, 0.0, 1.0, (rconsts,), (rconsts,))
    ts1("V", negea[:], negea[:], -1.0, ALU.mult, (rconsts,), (rconsts,))
    pr.emit("V", lambda e: e.memset(Sf[:], 0.0), writes=tuple(rSf))
    pr.emit("V", lambda e: e.memset(Sbf[:, :], 0.0), writes=tuple(rSbf))

    for kc in range(8):
        o = dma("Y", X[:, kc * S:(kc + 1) * S], d_x.ap()[kc * P:(kc + 1) * P, :], rX[kc][0])
        for c in range(nchunks):
            rX[kc][c].w = o

    dma("P", d_wqkv_bf.ap(), d_wqkv.ap(), rwqd)
    dma("P", d_wz_bf.ap(), d_wz.ap(), rwzd)
    rw1d = Res()
    rw2d = Res()
    for i in range(8):
        dma("P", d_w1bf.ap()[i * 32:(i + 1) * 32], d_w1.ap()[i * 32:(i + 1) * 32], rw1d)
        dma("P", d_w2bf.ap()[i * 32:(i + 1) * 32], d_w2.ap()[i * 32:(i + 1) * 32], rw2d)

    def p1_block(c):
        par = c % 2
        t0 = c * CH
        for kc in range(8):
            cp("S", xbfr[:, (kc * 2 + par) * CH:(kc * 2 + par + 1) * CH],
               X[:, kc * S + t0: kc * S + t0 + CH], (rX[kc][c],), (rxbfr[kc][par],))
            yield
        ccseq = [4, 5, 6, 7, 8, 9, 10, 11, 12, 13, 14, 15, 0, 1, 2, 3]
        for ci in range(16):
            cc = ccseq[ci]
            ws = ci % 2
            if ws == 0:
                for w2_ in range(2):
                    dma("Y", wqbuf[:, w2_ * 8 * P:(w2_ + 1) * 8 * P].rearrange("p (a c) -> p a c", c=P),
                        d_wqkv_bf.ap()[ccseq[ci + w2_] * 8:(ccseq[ci + w2_] + 1) * 8].rearrange("a p c -> p a c"),
                        rwq[w2_], (rwqd,))
                    yield
            bank = ci % 2
            for k in range(8):
                mm(RB[bank], pbk[bank][:, 0:CH],
                   wqbuf[:, (ws * 8 + k) * P:(ws * 8 + k + 1) * P],
                   xbfr[:, (k * 2 + par) * CH:(k * 2 + par + 1) * CH],
                   k == 0, k == 7, (rwq[0], rwq[1], rxbfr[k][par]))
                yield
            mxs = mixsb[:, par * CH:(par + 1) * CH]
            cp("V", mxs, pbk[bank][:, 0:CH], (RB[bank],), (rmix[par], RB[bank]))
            yield
            cb = convbuf[:, par * CH:(par + 1) * CH]

            def wbc(j, wdt, cc=cc):
                return bass.AP(convw, cc * 4 + j, [[64, P], [0, wdt]])
            if cc % 2 == 0:
                tt("P", cb, mxs, wbc(3, CH), ALU.mult, (rmix[par], rconsts), (rconvb[par],))
                yield
                for j in (2, 1, 0):
                    sh = 3 - j
                    tt("P", convtmp[:, 0:CH - sh], mxs[:, 0:CH - sh], wbc(j, CH - sh), ALU.mult,
                       (rmix[par], rconsts), (rcvt,))
                    yield
                    tt("P", cb[:, sh:CH], cb[:, sh:CH], convtmp[:, 0:CH - sh], ALU.add,
                       (rcvt,), (rconvb[par],))
                    yield
                    if c > 0:
                        stt("V", cb[:, 0:sh], mixtail[:, cc * 4 + 3 - sh:cc * 4 + 3],
                            convw[:, cc * 4 + j:cc * 4 + j + 1], cb[:, 0:sh],
                            ALU.mult, ALU.add, (rmt[cc], rconsts), (rconvb[par],))
                        yield
                cp("P", mixtail[:, cc * 4:cc * 4 + 3], mxs[:, CH - 3:CH], (rmix[par],), (rmt[cc],))
                yield
            else:
                ts1("V", cb, mxs, convw[:, cc * 4 + 3:cc * 4 + 4], ALU.mult, (rmix[par], rconsts), (rconvb[par],))
                yield
                for j in (2, 1, 0):
                    sh = 3 - j
                    wj = convw[:, cc * 4 + j:cc * 4 + j + 1]
                    stt("V", cb[:, sh:CH], mxs[:, 0:CH - sh], wj, cb[:, sh:CH], ALU.mult, ALU.add,
                        (rmix[par], rconsts), (rconvb[par],))
                    yield
                    if c > 0:
                        stt("V", cb[:, 0:sh], mixtail[:, cc * 4 + 3 - sh:cc * 4 + 3], wj, cb[:, 0:sh],
                            ALU.mult, ALU.add, (rmt[cc], rconsts), (rconvb[par],))
                        yield
                cp("V", mixtail[:, cc * 4:cc * 4 + 3], mxs[:, CH - 3:CH], (rmix[par],), (rmt[cc],))
                yield
            thb = thbuf[:, par * CH:(par + 1) * CH]
            act("S", thb, cb, AF.Tanh, 0.0, 0.5, (rconvb[par],), (rth[par],))
            yield
            ts2("V", thb, thb, 1.0, 0.5, ALU.add, ALU.mult, (rth[par],), (rth[par],))
            yield
            tt("P", qkv[:, (cc * 2 + par) * CH:(cc * 2 + par + 1) * CH], cb, thb, ALU.mult,
               (rconvb[par], rth[par]), (rqkv[cc][par],))
            yield
        for zc in range(8):
            ws = zc % 2
            if ws == 0:
                for w2_ in range(2):
                    dma("Y", wzbuf[:, w2_ * 8 * P:(w2_ + 1) * 8 * P].rearrange("p (a c) -> p a c", c=P),
                        d_wz_bf.ap()[(zc + w2_) * 8:(zc + w2_ + 1) * 8].rearrange("a p c -> p a c"),
                        rwz[w2_], (rwzd,))
                    yield
            bank = zc % 2
            for k in range(8):
                mm(RB[bank], pbk[bank][:, 0:CH],
                   wzbuf[:, (ws * 8 + k) * P:(ws * 8 + k + 1) * P],
                   xbfr[:, (k * 2 + par) * CH:(k * 2 + par + 1) * CH],
                   k == 0, k == 7, (rwz[0], rwz[1], rxbfr[k][par]))
                yield
            thb = thbuf[:, par * CH:(par + 1) * CH]
            act("S", thb, pbk[bank][:, 0:CH], AF.Tanh, 0.0, 0.5, (RB[bank],), (rth[par],))
            yield
            ts2("V", thb, thb, 1.0, 0.5, ALU.add, ALU.mult, (rth[par],), (rth[par],))
            yield
            tt("V", szr[:, (zc * 2 + par) * CH:(zc * 2 + par + 1) * CH], pbk[bank][:, 0:CH], thb,
               ALU.mult, (RB[bank], rth[par]), (rszr[zc][par], RB[bank]))
            yield

    def p1_gh(c):
        par = c % 2
        for k in range(8):
            mm(RB[6], pbk[6][0:16, 0:CH], wbT[:, k * NVH:(k + 1) * NVH],
               xbfr[:, (k * 2 + par) * CH:(k * 2 + par + 1) * CH],
               k == 0, k == 7, (rconsts, rxbfr[k][par]))
        act("S", gsB[:], pbk[6][0:16, 0:CH], AF.Abs, 0.0, 1.0, (RB[6],), (rsB,))
        ts1("V", gsA[:], pbk[6][0:16, 0:CH], 0.0, ALU.min, (RB[6],), (rsA, RB[6]))
        act("S", gsB[:], gsB[:], AF.Exp, 0.0, -1.0, (rsB,), (rsB,))
        act("S", gsC[:], gsB[:], AF.Ln, 1.0, 1.0, (rsB,), (rsC,))
        tt("V", gsC[:], gsA[:], gsC[:], ALU.subtract, (rsA, rsC), (rsC,))
        for k in range(8):
            mm(RB[6], pbk[6][0:16, 0:CH], waT[:, k * NVH:(k + 1) * NVH],
               xbfr[:, (k * 2 + par) * CH:(k * 2 + par + 1) * CH],
               k == 0, k == 7, (rconsts, rxbfr[k][par]))
        ub = ubuf[:, par * CH:(par + 1) * CH]
        gb = gbuf[:, par * CH:(par + 1) * CH]
        hb = hbuf[:, par * CH:(par + 1) * CH]
        ts1("V", ub[0:16, :], pbk[6][0:16, 0:CH], dtcol[:], ALU.add, (RB[6], rconsts), (rub[par], RB[6]))
        act("S", gsA[:], ub[0:16, :], AF.Abs, 0.0, 1.0, (rub[par],), (rsA,))
        act("S", gsA[:], gsA[:], AF.Exp, 0.0, -1.0, (rsA,), (rsA,))
        act("S", gl1p[:], gsA[:], AF.Ln, 1.0, 1.0, (rsA,), (rl1p,))
        ts1("V", gsA[:], ub[0:16, :], 0.0, ALU.max, (rub[par],), (rsA,))
        tt("V", gsA[:], gsA[:], gl1p[:], ALU.add, (rsA, rl1p), (rsA,))
        ts1("V", gb[0:16, :], gsA[:], negea[:], ALU.mult, (rsA, rconsts), (rgb[par],))
        srcv, srcr = gb[0:16, :], rgb[par]
        dstv, dstr = gsA[:], rsA
        s = 1
        while s < CH:
            tt("V", dstv[:, s:CH], srcv[:, s:CH], srcv[:, 0:CH - s], ALU.add, (srcr,), (dstr,))
            cp("V", dstv[:, 0:s], srcv[:, 0:s], (srcr,), (dstr,))
            srcv, dstv = dstv, srcv
            srcr, dstr = dstr, srcr
            s *= 2
        cp("V", gb[0:16, :], gsA[:], (rsA,), (rgb[par],))
        tt("V", hb[0:16, :], gb[0:16, :], gsC[:], ALU.subtract, (rgb[par], rsC), (rhb[par],))
        trans(RB[6], pbk[6][:, 0:16], gb[0:16, :], ident16[:], (rgb[par], rconsts))
        cp("V", gt[:, par * 16:(par + 1) * 16], pbk[6][:, 0:16], (RB[6],), (rgt[par], RB[6]))
        trans(RB[6], pbk[6][:, 16:32], hb[0:16, :], ident16[:], (rhb[par], rconsts))
        cp("V", ht[:, par * 16:(par + 1) * 16], pbk[6][:, 16:32], (RB[6],), (rht[par], RB[6]))
        act("S", expgt[:, par * 16:(par + 1) * 16], gt[:, par * 16:(par + 1) * 16], AF.Exp, 0.0, 1.0,
            (rgt[par],), (rexpgt[par],))
        ts1("V", negbt[:, par * 16:(par + 1) * 16], expgt[:, par * 16:(par + 1) * 16], -1.0, ALU.mult,
            (rexpgt[par],), (rnegbt[par],))


    def p1_kv_trans(c):
        par = c % 2
        for i in range(4):
            cc = 4 + i
            trans(RB[7], pbf[:, 0:P], qkv[:, (cc * 2 + par) * CH:(cc * 2 + par + 1) * CH], identbf[:],
                  (rqkv[cc][par], rmask))
            yield
            cp("V", kcm[:, par * 512 + i * P: par * 512 + (i + 1) * P], pbf[:, 0:P], (RB[7],), (rkcm[par], RB[7]))
            yield
        for i in range(8):
            cc = 8 + i
            trans(RB[7], pbf[:, 0:P], qkv[:, (cc * 2 + par) * CH:(cc * 2 + par + 1) * CH], identbf[:],
                  (rqkv[cc][par], rmask))
            yield
            cp("S" if i % 2 else "V", vcm[:, par * 1024 + i * P: par * 1024 + (i + 1) * P], pbf[:, 0:P],
               (RB[7],), (rvcm[par], RB[7]))
            yield

    def scan_chunk(c, bg_gen):
        par = c % 2
        def scan_head(h, par):
            hp = h % 2
            h4 = h % 4
            kvp = (h // 2) % 2
            qq = (h // 4) % 2
            kh = h // 2
            gcol = gt[:, par * 16 + h:par * 16 + h + 1]
            hcol = ht[:, par * 16 + h:par * 16 + h + 1]
            Q = qkv[:, ((kh // 2) * 2 + par) * CH:((kh // 2) * 2 + par + 1) * CH][(kh % 2) * 64:(kh % 2) * 64 + 64, :]
            K = qkv[:, ((4 + kh // 2) * 2 + par) * CH:((4 + kh // 2) * 2 + par + 1) * CH][(kh % 2) * 64:(kh % 2) * 64 + 64, :]
            rq = rqkv[kh // 2][par]
            rk = rqkv[4 + kh // 2][par]
            sb_ = pbk[4 + hp]
            sbR = RB[4 + hp]
            B0 = (h4 // 2) * 256
            nkkLs = nkkL[:, kvp * P:(kvp + 1) * P]
            nkkUs = nkkU[:, kvp * P:(kvp + 1) * P]
            kqIs = kqI[:, kvp * P:(kvp + 1) * P]
            if h % 2 == 0:
                kks = pbk[2][:, h4 * P:(h4 + 1) * P]
                kqs = pbk[3][:, h4 * P:(h4 + 1) * P]
                mm(RB[2], kks, K, K, True, True, (rk,))
                yield
                mm(RB[3], kqs, K, Q, True, True, (rk, rq))
                yield
                stt("V", nkkLs, kks, -1.0, maskL[:], ALU.mult, ALU.mult,
                    (RB[2], rmask), (rkkm[kvp][0],))
                yield
                stt("V", nkkUs, kks, -1.0, maskUs[:], ALU.mult, ALU.mult,
                    (RB[2], rmask), (rkkm[kvp][1], RB[2]))
                yield
                stt("V", kqIs, kqs, 1.0, maskUi[:], ALU.mult, ALU.mult,
                    (RB[3], rmask), (rkkm[kvp][2], RB[3]))
                yield
            t1s = tmpc[:, h4 * 2 * P: h4 * 2 * P + P]
            t2s = tmpc[:, h4 * 2 * P + P: h4 * 2 * P + 2 * P]
            g1 = gam1[:, h4 * P:(h4 + 1) * P]
            g2 = gam2[:, h4 * P:(h4 + 1) * P]
            g3 = gam3[:, h4 * P:(h4 + 1) * P]
            crepH = creps[:, h4 * P:(h4 + 1) * P]
            bc1 = pbk[2][:, h4 * P:(h4 + 1) * P]
            bc2 = pbk[3][:, h4 * P:(h4 + 1) * P]
            cp("P", crepH, bass.AP(ht, par * 16 + h, [[32, P], [0, P]]), (rht[par],), (rtmpc[h4][0],))
            yield
            trans(RB[2], bc1, crepH, identf[:], (rtmpc[h4][0], rconsts))
            yield
            ts2("V", t1s, bc1, gcol, -30.0, ALU.subtract, ALU.max,
                (RB[2], rgt[par]), (rtmpc[h4][0], RB[2]))
            yield
            act("S", g1, t1s, AF.Exp, 0.0, -1.0, (rtmpc[h4][0],), (rgam[h4][0],))
            yield
            cp("P", crepH, bass.AP(gt, par * 16 + h, [[32, P], [0, P]]), (rgt[par], rtmpc[h4][0]), (rtmpc[h4][1],))
            yield
            trans(RB[3], bc2, crepH, identf[:], (rtmpc[h4][1], rconsts))
            yield
            act("S", bccol[:, h4:h4 + 1], bc2[0:DK, P - 1:P], AF.Exp, 0.0, 1.0, (RB[3],), (rbccol[h4],))
            yield
            ts2("V", t2s, bc2, hcol, 30.0, ALU.subtract, ALU.min,
                (RB[3], rht[par]), (rtmpc[h4][1], RB[3]))
            yield
            act("S", g2, t2s, AF.Exp, 0.0, 1.0, (rtmpc[h4][1],), (rgam[h4][1],))
            yield
            Up0 = Upow[:, h4 * 2 * P: h4 * 2 * P + P]
            Up1 = Upow[:, h4 * 2 * P + P: h4 * 2 * P + 2 * P]
            Wp = [Wpow[:, (h4 * 7 + lv) * P:(h4 * 7 + lv + 1) * P] for lv in range(7)]
            tt("P", Up0, g1, nkkLs, ALU.mult, (rgam[h4][0], rkkm[kvp][0]), (rUp[h4][0],))
            yield
            tt("P", Wp[0], g2, nkkUs, ALU.mult, (rgam[h4][1], rkkm[kvp][1]), (rWp[h4][0],))
            yield
            tt("P", g3, g2, kqIs, ALU.mult, (rgam[h4][1], rkkm[kvp][2]), (rgam3[h4],))
            yield
            sb0 = (kh % 2) * 64
            mm(sbR, sb_[:, B0:B0 + DV], K, Sbf[sb0:sb0 + DK, h * DV:(h + 1) * DV], True, True, (rk, rSbf[h]))
            yield
            RpS = Rp[:, h4 * DV:(h4 + 1) * DV]
            stt("V", RpS, sb_[:, B0:B0 + DV], negbt[:, par * 16 + h:par * 16 + h + 1],
                vcm[:, par * 1024 + h * DV: par * 1024 + (h + 1) * DV], ALU.mult, ALU.add,
                (sbR, rnegbt[par], rvcm[par]), (rRp[h4], sbR))
            yield
            xsrc, xres = RpS, rRp[h4]
            for lev in range(7):
                mm(sbR, sb_[:, B0 + 64:B0 + 64 + DV], Wp[lev], xsrc, True, True, (rWp[h4][lev], xres))
                yield
                dst = Xch[:, (h4 * 2 + lev % 2) * DV:(h4 * 2 + lev % 2 + 1) * DV]
                tt("V", dst, sb_[:, B0 + 64:B0 + 64 + DV], xsrc, ALU.add,
                   (sbR, xres), (rXch[h4][lev % 2], sbR))
                yield
                xsrc, xres = dst, rXch[h4][lev % 2]
                if lev < 6:
                    sp_ = Up0 if lev % 2 == 0 else Up1
                    dp_ = Up1 if lev % 2 == 0 else Up0
                    rsp, rdp = rUp[h4][lev % 2], rUp[h4][(lev + 1) % 2]
                    sq2 = pbk[2][:, h4 * P:(h4 + 1) * P]
                    sq3 = pbk[3][:, h4 * P:(h4 + 1) * P]
                    if lev < 5:
                        mm(RB[2], sq2, Wp[lev], sp_, True, True, (rsp, rWp[h4][lev]))
                        yield
                        cp("V" if lev % 2 else "S", dp_, sq2, (RB[2],), (rdp, RB[2]))
                        yield
                    mm(RB[3], sq3, sp_, Wp[lev], True, True, (rsp, rWp[h4][lev]))
                    yield
                    cp("S" if lev % 2 else "V", Wp[lev + 1], sq3, (RB[3],), (rWp[h4][lev + 1], RB[3]))
                    yield
            act("S", scol[:, h4:h4 + 1], t2s[:, P - 1:P], AF.Exp, 0.0, 1.0, (rtmpc[h4][1],), (rscol[h4],))
            yield
            DtlS = Dtl[:, h4 * DV:(h4 + 1) * DV]
            ts1("V", DtlS, xsrc, scol[:, h4:h4 + 1], ALU.mult, (xres, rscol[h4]), (rDtl[h4],))
            yield
            mm(sbR, sb_[:, B0 + 128:B0 + 128 + DV], Q, Sbf[sb0:sb0 + DK, h * DV:(h + 1) * DV], True, True,
               (rq, rSbf[h]))
            yield
            t3S = t3[:, h4 * DV:(h4 + 1) * DV]
            ts2("V", t3S, sb_[:, B0 + 128:B0 + 128 + DV], expgt[:, par * 16 + h:par * 16 + h + 1], 0.125,
                ALU.mult, ALU.mult, (sbR, rexpgt[par]), (rt3[h4], sbR))
            yield
            mm(sbR, sb_[:, B0 + 192:B0 + 192 + DV], g3, xsrc, True, True, (rgam3[h4], xres))
            yield
            stt("V", obuf[:, par * 1024 + h * DV: par * 1024 + (h + 1) * DV], sb_[:, B0 + 192:B0 + 192 + DV], 0.125,
                t3S, ALU.mult, ALU.add, (sbR, rt3[h4]), (robuf[par], sbR))
            yield
            mm(sbR, sb_[0:DK, B0:B0 + DV], kcm[:, par * 512 + kh * DK: par * 512 + (kh + 1) * DK],
               DtlS, True, True, (rkcm[par], rDtl[h4]))
            yield
            stt("V", Sf[:, h * DV:(h + 1) * DV], Sf[:, h * DV:(h + 1) * DV], bccol[:, h4:h4 + 1],
                sb_[0:DK, B0:B0 + DV], ALU.mult, ALU.add, (rSf[h], rbccol[h4], sbR), (rSf[h], sbR))
            yield
            cp("P", Sbf[0:DK, h * DV:(h + 1) * DV], Sf[:, h * DV:(h + 1) * DV], (rSf[h],), (rSbf[h],))
            yield
            cp("P", Sbf[DK:P, h * DV:(h + 1) * DV], Sf[:, h * DV:(h + 1) * DV], (rSf[h],), (rSbf[h],))
            yield


        for hp_ in range(4):
            gens = [scan_head(4 * hp_, par), scan_head(4 * hp_ + 1, par),
                    scan_head(4 * hp_ + 2, par), scan_head(4 * hp_ + 3, par)]
            for g_ in gens[:2]:
                for _ in range(8):
                    next(g_)
            live = list(gens)
            while live:
                for g_ in list(live):
                    try:
                        next(g_)
                    except StopIteration:
                        live.remove(g_)
                if bg_gen[0] is not None:
                    try:
                        next(bg_gen[0])
                    except StopIteration:
                        bg_gen[0] = None

    def p45_chunk(c):
        par = c % 2
        t0 = c * CH
        for half in range(2):
            for fc4 in range(4):
                fc = half * 4 + fc4
                trans(RB[7], pbf[:, 0:P], obuf[:, par * 1024 + fc * P: par * 1024 + (fc + 1) * P],
                      identbf[:], (robuf[par], rmask))
                tt("V", htile[:, fc * P:(fc + 1) * P], pbf[:, 0:P],
                   szr[:, (fc * 2 + par) * CH:(fc * 2 + par + 1) * CH], ALU.mult,
                   (RB[7], rszr[fc][par]), (rhtile, RB[7]))
                act("S", sqt[:, fc * P:(fc + 1) * P], htile[:, fc * P:(fc + 1) * P], AF.Square, 0.0, 1.0,
                    (rhtile,), (rsqt,))
                mm(RB[6], pbk[6][0:2, fc4 * P:(fc4 + 1) * P], hm2[:], sqt[:, fc * P:(fc + 1) * P],
                   True, True, (rmask, rsqt))
            cp("V", rsq[0:2, 0:512], pbk[6][0:2, 0:512], (RB[6],), (rrsq, RB[6]))
            act("S", rsq[0:2, :], rsq[0:2, :], AF.Ln, 1e-6, 1.0 / DV, (rrsq,), (rrstd,))
            act("S", rsq[0:2, :], rsq[0:2, :], AF.Exp, 0.0, -0.5, (rrstd,), (rrstd,))
            cp("V", rstd2b[0:2, :], rsq[0:2, :], (rrstd,), (rrstd,))
            for fc4 in range(4):
                fc = half * 4 + fc4
                bank = fc % 2
                mm(RB[bank], pbk[bank][:, 0:P], ex2[:], rstd2b[0:2, fc4 * P:(fc4 + 1) * P], True, True,
                   (rmask, rrstd))
                tt("V", hnbf[:, fc * P:(fc + 1) * P], htile[:, fc * P:(fc + 1) * P], pbk[bank][:, 0:P],
                   ALU.mult, (rhtile, RB[bank]), (rhn, RB[bank]))
        for hc in range(8):
            bank = hc % 2
            for kc in range(8):
                mm(RB[bank], pbk[bank][:, 0:P], wout[:, kc * D + hc * P: kc * D + (hc + 1) * P],
                   hnbf[:, kc * P:(kc + 1) * P], kc == 0, kc == 7, (rwout, rhn))
            tt("V", X[:, hc * S + t0: hc * S + t0 + CH], X[:, hc * S + t0: hc * S + t0 + CH],
               pbk[bank][:, 0:P], ALU.add, (rX[hc][c], RB[bank]), (rX[hc][c], RB[bank]))
            cp("V", sqt[:, hc * P:(hc + 1) * P],
               X[:, hc * S + t0: hc * S + t0 + CH], (rX[hc][c],), (rsqt,))
        for hc in range(8):
            eng_ = "S" if hc % 2 else "V"
            if eng_ == "S":
                act("S", hnbf[:, hc * P:(hc + 1) * P], sqt[:, hc * P:(hc + 1) * P], AF.Square, 0.0, 1.0,
                    (rsqt,), (rhn,))
            else:
                tt("V", hnbf[:, hc * P:(hc + 1) * P], sqt[:, hc * P:(hc + 1) * P],
                   sqt[:, hc * P:(hc + 1) * P], ALU.mult, (rsqt,), (rhn,))
        for hc in range(8):
            mm(RB[6], pbk[6][0:1, 0:P], onesbf[:], sqt[:, hc * P:(hc + 1) * P], hc == 0, hc == 7,
               (rsqt,))
        ts1("V", m_row[0:1, 0:P], pbk[6][0:1, 0:P], 1.0 / D, ALU.mult, (RB[6],), (rrows, RB[6]))
        for hc in range(8):
            mm(RB[6], pbk[6][0:1, 0:P], onesbf[:], hnbf[:, hc * P:(hc + 1) * P], hc == 0, hc == 7,
               (rhn,))
        ts1("V", v_row[0:1, 0:P], pbk[6][0:1, 0:P], 1.0 / D, ALU.mult, (RB[6],), (rrows, RB[6]))
        tt("V", r_row[0:1, 0:P], m_row[0:1, 0:P], m_row[0:1, 0:P], ALU.mult, (rrows,), (rrows,))
        tt("V", v_row[0:1, 0:P], v_row[0:1, 0:P], r_row[0:1, 0:P], ALU.subtract, (rrows,), (rrows,))
        act("S", r_row[0:1, 0:P], v_row[0:1, 0:P], AF.Ln, 1e-5, 1.0, (rrows,), (rrows,))
        act("S", r_row[0:1, 0:P], r_row[0:1, 0:P], AF.Exp, 0.0, -0.5, (rrows,), (rrows,))
        tt("V", r_row[0:1, P:2 * P], m_row[0:1, 0:P], r_row[0:1, 0:P], ALU.mult, (rrows,), (rrows,))
        mm(RB[6], pbk[6][:, 0:2 * P], onesf1[:], r_row[0:1, 0:2 * P], True, True, (rconsts, rrows))
        for hc in range(8):
            tt("V", t4[:, 0:P], X[:, hc * S + t0: hc * S + t0 + CH], pbk[6][:, 0:P], ALU.mult,
               (rX[hc][c], RB[6]), (rt4,))
            tt("V", X[:, hc * S + t0: hc * S + t0 + CH], t4[:, 0:P], pbk[6][:, P:2 * P], ALU.subtract,
               (rt4, RB[6]), (rX[hc][c],))
            cp("S", x1bf[:, hc * S + t0: hc * S + t0 + CH], X[:, hc * S + t0: hc * S + t0 + CH],
               (rX[hc][c],), (rx1bf[hc][c],))

    def router():
        allx = [rX[kc][c] for kc in range(8) for c in range(nchunks)]
        for kc in range(8):
            pr.emit("V", lambda e, kc=kc: e.reduce_sum(srep[:, kc:kc + 1], X[:, kc * S:(kc + 1) * S],
                                                       axis=mybir.AxisListType.X),
                    reads=tuple(rX[kc]), writes=(rrouter,))
        ts1("V", srep[:], srep[:], 1.0 / S, ALU.mult, (rrouter,), (rrouter,))
        cp("V", srepbf[:, 0:8], srep[:], (rrouter,), (rrouter,))
        for kc in range(8):
            mm(RB[6], pbk[6][0:NE, 0:1], gatesb[:, kc * NE:(kc + 1) * NE], srepbf[:, kc:kc + 1],
               kc == 0, kc == 7, (rconsts, rrouter))
        cp("V", lcol[:], pbk[6][0:NE, 0:1], (RB[6],), (rrouter, RB[6]))
        trans(RB[6], pbk[6][0:1, 0:NE], lcol[:], ident16[0:NE, 0:NE], (rrouter, rconsts))
        cp("V", lrow[:], pbk[6][0:1, 0:NE], (RB[6],), (rrouter, RB[6]))
        pr.emit("V", lambda e: e.reduce_max(sc1[0:1, 0:1], lrow[:], axis=mybir.AxisListType.X),
                reads=(rrouter,), writes=(rrouter,))
        ts1("V", sc1[0:1, 1:2], sc1[0:1, 0:1], -1.0, ALU.mult, (rrouter,), (rrouter,))
        act("S", prow[:], lrow[:], AF.Exp, sc1[0:1, 1:2], 1.0, (rrouter,), (rrouter,))
        pr.emit("V", lambda e: e.reduce_sum(sc1[0:1, 2:3], prow[:], axis=mybir.AxisListType.X),
                reads=(rrouter,), writes=(rrouter,))
        pr.emit("V", lambda e: e.reciprocal(sc1[0:1, 3:4], sc1[0:1, 2:3]),
                reads=(rrouter,), writes=(rrouter,))
        ts1("V", prow[:], prow[:], sc1[0:1, 3:4], ALU.mult, (rrouter,), (rrouter,))
        pr.emit("V", lambda e: e.reduce_max(sc1[0:1, 4:5], prow[:], axis=mybir.AxisListType.X),
                reads=(rrouter,), writes=(rrouter,))
        ts1("V", sc1[0:1, 8:9], sc1[0:1, 4:5], -1.0, ALU.mult, (rrouter,), (rrouter,))
        ts1("V", p2row[:], prow[:], sc1[0:1, 8:9], ALU.add, (rrouter,), (rrouter,))
        pr.emit("S", lambda e: e.sign(p2row[:], p2row[:]), reads=(rrouter,), writes=(rrouter,))
        ts1("V", p2row[:], p2row[:], 1.0, ALU.add, (rrouter,), (rrouter,))
        stt("V", crow[:], p2row[:], -2.0, prow[:], ALU.mult, ALU.add, (rrouter,), (rrouter,))
        pr.emit("V", lambda e: e.reduce_max(sc1[0:1, 5:6], crow[:], axis=mybir.AxisListType.X),
                reads=(rrouter,), writes=(rrouter,))
        ts1("V", sc1[0:1, 9:10], sc1[0:1, 5:6], -1.0, ALU.mult, (rrouter,), (rrouter,))
        ts1("V", crow[:], crow[:], sc1[0:1, 9:10], ALU.add, (rrouter,), (rrouter,))
        pr.emit("S", lambda e: e.sign(crow[:], crow[:]), reads=(rrouter,), writes=(rrouter,))
        ts1("V", crow[:], crow[:], 1.0, ALU.add, (rrouter,), (rrouter,))
        tt("V", p2row[:], p2row[:], crow[:], ALU.add, (rrouter,), (rrouter,))
        tt("V", crow[:], prow[:], p2row[:], ALU.mult, (rrouter,), (rrouter,))
        pr.emit("V", lambda e: e.reduce_sum(sc1[0:1, 6:7], crow[:], axis=mybir.AxisListType.X),
                reads=(rrouter,), writes=(rrouter,))
        pr.emit("V", lambda e: e.reciprocal(sc1[0:1, 7:8], sc1[0:1, 6:7]),
                reads=(rrouter,), writes=(rrouter,))
        ts1("V", crow[:], crow[:], sc1[0:1, 7:8], ALU.mult, (rrouter,), (rrouter,))
        mm(RB[6], pbk[6][:, 0:NE], onesf1[:], crow[:], True, True, (rconsts, rrouter))
        cp("V", cbc[:], pbk[6][:, 0:NE], (RB[6],), (rrouter, RB[6]))
        trans(RB[6], pbk[6][0:NE, 32:33], crow[:], ident16[0:1, 0:1], (rrouter, rconsts))
        cp("V", cvec_c[:], pbk[6][0:NE, 32:33], (RB[6],), (rrouter, RB[6]))
        ts1("V", lrow[:], prow[:], sc1[0:1, 8:9], ALU.add, (rrouter,), (rrouter,))
        pr.emit("S", lambda e: e.sign(lrow[:], lrow[:]), reads=(rrouter,), writes=(rrouter,))
        ts1("V", lrow[:], lrow[:], 1.0, ALU.add, (rrouter,), (rrouter,))
        tt("V", p2row[:], p2row[:], lrow[:], ALU.subtract, (rrouter,), (rrouter,))
        trans(RB[6], pbk[6][0:1, 40:48], iotac[0:NE, 0:1], ident16[0:NE, 0:NE], (rconsts,))
        cp("V", iorow[:], pbk[6][0:1, 40:48], (RB[6],), (rrouter, RB[6]))
        tt("V", lrow[:], lrow[:], iorow[:], ALU.mult, (rrouter,), (rrouter,))
        pr.emit("V", lambda e: e.reduce_sum(sc1[0:1, 10:11], lrow[:], axis=mybir.AxisListType.X),
                reads=(rrouter,), writes=(rrouter,))
        tt("V", p2row[:], p2row[:], iorow[:], ALU.mult, (rrouter,), (rrouter,))
        pr.emit("V", lambda e: e.reduce_sum(sc1[0:1, 11:12], p2row[:], axis=mybir.AxisListType.X),
                reads=(rrouter,), writes=(rrouter,))
        tt("V", sc1[0:1, 12:13], sc1[0:1, 4:5], sc1[0:1, 7:8], ALU.mult, (rrouter,), (rrouter,))
        tt("V", sc1[0:1, 13:14], sc1[0:1, 5:6], sc1[0:1, 7:8], ALU.mult, (rrouter,), (rrouter,))
        mm(RB[6], pbk[6][:, 48:52], onesf1[:], sc1[0:1, 10:14], True, True, (rconsts, rrouter))
        cp("V", cbs[:], pbk[6][:, 50:52], (RB[6],), (rrouter,))
        for j_ in range(2):
            stt("V", idxf[:, j_:j_ + 1], pbk[6][:, 48 + j_:49 + j_], 4096.0, iotac[:],
                ALU.mult, ALU.add, (RB[6], rconsts), (rrouter,))
            stt("V", ebase[:, j_:j_ + 1], pbk[6][:, 48 + j_:49 + j_], 128.0, iotac[:],
                ALU.mult, ALU.add, (RB[6], rconsts), (rrouter, RB[6]))
        cp("V", idxi[:, 0:2], ebase[:, 0:2], (rrouter,), (rrouter,))
        for j_ in range(2):
            pr.emit("P", lambda e, j=j_: e.indirect_dma_start(
                b1g[:, j * 32:(j + 1) * 32], None,
                d_b1.ap().rearrange("a p c -> (a p) c"),
                IndirectOffsetOnAxis(ap=idxi[:, j:j + 1], axis=0)),
                reads=(rrouter,), writes=(rrouter,), dma=True)
        pr.emit("V", lambda e: e.memset(b2comb[:], 0.0), writes=(rrouter,))
        for e_ in range(NE):
            for hc in range(8):
                stt("V", b2comb[:, hc:hc + 1], b2sb[:, e_ * 8 + hc:e_ * 8 + hc + 1],
                    cbc[:, e_:e_ + 1], b2comb[:, hc:hc + 1], ALU.mult, ALU.add,
                    (rconsts, rrouter), (rrouter,))

    def moe():
        w1rows = d_w1bf.ap().rearrange("a p c -> (a p) c")
        w2rows = d_w2bf.ap().rearrange("a p c -> (a p) c")
        idxq = [0]

        def idx8(es, fcg):
            for f8 in range(8):
                fc = fcg * 8 + f8
                ts1("V", idxt[:, f8:f8 + 1], idxf[:, es:es + 1], float(fc * P), ALU.add,
                    (rrouter,), (ridx[f8],))
                cp("V", idxi[:, f8:f8 + 1], idxt[:, f8:f8 + 1], (ridx[f8],), (ridx[f8],))

        def w1g1(es, fc):
            ws = fc % 2
            q = fc % 8
            pr.emit("P", lambda e, ws=ws, q=q: e.indirect_dma_start(
                w1buf[:, ws * 8 * P:(ws + 1) * 8 * P], None, w1rows,
                IndirectOffsetOnAxis(ap=idxi[:, q:q + 1], axis=0)),
                reads=(ridx[q],), writes=(rw1[ws],), dma=True)

        def w2g(es, fcg):
            for f8 in range(8):
                pr.emit("P", lambda e, f8=f8: e.indirect_dma_start(
                    w2buf[:, f8 * D:(f8 + 1) * D], None, w2rows,
                    IndirectOffsetOnAxis(ap=idxi[:, f8:f8 + 1], axis=0)),
                    reads=(ridx[f8],), writes=(rw2[f8],), dma=True)
                ts1("V", w2buf[:, f8 * D:(f8 + 1) * D], w2buf[:, f8 * D:(f8 + 1) * D],
                    cbs[:, es:es + 1], ALU.mult, (rrouter,), (rw2[f8],))

        blocks = [(es, fcg) for es in range(2) for fcg in range(4)]
        for bi, (es, fcg) in enumerate(blocks):
            idx8(es, fcg)
            for f8 in range(8):
                fc = fcg * 8 + f8
                ws = f8 % 2
                w1g1(es, fc)
                for mb in range(nmb):
                    bank = mb % 2
                    for k in range(8):
                        mm(RB[bank], pbk[bank][:, 0:MW],
                           w1buf[:, (ws * 8 + k) * P:(ws * 8 + k + 1) * P],
                           x1bf[:, k * S + mb * MW: k * S + (mb + 1) * MW],
                           k == 0, k == 7,
                           (rw1[ws],) + tuple(rx1bf[k][mb * MW // CH + j] for j in range(MW // CH)))
                    act("S", hmid[:, f8 * S + mb * MW: f8 * S + (mb + 1) * MW], pbk[bank][:, 0:MW],
                        AF.Gelu, b1g[:, es * 32 + (fc % 32):es * 32 + (fc % 32) + 1], 1.0,
                        (RB[bank], rrouter), (rhmid[f8][mb], RB[bank]))
            w2g(es, fcg)
            for mb in range(nmb):
                for hc in range(8):
                    bank = 2 + (mb * 8 + hc) % 2
                    for f8 in range(8):
                        mm(RB[bank], pbk[bank][:, 0:MW],
                           w2buf[:, f8 * D + hc * P: f8 * D + (hc + 1) * P],
                           hmid[:, f8 * S + mb * MW: f8 * S + (mb + 1) * MW],
                           f8 == 0, f8 == 7, (rw2[f8], rhmid[f8][mb]))
                    rxs = tuple(rX[hc][mb * MW // CH + j] for j in range(MW // CH))
                    tt("V", X[:, hc * S + mb * MW: hc * S + (mb + 1) * MW],
                       X[:, hc * S + mb * MW: hc * S + (mb + 1) * MW],
                       pbk[bank][:, 0:MW], ALU.add, rxs + (RB[bank],), rxs + (RB[bank],))

    def ln2_out():
        for mb in range(nmb):
            rxs = [tuple(rX[hc][mb * MW // CH + j] for j in range(MW // CH)) for hc in range(8)]
            for hc in range(8):
                ts1("V", X[:, hc * S + mb * MW: hc * S + (mb + 1) * MW],
                    X[:, hc * S + mb * MW: hc * S + (mb + 1) * MW],
                    b2comb[:, hc:hc + 1], ALU.add, rxs[hc] + (rrouter,), rxs[hc])
            for hc in range(8):
                cp("V", ybft[:], X[:, hc * S + mb * MW: hc * S + (mb + 1) * MW], rxs[hc], (rybft,))
                act("S", ln2sq[:], ybft[:], AF.Square, 0.0, 1.0, (rybft,), (rln2sq,))
                mm(RB[4], pbk[4][0:1, 0:MW], onesbf[:], ybft[:], hc == 0, hc == 7, (rybft,))
                mm(RB[5], pbk[5][0:1, 0:MW], onesbf[:], ln2sq[:], hc == 0, hc == 7, (rln2sq,))
            ts1("V", m_row[:], pbk[4][0:1, 0:MW], 1.0 / D, ALU.mult, (RB[4],), (rrows, RB[4]))
            ts1("V", v_row[:], pbk[5][0:1, 0:MW], 1.0 / D, ALU.mult, (RB[5],), (rrows, RB[5]))
            tt("V", r_row[0:1, 0:MW], m_row[:], m_row[:], ALU.mult, (rrows,), (rrows,))
            tt("V", v_row[:], v_row[:], r_row[0:1, 0:MW], ALU.subtract, (rrows,), (rrows,))
            act("S", r_row[0:1, 0:MW], v_row[:], AF.Ln, 1e-5, 1.0, (rrows,), (rrows,))
            act("S", r_row[0:1, 0:MW], r_row[0:1, 0:MW], AF.Exp, 0.0, -0.5, (rrows,), (rrows,))
            tt("V", r_row[0:1, MW:2 * MW], m_row[:], r_row[0:1, 0:MW], ALU.mult, (rrows,), (rrows,))
            mm(RB[6], pbk[6][:, 0:MW], onesf1[:], r_row[0:1, 0:MW], True, True, (rconsts, rrows))
            bank2 = 4
            mm(RB[bank2], pbk[bank2][:, 0:MW], onesf1[:], r_row[0:1, MW:2 * MW], True, True,
               (rconsts, rrows))
            for hc in range(8):
                ost = outst[:, 0:MW]
                tt("V", t4[:, 0:MW], X[:, hc * S + mb * MW: hc * S + (mb + 1) * MW], pbk[6][:, 0:MW],
                   ALU.mult, rxs[hc] + (RB[6],), (rt4,))
                tt("V", ost, t4[:, 0:MW], pbk[bank2][:, 0:MW], ALU.subtract,
                   (rt4, RB[bank2]), (routst[0],))
                dma("Y", d_out.ap()[hc * P:(hc + 1) * P, mb * MW:(mb + 1) * MW], ost,
                    routst[0], (routst[0],))

    def p1_full(c):
        yield from p1_block(c)
        yield from p1_kv_trans(c)

    for _ in p1_full(0):
        pass
    p1_gh(0)
    for c in range(nchunks):
        bg = [p1_full(c + 1)] if c + 1 < nchunks else [None]
        scan_chunk(c, bg)
        while bg[0] is not None:
            try:
                next(bg[0])
            except StopIteration:
                bg[0] = None
        p45_chunk(c)
        if c + 1 < nchunks:
            p1_gh(c + 1)
    router()
    pr.barrier()
    moe()
    ln2_out()

    DMAQ = {"Y": 4, "S": 5, "P": 6}
    waited = {4: set(), 5: set(), 6: set()}
    for eng in ENGS:
        for waits, fn, sem, amt in pr.q[eng]:
            for s, v in waits:
                if s >= 4:
                    waited[s].add(v)
    for eng, semidx in DMAQ.items():
        wl = sorted(waited[semidx])
        tot = 0
        last = 0
        q2 = []
        import bisect
        for waits, fn, sem, amt in pr.q[eng]:
            if sem == semidx:
                i = bisect.bisect_right(wl, tot) - 1
                if i >= 0 and wl[i] > last:
                    waits = [(semidx, wl[i])] + [x for x in waits if x != (semidx, wl[i])]
                    last = wl[i]
                tot += amt
            q2.append((waits, fn, sem, amt))
        pr.q[eng] = q2

    sems = {}
    with (
        nc.semaphore("semT") as sT,
        nc.semaphore("semV") as sV,
        nc.semaphore("semS") as sS,
        nc.semaphore("semP") as sP,
        nc.semaphore("dmaY") as dY,
        nc.semaphore("dmaA") as dA,
        nc.semaphore("dmaP") as dP,
        nc.Block() as block,
    ):
        semh = [sT, sV, sS, sP, dY, dA, dP]

        def run_queue(engobj, lst):
            for waits, fn, sem, amt in lst:
                for s, v in waits:
                    engobj.wait_ge(semh[s], v)
                inst = fn(engobj)
                if amt:
                    inst.then_inc(semh[sem], amt)

        @block.tensor
        def _(e):
            run_queue(e, pr.q["T"])

        @block.vector
        def _(e):
            run_queue(e, pr.q["V"])

        @block.scalar
        def _(e):
            run_queue(e, pr.q["S"])

        @block.gpsimd
        def _(e):
            run_queue(e, pr.q["P"])

        @block.sync
        def _(e):
            run_queue(e, pr.q["Y"])
            e.wait_ge(dY, pr.cnt[4])
            e.wait_ge(dA, pr.cnt[5])
            e.wait_ge(dP, pr.cnt[6])

    return nc


def host_prep(inputs, S=2048):
    f = np.float32
    W_qkv = np.asarray(inputs["L0_attn_W_qkv"], f)
    W_z = np.asarray(inputs["L0_attn_W_z"], f)
    W_b = np.asarray(inputs["L0_attn_W_b"], f)
    W_a = np.asarray(inputs["L0_attn_W_a"], f)
    W_out = np.asarray(inputs["L0_attn_W_out"], f)
    conv_w = np.asarray(inputs["L0_attn_conv_w"], f)
    base = {
        "wqkvT": np.ascontiguousarray(
            W_qkv.T.reshape(8, 128, 16, 128).transpose(2, 0, 1, 3).reshape(16 * 8, 128, 128)),
        "wzT": np.ascontiguousarray(
            W_z.T.reshape(8, 128, 8, 128).transpose(2, 0, 1, 3).reshape(8 * 8, 128, 128)),
        "woutT": np.ascontiguousarray(W_out.T.reshape(8, 128, 1024)),
        "wbT": np.ascontiguousarray(W_b.T.reshape(8, 128, 16)),
        "waT": np.ascontiguousarray(W_a.T.reshape(8, 128, 16)),
        "convw": np.ascontiguousarray(
            conv_w.reshape(2048, 4).reshape(16, 128, 4).transpose(1, 0, 2).reshape(128, 64)),
        "dtb": np.asarray(inputs["L0_attn_dt_bias"], f).reshape(16, 1),
        "alog": np.asarray(inputs["L0_attn_A_log"], f).reshape(16, 1),
        "normw": np.ascontiguousarray(np.tile(np.asarray(inputs["L0_attn_norm_weight"], f), 2).reshape(128, 1)),
        "gatew": np.ascontiguousarray(np.asarray(inputs["L0_ffn_gate_w"], f).reshape(8, 128, 8)),
        "w1p": np.ascontiguousarray(
            np.asarray(inputs["L0_ffn_w1"], f).reshape(8, 8, 128, 32, 128).transpose(0, 3, 2, 1, 4)
            .reshape(8 * 32, 128, 8 * 128)),
        "b1p": np.ascontiguousarray(np.asarray(inputs["L0_ffn_b1"], f).reshape(8, 32, 128).transpose(0, 2, 1)),
        "w2p": np.ascontiguousarray(np.asarray(inputs["L0_ffn_w2"], f).reshape(8 * 32, 128, 1024)),
        "b2p": np.ascontiguousarray(np.asarray(inputs["L0_ffn_b2"], f).reshape(8, 8, 128).transpose(2, 0, 1).reshape(128, 64)),
        "identf": np.eye(128, dtype=f),
        "ident16": np.eye(16, dtype=f),
        "maskL": np.tril(np.ones((128, 128), f), -1),
        "maskUs": np.triu(np.ones((128, 128), f), 1),
        "maskUi": np.triu(np.ones((128, 128), f), 0),
        "hm2": (np.arange(128)[:, None] // 64 == np.arange(2)[None, :]).astype(f),
        "ex2": (np.arange(128)[None, :] // 64 == np.arange(2)[:, None]).astype(f),
        "onesin": np.ones((128, 128), f),
        "cvals": np.broadcast_to(np.array([1.0, 1e-6, 1e-5, 0, 0, 0, 0, 0], f), (128, 8)).copy(),
        "onescol": np.ones((128, 1), f),
        "iotac": np.arange(128, dtype=f).reshape(128, 1),
    }
    return base


def kernel(**inputs):
    S = 2048
    x_all = np.asarray(inputs["L0_attn_hidden_states"], np.float32)
    base = host_prep(inputs, S)
    nc = build_program(S)
    in_maps = []
    for b in range(8):
        m = dict(base)
        m["xT"] = np.ascontiguousarray(x_all[b].T)
        in_maps.append(m)
    res = run_bass_kernel_spmd(nc, in_maps, core_ids=list(range(8)))
    out = np.stack([np.asarray(r["outT"]).T for r in res.results])
    return out.astype(np.float32)


# version 24 (final)
# speedup vs baseline: 1.1617x; 1.1617x over previous
import numpy as np
import ml_dtypes
import concourse.bass as bass
import concourse.mybir as mybir
from concourse.bass_utils import run_bass_kernel_spmd
from concourse.bass_core_rust import IndirectOffsetOnAxis

F32 = mybir.dt.float32
BF16 = mybir.dt.bfloat16
AF = mybir.ActivationFunctionType
ALU = mybir.AluOpType
BF = ml_dtypes.bfloat16

P = 128
CH = 128
NKH, NVH, DK, DV = 8, 16, 64, 64
D = 1024
FF = 4096
NE = 8

SEM_OF_ENG = {"T": 0, "V": 1, "S": 2, "P": 3}
DMA_SEM = {"Y": 4, "S": 5, "P": 6}
NSEM = 7
ENGS = ("T", "V", "S", "P", "Y")


class Res:
    __slots__ = ("w", "r")

    def __init__(self):
        self.w = None
        self.r = {}


class Op:
    __slots__ = ("sem", "val")

    def __init__(self, sem, val):
        self.sem = sem
        self.val = val


class Prog:
    def __init__(self):
        self.q = {e: [] for e in ENGS}
        self.cnt = [0] * NSEM
        self.wm = {e: [0] * NSEM for e in ENGS}

    def emit(self, eng, fn, reads=(), writes=(), dma=False, group_mid=False):
        need = {}

        def need_op(op):
            if op is None:
                return
            if eng == "T" and op.sem == SEM_OF_ENG["T"]:
                return
            v = self.cnt[op.sem] if op.sem >= 4 else op.val
            if need.get(op.sem, 0) < v:
                need[op.sem] = v

        for r in reads:
            need_op(r.w)
        if not group_mid:
            for w in writes:
                need_op(w.w)
                for op in w.r.values():
                    need_op(op)
        wm = self.wm[eng]
        waits = [(s, v) for s, v in sorted(need.items()) if wm[s] < v]
        for s, v in waits:
            wm[s] = v
        if dma:
            sem, amt = DMA_SEM[eng], 16
        else:
            sem, amt = SEM_OF_ENG[eng], 1
        self.cnt[sem] += amt
        op = Op(sem, self.cnt[sem])
        self.q[eng].append((waits, fn, sem, amt))
        for r in reads:
            r.r[eng] = op
        for w in writes:
            w.w = op
            w.r = {}
        return op

    def barrier(self):
        snap = list(self.cnt)
        for eng in ("T", "V", "S", "P", "Y"):
            wm = self.wm[eng]
            waits = [(s, v) for s, v in enumerate(snap) if v > 0 and wm[s] < v]
            for s, v in waits:
                wm[s] = v
            if eng == "Y":
                self.q[eng].append((waits, lambda e: e.nop(), 4, 0))
            else:
                sem = SEM_OF_ENG[eng]
                self.cnt[sem] += 1
                self.q[eng].append((waits, lambda e: e.nop(), sem, 1))


def build_program(S):
    nchunks = S // CH
    MW = min(512, S)
    nmb = S // MW
    nc = bass.Bass("TRN2", target_bir_lowering=False)
    pr = Prog()

    d_x = nc.dram_tensor("xT", [D, S], F32, kind="ExternalInput")
    d_wqkv = nc.dram_tensor("wqkvT", [16 * 8, P, P], F32, kind="ExternalInput")
    d_wz = nc.dram_tensor("wzT", [8 * 8, P, P], F32, kind="ExternalInput")
    d_wout = nc.dram_tensor("woutT", [8, P, D], F32, kind="ExternalInput")
    d_wb = nc.dram_tensor("wbT", [8, P, NVH], F32, kind="ExternalInput")
    d_wa = nc.dram_tensor("waT", [8, P, NVH], F32, kind="ExternalInput")
    d_conv = nc.dram_tensor("convw", [P, 64], F32, kind="ExternalInput")
    d_dt = nc.dram_tensor("dtb", [NVH, 1], F32, kind="ExternalInput")
    d_alog = nc.dram_tensor("alog", [NVH, 1], F32, kind="ExternalInput")
    d_nw = nc.dram_tensor("normw", [P, 1], F32, kind="ExternalInput")
    d_gate = nc.dram_tensor("gatew", [8, P, NE], F32, kind="ExternalInput")
    d_w1 = nc.dram_tensor("w1p", [NE * 32, P, 8 * P], F32, kind="ExternalInput")
    d_b1 = nc.dram_tensor("b1p", [NE, P, 32], F32, kind="ExternalInput")
    d_w2 = nc.dram_tensor("w2p", [NE * 32, P, D], F32, kind="ExternalInput")
    d_b2 = nc.dram_tensor("b2p", [P, 64], F32, kind="ExternalInput")
    d_identf = nc.dram_tensor("identf", [P, P], F32, kind="ExternalInput")
    d_ident16 = nc.dram_tensor("ident16", [16, 16], F32, kind="ExternalInput")
    d_maskL = nc.dram_tensor("maskL", [P, P], F32, kind="ExternalInput")
    d_maskUs = nc.dram_tensor("maskUs", [P, P], F32, kind="ExternalInput")
    d_maskUi = nc.dram_tensor("maskUi", [P, P], F32, kind="ExternalInput")
    d_hm2 = nc.dram_tensor("hm2", [P, 2], F32, kind="ExternalInput")
    d_ex2 = nc.dram_tensor("ex2", [2, P], F32, kind="ExternalInput")
    d_ones = nc.dram_tensor("onesin", [P, P], F32, kind="ExternalInput")
    d_cvals = nc.dram_tensor("cvals", [P, 8], F32, kind="ExternalInput")
    d_onescol = nc.dram_tensor("onescol", [P, 1], F32, kind="ExternalInput")
    d_iotac = nc.dram_tensor("iotac", [P, 1], F32, kind="ExternalInput")
    d_out = nc.dram_tensor("outT", [D, S], F32, kind="ExternalOutput")
    d_wqkv_bf = nc.dram_tensor("wqkv_bfi", [16 * 8, P, P], BF16, kind="Internal")
    d_wz_bf = nc.dram_tensor("wz_bfi", [8 * 8, P, P], BF16, kind="Internal")
    d_w1bf = nc.dram_tensor("w1bfi", [NE * 32, P, 8 * P], BF16, kind="Internal")
    d_w2bf = nc.dram_tensor("w2bfi", [NE * 32, P, D], BF16, kind="Internal")

    ctx = []

    def sbt(name, shape, dtype):
        t = nc.sbuf_tensor(name, shape, dtype)
        h = t.__enter__()
        ctx.append(t)
        return h

    def psb(name, shape, dtype):
        t = nc.psum_tensor(name, shape, dtype)
        h = t.__enter__()
        ctx.append(t)
        return h

    AR_COLS = 28744
    arena = sbt("arena", [P, AR_COLS], BF16)
    _off = [0]

    def acarve(n):
        o = _off[0]
        _off[0] = o + n
        assert _off[0] <= AR_COLS, f"arena overflow {_off[0]}"
        return arena[:, o:o + n]

    qkv = acarve(16 * 2 * CH)
    szr = acarve(8 * 2 * CH)
    xbfr = acarve(8 * 2 * CH)
    wqbuf = acarve(2 * 8 * P)
    wzbuf = acarve(2 * 8 * P)
    kcm = acarve(2 * 512)
    vcm = acarve(2 * 1024)
    sqt = acarve(1024)
    hnbf = acarve(1024)
    mixsb = acarve(2 * CH)
    mixtail = acarve(16 * 4)
    Upow = acarve(4 * 2 * P)
    Wpow = acarve(4 * 7 * P)
    Xch = acarve(4 * 2 * DV)
    Rp = acarve(4 * DV)
    Dtl = acarve(4 * DV)
    gam1 = acarve(4 * P)
    gam2 = acarve(4 * P)
    gam3 = acarve(4 * P)
    nkkL = acarve(2 * P)
    nkkU = acarve(2 * P)
    kqI = acarve(2 * P)
    Sbf = acarve(NVH * DV)
    obuf = acarve(2 * 1024)
    scan_end = _off[0]
    srepbf = acarve(8)
    excl_end = _off[0]
    _off[0] = 0
    hmid = acarve(8 * S)
    w1buf = acarve(2 * 8 * P)
    w2buf = acarve(8 * D)
    ybft = acarve(MW)
    ln2sq = acarve(MW)
    moe_end = _off[0]
    assert moe_end <= AR_COLS
    assert max(excl_end, moe_end) <= AR_COLS, (excl_end, moe_end)

    X = sbt("Xs", [P, 8 * S], F32)
    x1bf = sbt("x1bfs", [P, 8 * S], BF16)
    wout = sbt("wouts", [P, 8 * D], BF16)
    convw = sbt("convws", [P, 64], F32)
    convbuf = sbt("convbufs", [P, 2 * CH], F32)
    convtmp = sbt("convtmps", [P, CH], F32)
    thbuf = sbt("thbufs", [P, 2 * CH], F32)
    ubuf = sbt("ubufs", [NVH, 2 * CH], F32)
    gbuf = sbt("gbufs", [NVH, 2 * CH], F32)
    hbuf = sbt("hbufs", [NVH, 2 * CH], F32)
    gl1p = sbt("gl1ps", [NVH, CH], F32)
    gsA = sbt("gsAs", [NVH, CH], F32)
    gsB = sbt("gsBs", [NVH, CH], F32)
    gsC = sbt("gsCs", [NVH, CH], F32)
    dtcol = sbt("dtcols", [NVH, 1], F32)
    negea = sbt("negeas", [NVH, 1], F32)
    alogc = sbt("alogcs", [NVH, 1], F32)
    gt = sbt("gts", [P, 2 * 16], F32)
    ht = sbt("hts", [P, 2 * 16], F32)
    negbt = sbt("negbts", [P, 2 * 16], F32)
    expgt = sbt("expgts", [P, 2 * 16], F32)
    creps = sbt("crepss", [P, 4 * P], F32)
    tmpc = sbt("tmpcs", [P, 8 * P], F32)
    bccol = sbt("bccols", [DK, 4], F32)
    scol = sbt("scols", [P, 4], F32)
    t3 = sbt("t3s", [P, 4 * DV], F32)
    Sf = sbt("Sfs", [DK, NVH * DV], F32)
    htile = sbt("htiles", [P, 8 * P], F32)
    t4 = htile[:, 0:MW] if MW <= 8 * P else None
    assert MW <= 8 * P
    rsq = sbt("rsqs", [2, 512], F32)
    rstd2b = sbt("rstd2bs", [2, 512], BF16)
    rowsb = sbt("rowsbs", [1, 4 * MW], F32)
    m_row = rowsb[0:1, 2 * MW:3 * MW]
    v_row = rowsb[0:1, 3 * MW:4 * MW]
    r_row = rowsb[0:1, 0:2 * MW]
    identbf = sbt("identbfs", [P, P], BF16)
    identf = sbt("identfs", [P, P], F32)
    ident16 = sbt("ident16s", [16, 16], F32)
    maskL = sbt("maskLsb", [P, P], BF16)
    maskUs = sbt("maskUssb", [P, P], BF16)
    maskUi = sbt("maskUisb", [P, P], BF16)
    hm2 = sbt("hm2sb", [P, 2], BF16)
    ex2 = sbt("ex2sb", [2, P], BF16)
    onesbf = sbt("onesbfs", [P, 1], BF16)
    onesf1 = sbt("onesf1s", [1, P], F32)
    nwcol = sbt("nwcols", [P, 1], F32)
    cvals = sbt("cvalss", [P, 8], F32)
    wbT = sbt("wbTsb", [P, 8 * NVH], BF16)
    waT = sbt("waTsb", [P, 8 * NVH], BF16)
    gatesb = sbt("gatesbs", [P, 8 * NE], BF16)
    srep = sbt("sreps", [P, 8], F32)
    lcol = sbt("lcols", [NE, 1], F32)
    lrow = sbt("lrows", [1, NE], F32)
    prow = sbt("prows", [1, NE], F32)
    p2row = sbt("p2rows", [1, NE], F32)
    crow = sbt("crows", [1, NE], F32)
    sc1 = sbt("sc1s", [1, 16], F32)
    cvec_c = sbt("cvec_cs", [NE, 1], F32)
    cbc = sbt("cbcs", [P, NE], F32)
    iotac = sbt("iotacs", [P, 1], F32)
    ebase = sbt("ebases", [P, 2], F32)
    idxf = sbt("idxfs", [P, 2], F32)
    idxi = sbt("idxis", [P, 8], mybir.dt.int32)
    cbs = sbt("cbss", [P, 2], F32)
    b1g = sbt("b1gs", [P, 2 * 32], F32)
    iorow = sbt("iorows", [1, NE], F32)
    idxt = sbt("idxts", [P, 8], F32)
    b2comb = sbt("b2combs", [P, 8], F32)
    b1sb = sbt("b1sbs", [P, NE * 32], F32)
    b2sb = sbt("b2sbs", [P, 64], F32)
    outst = tmpc

    pbk = [psb(f"pbk{i}", [P, 512], F32) for i in range(7)]
    pbf = psb("pbkbf", [P, 1024], BF16)
    RB = [Res() for _ in range(8)]

    rX = [[Res() for _ in range(nchunks)] for _ in range(8)]
    rqkv = [[Res(), Res()] for _ in range(16)]
    rszr = [[Res(), Res()] for _ in range(8)]
    rxbfr = [[Res(), Res()] for _ in range(8)]
    rwq = [Res(), Res()]
    rwz = [Res(), Res()]
    rmix = [Res(), Res()]
    rconvb = [Res(), Res()]
    rcvt = Res()
    rth = [Res(), Res()]
    rmt = [Res() for _ in range(16)]
    rwout = Res()
    rconsts = Res()
    rmask = Res()
    rub = [Res(), Res()]
    rgb = [Res(), Res()]
    rhb = [Res(), Res()]
    rl1p = Res()
    rsA = Res()
    rsB = Res()
    rsC = Res()
    rgt = [Res(), Res()]
    rht = [Res(), Res()]
    rnegbt = [Res(), Res()]
    rexpgt = [Res(), Res()]
    rbcg = Res()
    rbch = Res()
    rgrowd = [Res(), Res()]
    rhrowd = [Res(), Res()]
    rtmpc = [[Res(), Res()] for _ in range(4)]
    rgam = [[Res(), Res(), Res()] for _ in range(4)]
    rgam3 = [Res() for _ in range(4)]
    rkkm = [[Res(), Res(), Res()] for _ in range(2)]
    rUp = [[Res(), Res()] for _ in range(4)]
    rWp = [[Res() for _ in range(7)] for _ in range(4)]
    rXch = [[Res(), Res()] for _ in range(4)]
    rRp = [Res() for _ in range(4)]
    rDtl = [Res() for _ in range(4)]
    rt3 = [Res() for _ in range(4)]
    rt4 = Res()
    rbccol = [Res() for _ in range(4)]
    rscol = [Res() for _ in range(4)]
    rbcg2 = [Res(), Res()]
    rbch2 = [Res(), Res()]
    rkcm = [Res(), Res()]
    rvcm = [Res(), Res()]
    robuf = [Res(), Res()]
    rSf = [Res() for _ in range(NVH)]
    rSbf = [Res() for _ in range(NVH)]
    rhtile = Res()
    rsqt = Res()
    rrsq = Res()
    rrstd = Res()
    rhn = Res()
    rrows = Res()
    rrouter = Res()
    rw1 = [Res(), Res()]
    ridx = [Res() for _ in range(8)]
    rw2 = [Res() for _ in range(8)]
    rhmid = [[Res() for _ in range(4)] for _ in range(8)]
    rybft = Res()
    rln2sq = Res()
    routst = [Res(), Res()]
    rstage = Res()
    rwqd = Res()
    rwzd = Res()
    rx1bf = [[Res() for _ in range(nchunks)] for _ in range(8)]

    def dma(eng, dst, src, w, rr=()):
        return pr.emit(eng, lambda e, d=dst, s=src: e.dma_start(d, s),
                       reads=tuple(rr), writes=(w,), dma=True)

    def mm(bres, out, lhs, rhs, start, stop, reads):
        pr.emit("T", lambda e, o=out, l=lhs, r=rhs, s=start, q=stop: e.matmul(o, l, r, start=s, stop=q),
                reads=tuple(reads), writes=(bres,), group_mid=not start)

    def trans(bres, out, in_, ident, reads):
        pr.emit("T", lambda e, o=out, i=in_, d=ident: e.transpose(o, i, d),
                reads=tuple(reads), writes=(bres,))

    def act(eng, out, in_, func, bias, scale, reads, writes):
        pr.emit(eng, lambda e, o=out, i=in_, f=func, b=bias, s=scale: e.activation(o, i, f, bias=b, scale=s),
                reads=tuple(reads) + (rcval,), writes=tuple(writes))

    def cp(eng, out, in_, reads, writes):
        if eng == "S":
            pr.emit(eng, lambda e, o=out, i=in_: e.copy(o, i), reads=tuple(reads), writes=tuple(writes))
        else:
            pr.emit(eng, lambda e, o=out, i=in_: e.tensor_copy(o, i), reads=tuple(reads), writes=tuple(writes))

    def ts1(eng, out, in0, s1, op0, reads, writes):
        pr.emit(eng, lambda e, o=out, i=in0, a=s1, x=op0: e.tensor_single_scalar(o, i, a, x),
                reads=tuple(reads), writes=tuple(writes))

    def ts2(eng, out, in0, s1, s2, op0, op1, reads, writes):
        pr.emit(eng, lambda e, o=out, i=in0, a=s1, b=s2, x=op0, y=op1: e.tensor_scalar(o, i, a, b, x, y),
                reads=tuple(reads), writes=tuple(writes))

    def tt(eng, out, in0, in1, op, reads, writes):
        pr.emit(eng, lambda e, o=out, a=in0, b=in1, x=op: e.tensor_tensor(o, a, b, x),
                reads=tuple(reads), writes=tuple(writes))

    def stt(eng, out, in0, s, in1, op0, op1, reads, writes):
        pr.emit(eng, lambda e, o=out, a=in0, c=s, b=in1, x=op0, y=op1: e.scalar_tensor_tensor(o, a, c, b, x, y),
                reads=tuple(reads), writes=tuple(writes))

    rcval = Res()
    CONST_VALS = [1.0, 1e-6, 1e-5, 0.0, 0.0, 0.0, 0.0, 0.0]
    dma("Y", cvals[:], d_cvals.ap(), rcval)
    for i, v in enumerate(CONST_VALS[:3]):
        nc.const_aps.register(F32, v, cvals[:, i:i + 1])
    dma("Y", ident16[:], d_ident16.ap(), rconsts)
    dma("Y", convw[:], d_conv.ap(), rconsts)
    dma("Y", dtcol[:], d_dt.ap(), rconsts)
    dma("Y", alogc[:], d_alog.ap(), rconsts)
    dma("Y", nwcol[:], d_nw.ap(), rconsts)
    dma("Y", b1sb[:].rearrange("p (e c) -> p e c", c=32), d_b1.ap().rearrange("e p c -> p e c"), rconsts)
    dma("Y", b2sb[:], d_b2.ap(), rconsts)
    dma("Y", onesf1[:], d_ones.ap()[0:1, :], rconsts)
    dma("Y", iotac[:], d_iotac.ap(), rconsts)
    dma("P", identbf[:], d_identf.ap(), rmask)
    dma("Y", identf[:], d_identf.ap(), rconsts)
    dma("P", maskL[:], d_maskL.ap(), rmask)
    dma("P", maskUs[:], d_maskUs.ap(), rmask)
    dma("P", maskUi[:], d_maskUi.ap(), rmask)
    dma("P", hm2[:], d_hm2.ap(), rmask)
    dma("P", ex2[:], d_ex2.ap(), rmask)
    dma("P", onesbf[:], d_onescol.ap(), rmask)
    dma("P", wbT[:].rearrange("p (k c) -> p k c", c=NVH), d_wb.ap().rearrange("k p c -> p k c"), rconsts)
    dma("P", waT[:].rearrange("p (k c) -> p k c", c=NVH), d_wa.ap().rearrange("k p c -> p k c"), rconsts)
    dma("P", gatesb[:].rearrange("p (k c) -> p k c", c=NE), d_gate.ap().rearrange("k p c -> p k c"), rconsts)
    dma("P", wout[:].rearrange("p (k c) -> p k c", c=D), d_wout.ap().rearrange("k p c -> p k c"), rwout)
    ts1("V", wout[:], wout[:], nwcol[:], ALU.mult, (rconsts, rwout), (rwout,))
    act("S", negea[:], alogc[:], AF.Exp, 0.0, 1.0, (rconsts,), (rconsts,))
    ts1("V", negea[:], negea[:], -1.0, ALU.mult, (rconsts,), (rconsts,))
    pr.emit("V", lambda e: e.memset(Sf[:], 0.0), writes=tuple(rSf))
    pr.emit("V", lambda e: e.memset(Sbf[:, :], 0.0), writes=tuple(rSbf))

    for kc in range(8):
        o = dma("Y", X[:, kc * S:(kc + 1) * S], d_x.ap()[kc * P:(kc + 1) * P, :], rX[kc][0])
        for c in range(nchunks):
            rX[kc][c].w = o

    dma("P", d_wqkv_bf.ap(), d_wqkv.ap(), rwqd)
    dma("P", d_wz_bf.ap(), d_wz.ap(), rwzd)
    rw1d = Res()
    rw2d = Res()
    for i in range(8):
        dma("P", d_w1bf.ap()[i * 32:(i + 1) * 32], d_w1.ap()[i * 32:(i + 1) * 32], rw1d)
        dma("P", d_w2bf.ap()[i * 32:(i + 1) * 32], d_w2.ap()[i * 32:(i + 1) * 32], rw2d)

    def p1_block(c):
        par = c % 2
        t0 = c * CH
        for kc in range(8):
            cp("S", xbfr[:, (kc * 2 + par) * CH:(kc * 2 + par + 1) * CH],
               X[:, kc * S + t0: kc * S + t0 + CH], (rX[kc][c],), (rxbfr[kc][par],))
            yield
        for cc in range(16):
            ws = cc % 2
            if ws == 0:
                for w2_ in range(2):
                    dma("Y", wqbuf[:, w2_ * 8 * P:(w2_ + 1) * 8 * P].rearrange("p (a c) -> p a c", c=P),
                        d_wqkv_bf.ap()[(cc + w2_) * 8:(cc + w2_ + 1) * 8].rearrange("a p c -> p a c"),
                        rwq[w2_], (rwqd,))
                    yield
            bank = cc % 2
            for k in range(8):
                mm(RB[bank], pbk[bank][:, 0:CH],
                   wqbuf[:, (ws * 8 + k) * P:(ws * 8 + k + 1) * P],
                   xbfr[:, (k * 2 + par) * CH:(k * 2 + par + 1) * CH],
                   k == 0, k == 7, (rwq[0], rwq[1], rxbfr[k][par]))
                yield
            mxs = mixsb[:, par * CH:(par + 1) * CH]
            cp("V", mxs, pbk[bank][:, 0:CH], (RB[bank],), (rmix[par], RB[bank]))
            yield
            cb = convbuf[:, par * CH:(par + 1) * CH]

            def wbc(j, wdt, cc=cc):
                return bass.AP(convw, cc * 4 + j, [[64, P], [0, wdt]])
            if cc % 2 == 0:
                tt("P", cb, mxs, wbc(3, CH), ALU.mult, (rmix[par], rconsts), (rconvb[par],))
                yield
                for j in (2, 1, 0):
                    sh = 3 - j
                    tt("P", convtmp[:, 0:CH - sh], mxs[:, 0:CH - sh], wbc(j, CH - sh), ALU.mult,
                       (rmix[par], rconsts), (rcvt,))
                    yield
                    tt("P", cb[:, sh:CH], cb[:, sh:CH], convtmp[:, 0:CH - sh], ALU.add,
                       (rcvt,), (rconvb[par],))
                    yield
                    if c > 0:
                        stt("V", cb[:, 0:sh], mixtail[:, cc * 4 + 3 - sh:cc * 4 + 3],
                            convw[:, cc * 4 + j:cc * 4 + j + 1], cb[:, 0:sh],
                            ALU.mult, ALU.add, (rmt[cc], rconsts), (rconvb[par],))
                        yield
                cp("P", mixtail[:, cc * 4:cc * 4 + 3], mxs[:, CH - 3:CH], (rmix[par],), (rmt[cc],))
                yield
            else:
                ts1("V", cb, mxs, convw[:, cc * 4 + 3:cc * 4 + 4], ALU.mult, (rmix[par], rconsts), (rconvb[par],))
                yield
                for j in (2, 1, 0):
                    sh = 3 - j
                    wj = convw[:, cc * 4 + j:cc * 4 + j + 1]
                    stt("V", cb[:, sh:CH], mxs[:, 0:CH - sh], wj, cb[:, sh:CH], ALU.mult, ALU.add,
                        (rmix[par], rconsts), (rconvb[par],))
                    yield
                    if c > 0:
                        stt("V", cb[:, 0:sh], mixtail[:, cc * 4 + 3 - sh:cc * 4 + 3], wj, cb[:, 0:sh],
                            ALU.mult, ALU.add, (rmt[cc], rconsts), (rconvb[par],))
                        yield
                cp("V", mixtail[:, cc * 4:cc * 4 + 3], mxs[:, CH - 3:CH], (rmix[par],), (rmt[cc],))
                yield
            thb = thbuf[:, par * CH:(par + 1) * CH]
            act("S", thb, cb, AF.Tanh, 0.0, 0.5, (rconvb[par],), (rth[par],))
            yield
            ts2("V", thb, thb, 1.0, 0.5, ALU.add, ALU.mult, (rth[par],), (rth[par],))
            yield
            tt("P", qkv[:, (cc * 2 + par) * CH:(cc * 2 + par + 1) * CH], cb, thb, ALU.mult,
               (rconvb[par], rth[par]), (rqkv[cc][par],))
            yield
        for zc in range(8):
            ws = zc % 2
            if ws == 0:
                for w2_ in range(2):
                    dma("Y", wzbuf[:, w2_ * 8 * P:(w2_ + 1) * 8 * P].rearrange("p (a c) -> p a c", c=P),
                        d_wz_bf.ap()[(zc + w2_) * 8:(zc + w2_ + 1) * 8].rearrange("a p c -> p a c"),
                        rwz[w2_], (rwzd,))
                    yield
            bank = zc % 2
            for k in range(8):
                mm(RB[bank], pbk[bank][:, 0:CH],
                   wzbuf[:, (ws * 8 + k) * P:(ws * 8 + k + 1) * P],
                   xbfr[:, (k * 2 + par) * CH:(k * 2 + par + 1) * CH],
                   k == 0, k == 7, (rwz[0], rwz[1], rxbfr[k][par]))
                yield
            thb = thbuf[:, par * CH:(par + 1) * CH]
            act("S", thb, pbk[bank][:, 0:CH], AF.Tanh, 0.0, 0.5, (RB[bank],), (rth[par],))
            yield
            ts2("V", thb, thb, 1.0, 0.5, ALU.add, ALU.mult, (rth[par],), (rth[par],))
            yield
            tt("V", szr[:, (zc * 2 + par) * CH:(zc * 2 + par + 1) * CH], pbk[bank][:, 0:CH], thb,
               ALU.mult, (RB[bank], rth[par]), (rszr[zc][par], RB[bank]))
            yield

    def p1_gh(c):
        par = c % 2
        for k in range(8):
            mm(RB[6], pbk[6][0:16, 0:CH], wbT[:, k * NVH:(k + 1) * NVH],
               xbfr[:, (k * 2 + par) * CH:(k * 2 + par + 1) * CH],
               k == 0, k == 7, (rconsts, rxbfr[k][par]))
        act("S", gsB[:], pbk[6][0:16, 0:CH], AF.Abs, 0.0, 1.0, (RB[6],), (rsB,))
        ts1("V", gsA[:], pbk[6][0:16, 0:CH], 0.0, ALU.min, (RB[6],), (rsA, RB[6]))
        act("S", gsB[:], gsB[:], AF.Exp, 0.0, -1.0, (rsB,), (rsB,))
        act("S", gsC[:], gsB[:], AF.Ln, 1.0, 1.0, (rsB,), (rsC,))
        tt("V", gsC[:], gsA[:], gsC[:], ALU.subtract, (rsA, rsC), (rsC,))
        for k in range(8):
            mm(RB[6], pbk[6][0:16, 0:CH], waT[:, k * NVH:(k + 1) * NVH],
               xbfr[:, (k * 2 + par) * CH:(k * 2 + par + 1) * CH],
               k == 0, k == 7, (rconsts, rxbfr[k][par]))
        ub = ubuf[:, par * CH:(par + 1) * CH]
        gb = gbuf[:, par * CH:(par + 1) * CH]
        hb = hbuf[:, par * CH:(par + 1) * CH]
        ts1("V", ub[0:16, :], pbk[6][0:16, 0:CH], dtcol[:], ALU.add, (RB[6], rconsts), (rub[par], RB[6]))
        act("S", gsA[:], ub[0:16, :], AF.Abs, 0.0, 1.0, (rub[par],), (rsA,))
        act("S", gsA[:], gsA[:], AF.Exp, 0.0, -1.0, (rsA,), (rsA,))
        act("S", gl1p[:], gsA[:], AF.Ln, 1.0, 1.0, (rsA,), (rl1p,))
        ts1("V", gsA[:], ub[0:16, :], 0.0, ALU.max, (rub[par],), (rsA,))
        tt("V", gsA[:], gsA[:], gl1p[:], ALU.add, (rsA, rl1p), (rsA,))
        ts1("V", gb[0:16, :], gsA[:], negea[:], ALU.mult, (rsA, rconsts), (rgb[par],))
        srcv, srcr = gb[0:16, :], rgb[par]
        dstv, dstr = gsA[:], rsA
        s = 1
        while s < CH:
            tt("V", dstv[:, s:CH], srcv[:, s:CH], srcv[:, 0:CH - s], ALU.add, (srcr,), (dstr,))
            cp("V", dstv[:, 0:s], srcv[:, 0:s], (srcr,), (dstr,))
            srcv, dstv = dstv, srcv
            srcr, dstr = dstr, srcr
            s *= 2
        cp("V", gb[0:16, :], gsA[:], (rsA,), (rgb[par],))
        tt("V", hb[0:16, :], gb[0:16, :], gsC[:], ALU.subtract, (rgb[par], rsC), (rhb[par],))
        trans(RB[6], pbk[6][:, 0:16], gb[0:16, :], ident16[:], (rgb[par], rconsts))
        cp("V", gt[:, par * 16:(par + 1) * 16], pbk[6][:, 0:16], (RB[6],), (rgt[par], RB[6]))
        trans(RB[6], pbk[6][:, 16:32], hb[0:16, :], ident16[:], (rhb[par], rconsts))
        cp("V", ht[:, par * 16:(par + 1) * 16], pbk[6][:, 16:32], (RB[6],), (rht[par], RB[6]))
        act("S", expgt[:, par * 16:(par + 1) * 16], gt[:, par * 16:(par + 1) * 16], AF.Exp, 0.0, 1.0,
            (rgt[par],), (rexpgt[par],))
        ts1("V", negbt[:, par * 16:(par + 1) * 16], expgt[:, par * 16:(par + 1) * 16], -1.0, ALU.mult,
            (rexpgt[par],), (rnegbt[par],))


    def scan_chunk(c, bg_gen):
        par = c % 2

        def kv_trans():
            for i in range(4):
                cc = 4 + i
                trans(RB[7], pbf[:, 0:P], qkv[:, (cc * 2 + par) * CH:(cc * 2 + par + 1) * CH], identbf[:],
                      (rqkv[cc][par], rmask))
                cp("V", kcm[:, par * 512 + i * P: par * 512 + (i + 1) * P], pbf[:, 0:P], (RB[7],), (rkcm[par], RB[7]))
                yield
            for i in range(8):
                cc = 8 + i
                trans(RB[7], pbf[:, 0:P], qkv[:, (cc * 2 + par) * CH:(cc * 2 + par + 1) * CH], identbf[:],
                      (rqkv[cc][par], rmask))
                cp("S" if i % 2 else "V", vcm[:, par * 1024 + i * P: par * 1024 + (i + 1) * P], pbf[:, 0:P],
                   (RB[7],), (rvcm[par], RB[7]))
                yield

        kvg = [kv_trans()]

        def scan_head(h, par):
            hp = h % 2
            h4 = h % 4
            kvp = (h // 2) % 2
            qq = (h // 4) % 2
            kh = h // 2
            gcol = gt[:, par * 16 + h:par * 16 + h + 1]
            hcol = ht[:, par * 16 + h:par * 16 + h + 1]
            Q = qkv[:, ((kh // 2) * 2 + par) * CH:((kh // 2) * 2 + par + 1) * CH][(kh % 2) * 64:(kh % 2) * 64 + 64, :]
            K = qkv[:, ((4 + kh // 2) * 2 + par) * CH:((4 + kh // 2) * 2 + par + 1) * CH][(kh % 2) * 64:(kh % 2) * 64 + 64, :]
            rq = rqkv[kh // 2][par]
            rk = rqkv[4 + kh // 2][par]
            sb_ = pbk[4 + hp]
            sbR = RB[4 + hp]
            B0 = (h4 // 2) * 256
            nkkLs = nkkL[:, kvp * P:(kvp + 1) * P]
            nkkUs = nkkU[:, kvp * P:(kvp + 1) * P]
            kqIs = kqI[:, kvp * P:(kvp + 1) * P]
            if h % 2 == 0:
                kks = pbk[2][:, h4 * P:(h4 + 1) * P]
                kqs = pbk[3][:, h4 * P:(h4 + 1) * P]
                mm(RB[2], kks, K, K, True, True, (rk,))
                yield
                mm(RB[3], kqs, K, Q, True, True, (rk, rq))
                yield
                stt("V", nkkLs, kks, -1.0, maskL[:], ALU.mult, ALU.mult,
                    (RB[2], rmask), (rkkm[kvp][0],))
                yield
                stt("V", nkkUs, kks, -1.0, maskUs[:], ALU.mult, ALU.mult,
                    (RB[2], rmask), (rkkm[kvp][1], RB[2]))
                yield
                stt("V", kqIs, kqs, 1.0, maskUi[:], ALU.mult, ALU.mult,
                    (RB[3], rmask), (rkkm[kvp][2], RB[3]))
                yield
            t1s = tmpc[:, h4 * 2 * P: h4 * 2 * P + P]
            t2s = tmpc[:, h4 * 2 * P + P: h4 * 2 * P + 2 * P]
            g1 = gam1[:, h4 * P:(h4 + 1) * P]
            g2 = gam2[:, h4 * P:(h4 + 1) * P]
            g3 = gam3[:, h4 * P:(h4 + 1) * P]
            crepH = creps[:, h4 * P:(h4 + 1) * P]
            bc1 = pbk[2][:, h4 * P:(h4 + 1) * P]
            bc2 = pbk[3][:, h4 * P:(h4 + 1) * P]
            cp("P", crepH, bass.AP(ht, par * 16 + h, [[32, P], [0, P]]), (rht[par],), (rtmpc[h4][0],))
            yield
            trans(RB[2], bc1, crepH, identf[:], (rtmpc[h4][0], rconsts))
            yield
            ts2("V", t1s, bc1, gcol, -30.0, ALU.subtract, ALU.max,
                (RB[2], rgt[par]), (rtmpc[h4][0], RB[2]))
            yield
            act("S", g1, t1s, AF.Exp, 0.0, -1.0, (rtmpc[h4][0],), (rgam[h4][0],))
            yield
            cp("P", crepH, bass.AP(gt, par * 16 + h, [[32, P], [0, P]]), (rgt[par], rtmpc[h4][0]), (rtmpc[h4][1],))
            yield
            trans(RB[3], bc2, crepH, identf[:], (rtmpc[h4][1], rconsts))
            yield
            act("S", bccol[:, h4:h4 + 1], bc2[0:DK, P - 1:P], AF.Exp, 0.0, 1.0, (RB[3],), (rbccol[h4],))
            yield
            ts2("V", t2s, bc2, hcol, 30.0, ALU.subtract, ALU.min,
                (RB[3], rht[par]), (rtmpc[h4][1], RB[3]))
            yield
            act("S", g2, t2s, AF.Exp, 0.0, 1.0, (rtmpc[h4][1],), (rgam[h4][1],))
            yield
            Up0 = Upow[:, h4 * 2 * P: h4 * 2 * P + P]
            Up1 = Upow[:, h4 * 2 * P + P: h4 * 2 * P + 2 * P]
            Wp = [Wpow[:, (h4 * 7 + lv) * P:(h4 * 7 + lv + 1) * P] for lv in range(7)]
            tt("P", Up0, g1, nkkLs, ALU.mult, (rgam[h4][0], rkkm[kvp][0]), (rUp[h4][0],))
            yield
            tt("P", Wp[0], g2, nkkUs, ALU.mult, (rgam[h4][1], rkkm[kvp][1]), (rWp[h4][0],))
            yield
            tt("P", g3, g2, kqIs, ALU.mult, (rgam[h4][1], rkkm[kvp][2]), (rgam3[h4],))
            yield
            sb0 = (kh % 2) * 64
            mm(sbR, sb_[:, B0:B0 + DV], K, Sbf[sb0:sb0 + DK, h * DV:(h + 1) * DV], True, True, (rk, rSbf[h]))
            yield
            RpS = Rp[:, h4 * DV:(h4 + 1) * DV]
            stt("V", RpS, sb_[:, B0:B0 + DV], negbt[:, par * 16 + h:par * 16 + h + 1],
                vcm[:, par * 1024 + h * DV: par * 1024 + (h + 1) * DV], ALU.mult, ALU.add,
                (sbR, rnegbt[par], rvcm[par]), (rRp[h4], sbR))
            yield
            xsrc, xres = RpS, rRp[h4]
            for lev in range(7):
                mm(sbR, sb_[:, B0 + 64:B0 + 64 + DV], Wp[lev], xsrc, True, True, (rWp[h4][lev], xres))
                yield
                dst = Xch[:, (h4 * 2 + lev % 2) * DV:(h4 * 2 + lev % 2 + 1) * DV]
                tt("V", dst, sb_[:, B0 + 64:B0 + 64 + DV], xsrc, ALU.add,
                   (sbR, xres), (rXch[h4][lev % 2], sbR))
                yield
                xsrc, xres = dst, rXch[h4][lev % 2]
                if lev < 6:
                    sp_ = Up0 if lev % 2 == 0 else Up1
                    dp_ = Up1 if lev % 2 == 0 else Up0
                    rsp, rdp = rUp[h4][lev % 2], rUp[h4][(lev + 1) % 2]
                    sq2 = pbk[2][:, h4 * P:(h4 + 1) * P]
                    sq3 = pbk[3][:, h4 * P:(h4 + 1) * P]
                    if lev < 5:
                        mm(RB[2], sq2, Wp[lev], sp_, True, True, (rsp, rWp[h4][lev]))
                        yield
                        cp("V" if lev % 2 else "S", dp_, sq2, (RB[2],), (rdp, RB[2]))
                        yield
                    mm(RB[3], sq3, sp_, Wp[lev], True, True, (rsp, rWp[h4][lev]))
                    yield
                    cp("S" if lev % 2 else "V", Wp[lev + 1], sq3, (RB[3],), (rWp[h4][lev + 1], RB[3]))
                    yield
            act("S", scol[:, h4:h4 + 1], t2s[:, P - 1:P], AF.Exp, 0.0, 1.0, (rtmpc[h4][1],), (rscol[h4],))
            yield
            DtlS = Dtl[:, h4 * DV:(h4 + 1) * DV]
            ts1("V", DtlS, xsrc, scol[:, h4:h4 + 1], ALU.mult, (xres, rscol[h4]), (rDtl[h4],))
            yield
            mm(sbR, sb_[:, B0 + 128:B0 + 128 + DV], Q, Sbf[sb0:sb0 + DK, h * DV:(h + 1) * DV], True, True,
               (rq, rSbf[h]))
            yield
            t3S = t3[:, h4 * DV:(h4 + 1) * DV]
            ts2("V", t3S, sb_[:, B0 + 128:B0 + 128 + DV], expgt[:, par * 16 + h:par * 16 + h + 1], 0.125,
                ALU.mult, ALU.mult, (sbR, rexpgt[par]), (rt3[h4], sbR))
            yield
            mm(sbR, sb_[:, B0 + 192:B0 + 192 + DV], g3, xsrc, True, True, (rgam3[h4], xres))
            yield
            stt("V", obuf[:, par * 1024 + h * DV: par * 1024 + (h + 1) * DV], sb_[:, B0 + 192:B0 + 192 + DV], 0.125,
                t3S, ALU.mult, ALU.add, (sbR, rt3[h4]), (robuf[par], sbR))
            yield
            mm(sbR, sb_[0:DK, B0:B0 + DV], kcm[:, par * 512 + kh * DK: par * 512 + (kh + 1) * DK],
               DtlS, True, True, (rkcm[par], rDtl[h4]))
            yield
            stt("V", Sf[:, h * DV:(h + 1) * DV], Sf[:, h * DV:(h + 1) * DV], bccol[:, h4:h4 + 1],
                sb_[0:DK, B0:B0 + DV], ALU.mult, ALU.add, (rSf[h], rbccol[h4], sbR), (rSf[h], sbR))
            yield
            cp("P", Sbf[0:DK, h * DV:(h + 1) * DV], Sf[:, h * DV:(h + 1) * DV], (rSf[h],), (rSbf[h],))
            yield
            cp("P", Sbf[DK:P, h * DV:(h + 1) * DV], Sf[:, h * DV:(h + 1) * DV], (rSf[h],), (rSbf[h],))
            yield


        for hp_ in range(4):
            gens = [scan_head(4 * hp_, par), scan_head(4 * hp_ + 1, par),
                    scan_head(4 * hp_ + 2, par), scan_head(4 * hp_ + 3, par)]
            for g_ in gens[:2]:
                for _ in range(8):
                    next(g_)
            live = list(gens)
            while live:
                for g_ in list(live):
                    try:
                        next(g_)
                    except StopIteration:
                        live.remove(g_)
                if kvg[0] is not None:
                    try:
                        next(kvg[0])
                    except StopIteration:
                        kvg[0] = None
                if bg_gen[0] is not None:
                    try:
                        next(bg_gen[0])
                    except StopIteration:
                        bg_gen[0] = None

    def p45_chunk(c):
        par = c % 2
        t0 = c * CH
        for half in range(2):
            for fc4 in range(4):
                fc = half * 4 + fc4
                trans(RB[7], pbf[:, 0:P], obuf[:, par * 1024 + fc * P: par * 1024 + (fc + 1) * P],
                      identbf[:], (robuf[par], rmask))
                tt("V", htile[:, fc * P:(fc + 1) * P], pbf[:, 0:P],
                   szr[:, (fc * 2 + par) * CH:(fc * 2 + par + 1) * CH], ALU.mult,
                   (RB[7], rszr[fc][par]), (rhtile, RB[7]))
                act("S", sqt[:, fc * P:(fc + 1) * P], htile[:, fc * P:(fc + 1) * P], AF.Square, 0.0, 1.0,
                    (rhtile,), (rsqt,))
                mm(RB[6], pbk[6][0:2, fc4 * P:(fc4 + 1) * P], hm2[:], sqt[:, fc * P:(fc + 1) * P],
                   True, True, (rmask, rsqt))
            cp("V", rsq[0:2, 0:512], pbk[6][0:2, 0:512], (RB[6],), (rrsq, RB[6]))
            act("S", rsq[0:2, :], rsq[0:2, :], AF.Ln, 1e-6, 1.0 / DV, (rrsq,), (rrstd,))
            act("S", rsq[0:2, :], rsq[0:2, :], AF.Exp, 0.0, -0.5, (rrstd,), (rrstd,))
            cp("V", rstd2b[0:2, :], rsq[0:2, :], (rrstd,), (rrstd,))
            for fc4 in range(4):
                fc = half * 4 + fc4
                bank = fc % 2
                mm(RB[bank], pbk[bank][:, 0:P], ex2[:], rstd2b[0:2, fc4 * P:(fc4 + 1) * P], True, True,
                   (rmask, rrstd))
                tt("V", hnbf[:, fc * P:(fc + 1) * P], htile[:, fc * P:(fc + 1) * P], pbk[bank][:, 0:P],
                   ALU.mult, (rhtile, RB[bank]), (rhn, RB[bank]))
        for hc in range(8):
            bank = hc % 2
            for kc in range(8):
                mm(RB[bank], pbk[bank][:, 0:P], wout[:, kc * D + hc * P: kc * D + (hc + 1) * P],
                   hnbf[:, kc * P:(kc + 1) * P], kc == 0, kc == 7, (rwout, rhn))
            tt("V", X[:, hc * S + t0: hc * S + t0 + CH], X[:, hc * S + t0: hc * S + t0 + CH],
               pbk[bank][:, 0:P], ALU.add, (rX[hc][c], RB[bank]), (rX[hc][c], RB[bank]))
            cp("V", sqt[:, hc * P:(hc + 1) * P],
               X[:, hc * S + t0: hc * S + t0 + CH], (rX[hc][c],), (rsqt,))
        for hc in range(8):
            eng_ = "S" if hc % 2 else "V"
            if eng_ == "S":
                act("S", hnbf[:, hc * P:(hc + 1) * P], sqt[:, hc * P:(hc + 1) * P], AF.Square, 0.0, 1.0,
                    (rsqt,), (rhn,))
            else:
                tt("V", hnbf[:, hc * P:(hc + 1) * P], sqt[:, hc * P:(hc + 1) * P],
                   sqt[:, hc * P:(hc + 1) * P], ALU.mult, (rsqt,), (rhn,))
        for hc in range(8):
            mm(RB[6], pbk[6][0:1, 0:P], onesbf[:], sqt[:, hc * P:(hc + 1) * P], hc == 0, hc == 7,
               (rsqt,))
        ts1("V", m_row[0:1, 0:P], pbk[6][0:1, 0:P], 1.0 / D, ALU.mult, (RB[6],), (rrows, RB[6]))
        for hc in range(8):
            mm(RB[6], pbk[6][0:1, 0:P], onesbf[:], hnbf[:, hc * P:(hc + 1) * P], hc == 0, hc == 7,
               (rhn,))
        ts1("V", v_row[0:1, 0:P], pbk[6][0:1, 0:P], 1.0 / D, ALU.mult, (RB[6],), (rrows, RB[6]))
        tt("V", r_row[0:1, 0:P], m_row[0:1, 0:P], m_row[0:1, 0:P], ALU.mult, (rrows,), (rrows,))
        tt("V", v_row[0:1, 0:P], v_row[0:1, 0:P], r_row[0:1, 0:P], ALU.subtract, (rrows,), (rrows,))
        act("S", r_row[0:1, 0:P], v_row[0:1, 0:P], AF.Ln, 1e-5, 1.0, (rrows,), (rrows,))
        act("S", r_row[0:1, 0:P], r_row[0:1, 0:P], AF.Exp, 0.0, -0.5, (rrows,), (rrows,))
        tt("V", r_row[0:1, P:2 * P], m_row[0:1, 0:P], r_row[0:1, 0:P], ALU.mult, (rrows,), (rrows,))
        mm(RB[6], pbk[6][:, 0:2 * P], onesf1[:], r_row[0:1, 0:2 * P], True, True, (rconsts, rrows))
        for hc in range(8):
            tt("V", t4[:, 0:P], X[:, hc * S + t0: hc * S + t0 + CH], pbk[6][:, 0:P], ALU.mult,
               (rX[hc][c], RB[6]), (rt4,))
            tt("V", X[:, hc * S + t0: hc * S + t0 + CH], t4[:, 0:P], pbk[6][:, P:2 * P], ALU.subtract,
               (rt4, RB[6]), (rX[hc][c],))
            cp("S", x1bf[:, hc * S + t0: hc * S + t0 + CH], X[:, hc * S + t0: hc * S + t0 + CH],
               (rX[hc][c],), (rx1bf[hc][c],))

    def router():
        allx = [rX[kc][c] for kc in range(8) for c in range(nchunks)]
        for kc in range(8):
            pr.emit("V", lambda e, kc=kc: e.reduce_sum(srep[:, kc:kc + 1], X[:, kc * S:(kc + 1) * S],
                                                       axis=mybir.AxisListType.X),
                    reads=tuple(rX[kc]), writes=(rrouter,))
        ts1("V", srep[:], srep[:], 1.0 / S, ALU.mult, (rrouter,), (rrouter,))
        cp("V", srepbf[:, 0:8], srep[:], (rrouter,), (rrouter,))
        for kc in range(8):
            mm(RB[6], pbk[6][0:NE, 0:1], gatesb[:, kc * NE:(kc + 1) * NE], srepbf[:, kc:kc + 1],
               kc == 0, kc == 7, (rconsts, rrouter))
        cp("V", lcol[:], pbk[6][0:NE, 0:1], (RB[6],), (rrouter, RB[6]))
        trans(RB[6], pbk[6][0:1, 0:NE], lcol[:], ident16[0:NE, 0:NE], (rrouter, rconsts))
        cp("V", lrow[:], pbk[6][0:1, 0:NE], (RB[6],), (rrouter, RB[6]))
        pr.emit("V", lambda e: e.reduce_max(sc1[0:1, 0:1], lrow[:], axis=mybir.AxisListType.X),
                reads=(rrouter,), writes=(rrouter,))
        ts1("V", sc1[0:1, 1:2], sc1[0:1, 0:1], -1.0, ALU.mult, (rrouter,), (rrouter,))
        act("S", prow[:], lrow[:], AF.Exp, sc1[0:1, 1:2], 1.0, (rrouter,), (rrouter,))
        pr.emit("V", lambda e: e.reduce_sum(sc1[0:1, 2:3], prow[:], axis=mybir.AxisListType.X),
                reads=(rrouter,), writes=(rrouter,))
        pr.emit("V", lambda e: e.reciprocal(sc1[0:1, 3:4], sc1[0:1, 2:3]),
                reads=(rrouter,), writes=(rrouter,))
        ts1("V", prow[:], prow[:], sc1[0:1, 3:4], ALU.mult, (rrouter,), (rrouter,))
        pr.emit("V", lambda e: e.reduce_max(sc1[0:1, 4:5], prow[:], axis=mybir.AxisListType.X),
                reads=(rrouter,), writes=(rrouter,))
        ts1("V", sc1[0:1, 8:9], sc1[0:1, 4:5], -1.0, ALU.mult, (rrouter,), (rrouter,))
        ts1("V", p2row[:], prow[:], sc1[0:1, 8:9], ALU.add, (rrouter,), (rrouter,))
        pr.emit("S", lambda e: e.sign(p2row[:], p2row[:]), reads=(rrouter,), writes=(rrouter,))
        ts1("V", p2row[:], p2row[:], 1.0, ALU.add, (rrouter,), (rrouter,))
        stt("V", crow[:], p2row[:], -2.0, prow[:], ALU.mult, ALU.add, (rrouter,), (rrouter,))
        pr.emit("V", lambda e: e.reduce_max(sc1[0:1, 5:6], crow[:], axis=mybir.AxisListType.X),
                reads=(rrouter,), writes=(rrouter,))
        ts1("V", sc1[0:1, 9:10], sc1[0:1, 5:6], -1.0, ALU.mult, (rrouter,), (rrouter,))
        ts1("V", crow[:], crow[:], sc1[0:1, 9:10], ALU.add, (rrouter,), (rrouter,))
        pr.emit("S", lambda e: e.sign(crow[:], crow[:]), reads=(rrouter,), writes=(rrouter,))
        ts1("V", crow[:], crow[:], 1.0, ALU.add, (rrouter,), (rrouter,))
        tt("V", p2row[:], p2row[:], crow[:], ALU.add, (rrouter,), (rrouter,))
        tt("V", crow[:], prow[:], p2row[:], ALU.mult, (rrouter,), (rrouter,))
        pr.emit("V", lambda e: e.reduce_sum(sc1[0:1, 6:7], crow[:], axis=mybir.AxisListType.X),
                reads=(rrouter,), writes=(rrouter,))
        pr.emit("V", lambda e: e.reciprocal(sc1[0:1, 7:8], sc1[0:1, 6:7]),
                reads=(rrouter,), writes=(rrouter,))
        ts1("V", crow[:], crow[:], sc1[0:1, 7:8], ALU.mult, (rrouter,), (rrouter,))
        mm(RB[6], pbk[6][:, 0:NE], onesf1[:], crow[:], True, True, (rconsts, rrouter))
        cp("V", cbc[:], pbk[6][:, 0:NE], (RB[6],), (rrouter, RB[6]))
        trans(RB[6], pbk[6][0:NE, 32:33], crow[:], ident16[0:1, 0:1], (rrouter, rconsts))
        cp("V", cvec_c[:], pbk[6][0:NE, 32:33], (RB[6],), (rrouter, RB[6]))
        ts1("V", lrow[:], prow[:], sc1[0:1, 8:9], ALU.add, (rrouter,), (rrouter,))
        pr.emit("S", lambda e: e.sign(lrow[:], lrow[:]), reads=(rrouter,), writes=(rrouter,))
        ts1("V", lrow[:], lrow[:], 1.0, ALU.add, (rrouter,), (rrouter,))
        tt("V", p2row[:], p2row[:], lrow[:], ALU.subtract, (rrouter,), (rrouter,))
        trans(RB[6], pbk[6][0:1, 40:48], iotac[0:NE, 0:1], ident16[0:NE, 0:NE], (rconsts,))
        cp("V", iorow[:], pbk[6][0:1, 40:48], (RB[6],), (rrouter, RB[6]))
        tt("V", lrow[:], lrow[:], iorow[:], ALU.mult, (rrouter,), (rrouter,))
        pr.emit("V", lambda e: e.reduce_sum(sc1[0:1, 10:11], lrow[:], axis=mybir.AxisListType.X),
                reads=(rrouter,), writes=(rrouter,))
        tt("V", p2row[:], p2row[:], iorow[:], ALU.mult, (rrouter,), (rrouter,))
        pr.emit("V", lambda e: e.reduce_sum(sc1[0:1, 11:12], p2row[:], axis=mybir.AxisListType.X),
                reads=(rrouter,), writes=(rrouter,))
        tt("V", sc1[0:1, 12:13], sc1[0:1, 4:5], sc1[0:1, 7:8], ALU.mult, (rrouter,), (rrouter,))
        tt("V", sc1[0:1, 13:14], sc1[0:1, 5:6], sc1[0:1, 7:8], ALU.mult, (rrouter,), (rrouter,))
        mm(RB[6], pbk[6][:, 48:52], onesf1[:], sc1[0:1, 10:14], True, True, (rconsts, rrouter))
        cp("V", cbs[:], pbk[6][:, 50:52], (RB[6],), (rrouter,))
        for j_ in range(2):
            stt("V", idxf[:, j_:j_ + 1], pbk[6][:, 48 + j_:49 + j_], 4096.0, iotac[:],
                ALU.mult, ALU.add, (RB[6], rconsts), (rrouter,))
            stt("V", ebase[:, j_:j_ + 1], pbk[6][:, 48 + j_:49 + j_], 128.0, iotac[:],
                ALU.mult, ALU.add, (RB[6], rconsts), (rrouter, RB[6]))
        cp("V", idxi[:, 0:2], ebase[:, 0:2], (rrouter,), (rrouter,))
        for j_ in range(2):
            pr.emit("P", lambda e, j=j_: e.indirect_dma_start(
                b1g[:, j * 32:(j + 1) * 32], None,
                d_b1.ap().rearrange("a p c -> (a p) c"),
                IndirectOffsetOnAxis(ap=idxi[:, j:j + 1], axis=0)),
                reads=(rrouter,), writes=(rrouter,), dma=True)
        pr.emit("V", lambda e: e.memset(b2comb[:], 0.0), writes=(rrouter,))
        for e_ in range(NE):
            for hc in range(8):
                stt("V", b2comb[:, hc:hc + 1], b2sb[:, e_ * 8 + hc:e_ * 8 + hc + 1],
                    cbc[:, e_:e_ + 1], b2comb[:, hc:hc + 1], ALU.mult, ALU.add,
                    (rconsts, rrouter), (rrouter,))

    def moe():
        w1rows = d_w1bf.ap().rearrange("a p c -> (a p) c")
        w2rows = d_w2bf.ap().rearrange("a p c -> (a p) c")
        idxq = [0]

        def idx8(es, fcg):
            for f8 in range(8):
                fc = fcg * 8 + f8
                ts1("V", idxt[:, f8:f8 + 1], idxf[:, es:es + 1], float(fc * P), ALU.add,
                    (rrouter,), (ridx[f8],))
                cp("V", idxi[:, f8:f8 + 1], idxt[:, f8:f8 + 1], (ridx[f8],), (ridx[f8],))

        def w1g1(es, fc):
            ws = fc % 2
            q = fc % 8
            pr.emit("P", lambda e, ws=ws, q=q: e.indirect_dma_start(
                w1buf[:, ws * 8 * P:(ws + 1) * 8 * P], None, w1rows,
                IndirectOffsetOnAxis(ap=idxi[:, q:q + 1], axis=0)),
                reads=(ridx[q],), writes=(rw1[ws],), dma=True)

        def w2g(es, fcg):
            for f8 in range(8):
                pr.emit("P", lambda e, f8=f8: e.indirect_dma_start(
                    w2buf[:, f8 * D:(f8 + 1) * D], None, w2rows,
                    IndirectOffsetOnAxis(ap=idxi[:, f8:f8 + 1], axis=0)),
                    reads=(ridx[f8],), writes=(rw2[f8],), dma=True)
                ts1("V", w2buf[:, f8 * D:(f8 + 1) * D], w2buf[:, f8 * D:(f8 + 1) * D],
                    cbs[:, es:es + 1], ALU.mult, (rrouter,), (rw2[f8],))

        blocks = [(es, fcg) for es in range(2) for fcg in range(4)]
        for bi, (es, fcg) in enumerate(blocks):
            idx8(es, fcg)
            for f8 in range(8):
                fc = fcg * 8 + f8
                ws = f8 % 2
                w1g1(es, fc)
                for mb in range(nmb):
                    bank = mb % 2
                    for k in range(8):
                        mm(RB[bank], pbk[bank][:, 0:MW],
                           w1buf[:, (ws * 8 + k) * P:(ws * 8 + k + 1) * P],
                           x1bf[:, k * S + mb * MW: k * S + (mb + 1) * MW],
                           k == 0, k == 7,
                           (rw1[ws],) + tuple(rx1bf[k][mb * MW // CH + j] for j in range(MW // CH)))
                    act("S", hmid[:, f8 * S + mb * MW: f8 * S + (mb + 1) * MW], pbk[bank][:, 0:MW],
                        AF.Gelu, b1g[:, es * 32 + (fc % 32):es * 32 + (fc % 32) + 1], 1.0,
                        (RB[bank], rrouter), (rhmid[f8][mb], RB[bank]))
            w2g(es, fcg)
            for mb in range(nmb):
                for hc in range(8):
                    bank = 2 + (mb * 8 + hc) % 2
                    for f8 in range(8):
                        mm(RB[bank], pbk[bank][:, 0:MW],
                           w2buf[:, f8 * D + hc * P: f8 * D + (hc + 1) * P],
                           hmid[:, f8 * S + mb * MW: f8 * S + (mb + 1) * MW],
                           f8 == 0, f8 == 7, (rw2[f8], rhmid[f8][mb]))
                    rxs = tuple(rX[hc][mb * MW // CH + j] for j in range(MW // CH))
                    tt("V", X[:, hc * S + mb * MW: hc * S + (mb + 1) * MW],
                       X[:, hc * S + mb * MW: hc * S + (mb + 1) * MW],
                       pbk[bank][:, 0:MW], ALU.add, rxs + (RB[bank],), rxs + (RB[bank],))

    def ln2_out():
        for mb in range(nmb):
            rxs = [tuple(rX[hc][mb * MW // CH + j] for j in range(MW // CH)) for hc in range(8)]
            for hc in range(8):
                ts1("V", X[:, hc * S + mb * MW: hc * S + (mb + 1) * MW],
                    X[:, hc * S + mb * MW: hc * S + (mb + 1) * MW],
                    b2comb[:, hc:hc + 1], ALU.add, rxs[hc] + (rrouter,), rxs[hc])
            for hc in range(8):
                cp("V", ybft[:], X[:, hc * S + mb * MW: hc * S + (mb + 1) * MW], rxs[hc], (rybft,))
                act("S", ln2sq[:], ybft[:], AF.Square, 0.0, 1.0, (rybft,), (rln2sq,))
                mm(RB[4], pbk[4][0:1, 0:MW], onesbf[:], ybft[:], hc == 0, hc == 7, (rybft,))
                mm(RB[5], pbk[5][0:1, 0:MW], onesbf[:], ln2sq[:], hc == 0, hc == 7, (rln2sq,))
            ts1("V", m_row[:], pbk[4][0:1, 0:MW], 1.0 / D, ALU.mult, (RB[4],), (rrows, RB[4]))
            ts1("V", v_row[:], pbk[5][0:1, 0:MW], 1.0 / D, ALU.mult, (RB[5],), (rrows, RB[5]))
            tt("V", r_row[0:1, 0:MW], m_row[:], m_row[:], ALU.mult, (rrows,), (rrows,))
            tt("V", v_row[:], v_row[:], r_row[0:1, 0:MW], ALU.subtract, (rrows,), (rrows,))
            act("S", r_row[0:1, 0:MW], v_row[:], AF.Ln, 1e-5, 1.0, (rrows,), (rrows,))
            act("S", r_row[0:1, 0:MW], r_row[0:1, 0:MW], AF.Exp, 0.0, -0.5, (rrows,), (rrows,))
            tt("V", r_row[0:1, MW:2 * MW], m_row[:], r_row[0:1, 0:MW], ALU.mult, (rrows,), (rrows,))
            mm(RB[6], pbk[6][:, 0:MW], onesf1[:], r_row[0:1, 0:MW], True, True, (rconsts, rrows))
            bank2 = 4
            mm(RB[bank2], pbk[bank2][:, 0:MW], onesf1[:], r_row[0:1, MW:2 * MW], True, True,
               (rconsts, rrows))
            for hc in range(8):
                ost = outst[:, 0:MW]
                tt("V", t4[:, 0:MW], X[:, hc * S + mb * MW: hc * S + (mb + 1) * MW], pbk[6][:, 0:MW],
                   ALU.mult, rxs[hc] + (RB[6],), (rt4,))
                tt("V", ost, t4[:, 0:MW], pbk[bank2][:, 0:MW], ALU.subtract,
                   (rt4, RB[bank2]), (routst[0],))
                dma("Y", d_out.ap()[hc * P:(hc + 1) * P, mb * MW:(mb + 1) * MW], ost,
                    routst[0], (routst[0],))

    for _ in p1_block(0):
        pass
    p1_gh(0)
    for c in range(nchunks):
        bg = [p1_block(c + 1)] if c + 1 < nchunks else [None]
        scan_chunk(c, bg)
        while bg[0] is not None:
            try:
                next(bg[0])
            except StopIteration:
                bg[0] = None
        p45_chunk(c)
        if c + 1 < nchunks:
            p1_gh(c + 1)
    router()
    pr.barrier()
    moe()
    ln2_out()

    DMAQ = {"Y": 4, "S": 5, "P": 6}
    waited = {4: set(), 5: set(), 6: set()}
    for eng in ENGS:
        for waits, fn, sem, amt in pr.q[eng]:
            for s, v in waits:
                if s >= 4:
                    waited[s].add(v)
    for eng, semidx in DMAQ.items():
        wl = sorted(waited[semidx])
        tot = 0
        last = 0
        q2 = []
        import bisect
        for waits, fn, sem, amt in pr.q[eng]:
            if sem == semidx:
                i = bisect.bisect_right(wl, tot) - 1
                if i >= 0 and wl[i] > last:
                    waits = [(semidx, wl[i])] + [x for x in waits if x != (semidx, wl[i])]
                    last = wl[i]
                tot += amt
            q2.append((waits, fn, sem, amt))
        pr.q[eng] = q2

    sems = {}
    with (
        nc.semaphore("semT") as sT,
        nc.semaphore("semV") as sV,
        nc.semaphore("semS") as sS,
        nc.semaphore("semP") as sP,
        nc.semaphore("dmaY") as dY,
        nc.semaphore("dmaA") as dA,
        nc.semaphore("dmaP") as dP,
        nc.Block() as block,
    ):
        semh = [sT, sV, sS, sP, dY, dA, dP]

        def run_queue(engobj, lst):
            for waits, fn, sem, amt in lst:
                for s, v in waits:
                    engobj.wait_ge(semh[s], v)
                inst = fn(engobj)
                if amt:
                    inst.then_inc(semh[sem], amt)

        @block.tensor
        def _(e):
            run_queue(e, pr.q["T"])

        @block.vector
        def _(e):
            run_queue(e, pr.q["V"])

        @block.scalar
        def _(e):
            run_queue(e, pr.q["S"])

        @block.gpsimd
        def _(e):
            run_queue(e, pr.q["P"])

        @block.sync
        def _(e):
            run_queue(e, pr.q["Y"])
            e.wait_ge(dY, pr.cnt[4])
            e.wait_ge(dA, pr.cnt[5])
            e.wait_ge(dP, pr.cnt[6])

    return nc


def host_prep(inputs, S=2048):
    f = np.float32
    W_qkv = np.asarray(inputs["L0_attn_W_qkv"], f)
    W_z = np.asarray(inputs["L0_attn_W_z"], f)
    W_b = np.asarray(inputs["L0_attn_W_b"], f)
    W_a = np.asarray(inputs["L0_attn_W_a"], f)
    W_out = np.asarray(inputs["L0_attn_W_out"], f)
    conv_w = np.asarray(inputs["L0_attn_conv_w"], f)
    base = {
        "wqkvT": np.ascontiguousarray(
            W_qkv.T.reshape(8, 128, 16, 128).transpose(2, 0, 1, 3).reshape(16 * 8, 128, 128)),
        "wzT": np.ascontiguousarray(
            W_z.T.reshape(8, 128, 8, 128).transpose(2, 0, 1, 3).reshape(8 * 8, 128, 128)),
        "woutT": np.ascontiguousarray(W_out.T.reshape(8, 128, 1024)),
        "wbT": np.ascontiguousarray(W_b.T.reshape(8, 128, 16)),
        "waT": np.ascontiguousarray(W_a.T.reshape(8, 128, 16)),
        "convw": np.ascontiguousarray(
            conv_w.reshape(2048, 4).reshape(16, 128, 4).transpose(1, 0, 2).reshape(128, 64)),
        "dtb": np.asarray(inputs["L0_attn_dt_bias"], f).reshape(16, 1),
        "alog": np.asarray(inputs["L0_attn_A_log"], f).reshape(16, 1),
        "normw": np.ascontiguousarray(np.tile(np.asarray(inputs["L0_attn_norm_weight"], f), 2).reshape(128, 1)),
        "gatew": np.ascontiguousarray(np.asarray(inputs["L0_ffn_gate_w"], f).reshape(8, 128, 8)),
        "w1p": np.ascontiguousarray(
            np.asarray(inputs["L0_ffn_w1"], f).reshape(8, 8, 128, 32, 128).transpose(0, 3, 2, 1, 4)
            .reshape(8 * 32, 128, 8 * 128)),
        "b1p": np.ascontiguousarray(np.asarray(inputs["L0_ffn_b1"], f).reshape(8, 32, 128).transpose(0, 2, 1)),
        "w2p": np.ascontiguousarray(np.asarray(inputs["L0_ffn_w2"], f).reshape(8 * 32, 128, 1024)),
        "b2p": np.ascontiguousarray(np.asarray(inputs["L0_ffn_b2"], f).reshape(8, 8, 128).transpose(2, 0, 1).reshape(128, 64)),
        "identf": np.eye(128, dtype=f),
        "ident16": np.eye(16, dtype=f),
        "maskL": np.tril(np.ones((128, 128), f), -1),
        "maskUs": np.triu(np.ones((128, 128), f), 1),
        "maskUi": np.triu(np.ones((128, 128), f), 0),
        "hm2": (np.arange(128)[:, None] // 64 == np.arange(2)[None, :]).astype(f),
        "ex2": (np.arange(128)[None, :] // 64 == np.arange(2)[:, None]).astype(f),
        "onesin": np.ones((128, 128), f),
        "cvals": np.broadcast_to(np.array([1.0, 1e-6, 1e-5, 0, 0, 0, 0, 0], f), (128, 8)).copy(),
        "onescol": np.ones((128, 1), f),
        "iotac": np.arange(128, dtype=f).reshape(128, 1),
    }
    return base


def kernel(**inputs):
    S = 2048
    x_all = np.asarray(inputs["L0_attn_hidden_states"], np.float32)
    base = host_prep(inputs, S)
    nc = build_program(S)
    in_maps = []
    for b in range(8):
        m = dict(base)
        m["xT"] = np.ascontiguousarray(x_all[b].T)
        in_maps.append(m)
    res = run_bass_kernel_spmd(nc, in_maps, core_ids=list(range(8)))
    out = np.stack([np.asarray(r["outT"]).T for r in res.results])
    return out.astype(np.float32)


# version 25
# speedup vs baseline: 1.1802x; 1.0159x over previous
import numpy as np
import ml_dtypes
import concourse.bass as bass
import concourse.mybir as mybir
from concourse.bass_utils import run_bass_kernel_spmd
from concourse.bass_core_rust import IndirectOffsetOnAxis

F32 = mybir.dt.float32
BF16 = mybir.dt.bfloat16
AF = mybir.ActivationFunctionType
ALU = mybir.AluOpType
BF = ml_dtypes.bfloat16

P = 128
CH = 128
NKH, NVH, DK, DV = 8, 16, 64, 64
D = 1024
FF = 4096
NE = 8

SEM_OF_ENG = {"T": 0, "V": 1, "S": 2, "P": 3}
DMA_SEM = {"Y": 4, "S": 5, "P": 6}
NSEM = 7
ENGS = ("T", "V", "S", "P", "Y")


class Res:
    __slots__ = ("w", "r")

    def __init__(self):
        self.w = None
        self.r = {}


class Op:
    __slots__ = ("sem", "val")

    def __init__(self, sem, val):
        self.sem = sem
        self.val = val


class Prog:
    def __init__(self):
        self.q = {e: [] for e in ENGS}
        self.cnt = [0] * NSEM
        self.wm = {e: [0] * NSEM for e in ENGS}

    def emit(self, eng, fn, reads=(), writes=(), dma=False, group_mid=False):
        need = {}

        def need_op(op):
            if op is None:
                return
            if eng == "T" and op.sem == SEM_OF_ENG["T"]:
                return
            v = self.cnt[op.sem] if op.sem >= 4 else op.val
            if need.get(op.sem, 0) < v:
                need[op.sem] = v

        for r in reads:
            need_op(r.w)
        if not group_mid:
            for w in writes:
                need_op(w.w)
                for op in w.r.values():
                    need_op(op)
        wm = self.wm[eng]
        waits = [(s, v) for s, v in sorted(need.items()) if wm[s] < v]
        for s, v in waits:
            wm[s] = v
        if dma:
            sem, amt = DMA_SEM[eng], 16
        else:
            sem, amt = SEM_OF_ENG[eng], 1
        self.cnt[sem] += amt
        op = Op(sem, self.cnt[sem])
        self.q[eng].append((waits, fn, sem, amt))
        for r in reads:
            r.r[eng] = op
        for w in writes:
            w.w = op
            w.r = {}
        return op

    def barrier(self):
        snap = list(self.cnt)
        for eng in ("T", "V", "S", "P", "Y"):
            wm = self.wm[eng]
            waits = [(s, v) for s, v in enumerate(snap) if v > 0 and wm[s] < v]
            for s, v in waits:
                wm[s] = v
            if eng == "Y":
                self.q[eng].append((waits, lambda e: e.nop(), 4, 0))
            else:
                sem = SEM_OF_ENG[eng]
                self.cnt[sem] += 1
                self.q[eng].append((waits, lambda e: e.nop(), sem, 1))


def build_program(S):
    nchunks = S // CH
    MW = min(512, S)
    nmb = S // MW
    nc = bass.Bass("TRN2", target_bir_lowering=False)
    pr = Prog()

    d_x = nc.dram_tensor("xT", [D, S], F32, kind="ExternalInput")
    d_wqkv = nc.dram_tensor("wqkvT", [16 * 8, P, P], F32, kind="ExternalInput")
    d_wz = nc.dram_tensor("wzT", [8 * 8, P, P], F32, kind="ExternalInput")
    d_wout = nc.dram_tensor("woutT", [8, P, D], F32, kind="ExternalInput")
    d_wb = nc.dram_tensor("wbT", [8, P, NVH], F32, kind="ExternalInput")
    d_wa = nc.dram_tensor("waT", [8, P, NVH], F32, kind="ExternalInput")
    d_conv = nc.dram_tensor("convw", [P, 64], F32, kind="ExternalInput")
    d_dt = nc.dram_tensor("dtb", [NVH, 1], F32, kind="ExternalInput")
    d_alog = nc.dram_tensor("alog", [NVH, 1], F32, kind="ExternalInput")
    d_nw = nc.dram_tensor("normw", [P, 1], F32, kind="ExternalInput")
    d_gate = nc.dram_tensor("gatew", [8, P, NE], F32, kind="ExternalInput")
    d_w1 = nc.dram_tensor("w1p", [NE * 32, P, 8 * P], F32, kind="ExternalInput")
    d_b1 = nc.dram_tensor("b1p", [NE, P, 32], F32, kind="ExternalInput")
    d_w2 = nc.dram_tensor("w2p", [NE * 32, P, D], F32, kind="ExternalInput")
    d_b2 = nc.dram_tensor("b2p", [P, 64], F32, kind="ExternalInput")
    d_identf = nc.dram_tensor("identf", [P, P], F32, kind="ExternalInput")
    d_ident16 = nc.dram_tensor("ident16", [16, 16], F32, kind="ExternalInput")
    d_maskL = nc.dram_tensor("maskL", [P, P], F32, kind="ExternalInput")
    d_maskUs = nc.dram_tensor("maskUs", [P, P], F32, kind="ExternalInput")
    d_maskUi = nc.dram_tensor("maskUi", [P, P], F32, kind="ExternalInput")
    d_hm2 = nc.dram_tensor("hm2", [P, 2], F32, kind="ExternalInput")
    d_ex2 = nc.dram_tensor("ex2", [2, P], F32, kind="ExternalInput")
    d_ones = nc.dram_tensor("onesin", [P, P], F32, kind="ExternalInput")
    d_cvals = nc.dram_tensor("cvals", [P, 8], F32, kind="ExternalInput")
    d_onescol = nc.dram_tensor("onescol", [P, 1], F32, kind="ExternalInput")
    d_iotac = nc.dram_tensor("iotac", [P, 1], F32, kind="ExternalInput")
    d_out = nc.dram_tensor("outT", [D, S], F32, kind="ExternalOutput")
    d_wqkv_bf = nc.dram_tensor("wqkv_bfi", [16 * 8, P, P], BF16, kind="Internal")
    d_wz_bf = nc.dram_tensor("wz_bfi", [8 * 8, P, P], BF16, kind="Internal")
    d_w1bf = nc.dram_tensor("w1bfi", [NE * 32, P, 8 * P], BF16, kind="Internal")
    d_w2bf = nc.dram_tensor("w2bfi", [NE * 32, P, D], BF16, kind="Internal")

    ctx = []

    def sbt(name, shape, dtype):
        t = nc.sbuf_tensor(name, shape, dtype)
        h = t.__enter__()
        ctx.append(t)
        return h

    def psb(name, shape, dtype):
        t = nc.psum_tensor(name, shape, dtype)
        h = t.__enter__()
        ctx.append(t)
        return h

    AR_COLS = 28744
    arena = sbt("arena", [P, AR_COLS], BF16)
    _off = [0]

    def acarve(n):
        o = _off[0]
        _off[0] = o + n
        assert _off[0] <= AR_COLS, f"arena overflow {_off[0]}"
        return arena[:, o:o + n]

    qkv = acarve(16 * 2 * CH)
    szr = acarve(8 * 2 * CH)
    xbfr = acarve(8 * 2 * CH)
    wqbuf = acarve(2 * 8 * P)
    wzbuf = acarve(2 * 8 * P)
    kcm = acarve(2 * 512)
    vcm = acarve(2 * 1024)
    sqt = acarve(1024)
    hnbf = acarve(1024)
    mixsb = acarve(2 * CH)
    mixtail = acarve(16 * 4)
    Upow = acarve(4 * 2 * P)
    Wpow = acarve(4 * 7 * P)
    Xch = acarve(4 * 2 * DV)
    Rp = acarve(4 * DV)
    Dtl = acarve(4 * DV)
    gam1 = acarve(4 * P)
    gam2 = acarve(4 * P)
    gam3 = acarve(4 * P)
    nkkL = acarve(2 * P)
    nkkU = acarve(2 * P)
    kqI = acarve(2 * P)
    Sbf = acarve(NVH * DV)
    obuf = acarve(2 * 1024)
    scan_end = _off[0]
    srepbf = acarve(8)
    excl_end = _off[0]
    _off[0] = 0
    hmid = acarve(8 * S)
    w1buf = acarve(2 * 8 * P)
    w2buf = acarve(8 * D)
    ybft = acarve(MW)
    ln2sq = acarve(MW)
    moe_end = _off[0]
    assert moe_end <= AR_COLS
    assert max(excl_end, moe_end) <= AR_COLS, (excl_end, moe_end)

    X = sbt("Xs", [P, 8 * S], F32)
    x1bf = sbt("x1bfs", [P, 8 * S], BF16)
    wout = sbt("wouts", [P, 8 * D], BF16)
    convw = sbt("convws", [P, 64], F32)
    convbuf = sbt("convbufs", [P, 2 * CH], F32)
    convtmp = sbt("convtmps", [P, CH], F32)
    thbuf = sbt("thbufs", [P, 2 * CH], F32)
    ubuf = sbt("ubufs", [NVH, 2 * CH], F32)
    gbuf = sbt("gbufs", [NVH, 2 * CH], F32)
    hbuf = sbt("hbufs", [NVH, 2 * CH], F32)
    gl1p = sbt("gl1ps", [NVH, CH], F32)
    gsA = sbt("gsAs", [NVH, CH], F32)
    gsB = sbt("gsBs", [NVH, CH], F32)
    gsC = sbt("gsCs", [NVH, CH], F32)
    dtcol = sbt("dtcols", [NVH, 1], F32)
    negea = sbt("negeas", [NVH, 1], F32)
    alogc = sbt("alogcs", [NVH, 1], F32)
    gt = sbt("gts", [P, 2 * 16], F32)
    ht = sbt("hts", [P, 2 * 16], F32)
    negbt = sbt("negbts", [P, 2 * 16], F32)
    expgt = sbt("expgts", [P, 2 * 16], F32)
    creps = sbt("crepss", [P, 4 * P], F32)
    tmpc = sbt("tmpcs", [P, 8 * P], F32)
    bccol = sbt("bccols", [DK, 4], F32)
    scol = sbt("scols", [P, 4], F32)
    t3 = sbt("t3s", [P, 4 * DV], F32)
    Sf = sbt("Sfs", [DK, NVH * DV], F32)
    htile = sbt("htiles", [P, 8 * P], F32)
    t4 = htile[:, 0:MW] if MW <= 8 * P else None
    assert MW <= 8 * P
    rsq = sbt("rsqs", [2, 512], F32)
    rstd2b = sbt("rstd2bs", [2, 512], BF16)
    rowsb = sbt("rowsbs", [1, 4 * MW], F32)
    m_row = rowsb[0:1, 2 * MW:3 * MW]
    v_row = rowsb[0:1, 3 * MW:4 * MW]
    r_row = rowsb[0:1, 0:2 * MW]
    identbf = sbt("identbfs", [P, P], BF16)
    identf = sbt("identfs", [P, P], F32)
    ident16 = sbt("ident16s", [16, 16], F32)
    maskL = sbt("maskLsb", [P, P], BF16)
    maskUs = sbt("maskUssb", [P, P], BF16)
    maskUi = sbt("maskUisb", [P, P], BF16)
    hm2 = sbt("hm2sb", [P, 2], BF16)
    ex2 = sbt("ex2sb", [2, P], BF16)
    onesbf = sbt("onesbfs", [P, 1], BF16)
    onesf1 = sbt("onesf1s", [1, P], F32)
    nwcol = sbt("nwcols", [P, 1], F32)
    cvals = sbt("cvalss", [P, 8], F32)
    wbT = sbt("wbTsb", [P, 8 * NVH], BF16)
    waT = sbt("waTsb", [P, 8 * NVH], BF16)
    gatesb = sbt("gatesbs", [P, 8 * NE], BF16)
    srep = sbt("sreps", [P, 8], F32)
    lcol = sbt("lcols", [NE, 1], F32)
    lrow = sbt("lrows", [1, NE], F32)
    prow = sbt("prows", [1, NE], F32)
    p2row = sbt("p2rows", [1, NE], F32)
    crow = sbt("crows", [1, NE], F32)
    sc1 = sbt("sc1s", [1, 16], F32)
    cvec_c = sbt("cvec_cs", [NE, 1], F32)
    cbc = sbt("cbcs", [P, NE], F32)
    iotac = sbt("iotacs", [P, 1], F32)
    ebase = sbt("ebases", [P, 2], F32)
    idxf = sbt("idxfs", [P, 2], F32)
    idxi = sbt("idxis", [P, 8], mybir.dt.int32)
    cbs = sbt("cbss", [P, 2], F32)
    b1g = sbt("b1gs", [P, 2 * 32], F32)
    iorow = sbt("iorows", [1, NE], F32)
    idxt = sbt("idxts", [P, 8], F32)
    b2comb = sbt("b2combs", [P, 8], F32)
    b1sb = sbt("b1sbs", [P, NE * 32], F32)
    b2sb = sbt("b2sbs", [P, 64], F32)
    outst = tmpc

    pbk = [psb(f"pbk{i}", [P, 512], F32) for i in range(7)]
    pbf = psb("pbkbf", [P, 1024], BF16)
    RB = [Res() for _ in range(8)]

    rX = [[Res() for _ in range(nchunks)] for _ in range(8)]
    rqkv = [[Res(), Res()] for _ in range(16)]
    rszr = [[Res(), Res()] for _ in range(8)]
    rxbfr = [[Res(), Res()] for _ in range(8)]
    rwq = [Res(), Res()]
    rwz = [Res(), Res()]
    rmix = [Res(), Res()]
    rconvb = [Res(), Res()]
    rcvt = Res()
    rth = [Res(), Res()]
    rmt = [Res() for _ in range(16)]
    rwout = Res()
    rconsts = Res()
    rmask = Res()
    rub = [Res(), Res()]
    rgb = [Res(), Res()]
    rhb = [Res(), Res()]
    rl1p = Res()
    rsA = Res()
    rsB = Res()
    rsC = Res()
    rgt = [Res(), Res()]
    rht = [Res(), Res()]
    rnegbt = [Res(), Res()]
    rexpgt = [Res(), Res()]
    rbcg = Res()
    rbch = Res()
    rgrowd = [Res(), Res()]
    rhrowd = [Res(), Res()]
    rtmpc = [[Res(), Res()] for _ in range(4)]
    rgam = [[Res(), Res(), Res()] for _ in range(4)]
    rgam3 = [Res() for _ in range(4)]
    rkkm = [[Res(), Res(), Res()] for _ in range(2)]
    rUp = [[Res(), Res()] for _ in range(4)]
    rWp = [[Res() for _ in range(7)] for _ in range(4)]
    rXch = [[Res(), Res()] for _ in range(4)]
    rRp = [Res() for _ in range(4)]
    rDtl = [Res() for _ in range(4)]
    rt3 = [Res() for _ in range(4)]
    rt4 = Res()
    rbccol = [Res() for _ in range(4)]
    rscol = [Res() for _ in range(4)]
    rbcg2 = [Res(), Res()]
    rbch2 = [Res(), Res()]
    rkcm = [Res(), Res()]
    rvcm = [Res(), Res()]
    robuf = [Res(), Res()]
    rSf = [Res() for _ in range(NVH)]
    rSbf = [Res() for _ in range(NVH)]
    rhtile = Res()
    rsqt = Res()
    rrsq = Res()
    rrstd = Res()
    rhn = Res()
    rrows = Res()
    rrouter = Res()
    rw1 = [Res(), Res()]
    ridx = [Res() for _ in range(8)]
    rw2 = [Res() for _ in range(8)]
    rhmid = [[Res() for _ in range(4)] for _ in range(8)]
    rybft = Res()
    rln2sq = Res()
    routst = [Res(), Res()]
    rstage = Res()
    rwqd = Res()
    rwzd = Res()
    rx1bf = [[Res() for _ in range(nchunks)] for _ in range(8)]

    def dma(eng, dst, src, w, rr=()):
        return pr.emit(eng, lambda e, d=dst, s=src: e.dma_start(d, s),
                       reads=tuple(rr), writes=(w,), dma=True)

    def mm(bres, out, lhs, rhs, start, stop, reads):
        pr.emit("T", lambda e, o=out, l=lhs, r=rhs, s=start, q=stop: e.matmul(o, l, r, start=s, stop=q),
                reads=tuple(reads), writes=(bres,), group_mid=not start)

    def trans(bres, out, in_, ident, reads):
        pr.emit("T", lambda e, o=out, i=in_, d=ident: e.transpose(o, i, d),
                reads=tuple(reads), writes=(bres,))

    def act(eng, out, in_, func, bias, scale, reads, writes):
        pr.emit(eng, lambda e, o=out, i=in_, f=func, b=bias, s=scale: e.activation(o, i, f, bias=b, scale=s),
                reads=tuple(reads) + (rcval,), writes=tuple(writes))

    def cp(eng, out, in_, reads, writes):
        if eng == "S":
            pr.emit(eng, lambda e, o=out, i=in_: e.copy(o, i), reads=tuple(reads), writes=tuple(writes))
        else:
            pr.emit(eng, lambda e, o=out, i=in_: e.tensor_copy(o, i), reads=tuple(reads), writes=tuple(writes))

    def ts1(eng, out, in0, s1, op0, reads, writes):
        pr.emit(eng, lambda e, o=out, i=in0, a=s1, x=op0: e.tensor_single_scalar(o, i, a, x),
                reads=tuple(reads), writes=tuple(writes))

    def ts2(eng, out, in0, s1, s2, op0, op1, reads, writes):
        pr.emit(eng, lambda e, o=out, i=in0, a=s1, b=s2, x=op0, y=op1: e.tensor_scalar(o, i, a, b, x, y),
                reads=tuple(reads), writes=tuple(writes))

    def tt(eng, out, in0, in1, op, reads, writes):
        pr.emit(eng, lambda e, o=out, a=in0, b=in1, x=op: e.tensor_tensor(o, a, b, x),
                reads=tuple(reads), writes=tuple(writes))

    def stt(eng, out, in0, s, in1, op0, op1, reads, writes):
        pr.emit(eng, lambda e, o=out, a=in0, c=s, b=in1, x=op0, y=op1: e.scalar_tensor_tensor(o, a, c, b, x, y),
                reads=tuple(reads), writes=tuple(writes))

    rcval = Res()
    CONST_VALS = [1.0, 1e-6, 1e-5, 0.0, 0.0, 0.0, 0.0, 0.0]
    dma("Y", cvals[:], d_cvals.ap(), rcval)
    for i, v in enumerate(CONST_VALS[:3]):
        nc.const_aps.register(F32, v, cvals[:, i:i + 1])
    dma("Y", ident16[:], d_ident16.ap(), rconsts)
    dma("Y", convw[:], d_conv.ap(), rconsts)
    dma("Y", dtcol[:], d_dt.ap(), rconsts)
    dma("Y", alogc[:], d_alog.ap(), rconsts)
    dma("Y", nwcol[:], d_nw.ap(), rconsts)
    dma("Y", b1sb[:].rearrange("p (e c) -> p e c", c=32), d_b1.ap().rearrange("e p c -> p e c"), rconsts)
    dma("Y", b2sb[:], d_b2.ap(), rconsts)
    dma("Y", onesf1[:], d_ones.ap()[0:1, :], rconsts)
    dma("Y", iotac[:], d_iotac.ap(), rconsts)
    dma("P", identbf[:], d_identf.ap(), rmask)
    dma("Y", identf[:], d_identf.ap(), rconsts)
    dma("P", maskL[:], d_maskL.ap(), rmask)
    dma("P", maskUs[:], d_maskUs.ap(), rmask)
    dma("P", maskUi[:], d_maskUi.ap(), rmask)
    dma("P", hm2[:], d_hm2.ap(), rmask)
    dma("P", ex2[:], d_ex2.ap(), rmask)
    dma("P", onesbf[:], d_onescol.ap(), rmask)
    dma("P", wbT[:].rearrange("p (k c) -> p k c", c=NVH), d_wb.ap().rearrange("k p c -> p k c"), rconsts)
    dma("P", waT[:].rearrange("p (k c) -> p k c", c=NVH), d_wa.ap().rearrange("k p c -> p k c"), rconsts)
    dma("P", gatesb[:].rearrange("p (k c) -> p k c", c=NE), d_gate.ap().rearrange("k p c -> p k c"), rconsts)
    dma("P", wout[:].rearrange("p (k c) -> p k c", c=D), d_wout.ap().rearrange("k p c -> p k c"), rwout)
    ts1("V", wout[:], wout[:], nwcol[:], ALU.mult, (rconsts, rwout), (rwout,))
    act("S", negea[:], alogc[:], AF.Exp, 0.0, 1.0, (rconsts,), (rconsts,))
    ts1("V", negea[:], negea[:], -1.0, ALU.mult, (rconsts,), (rconsts,))
    pr.emit("V", lambda e: e.memset(Sf[:], 0.0), writes=tuple(rSf))
    pr.emit("V", lambda e: e.memset(Sbf[:, :], 0.0), writes=tuple(rSbf))

    for kc in range(8):
        o = dma("Y", X[:, kc * S:(kc + 1) * S], d_x.ap()[kc * P:(kc + 1) * P, :], rX[kc][0])
        for c in range(nchunks):
            rX[kc][c].w = o

    dma("P", d_wqkv_bf.ap(), d_wqkv.ap(), rwqd)
    dma("P", d_wz_bf.ap(), d_wz.ap(), rwzd)
    rw1d = Res()
    rw2d = Res()
    for i in range(8):
        dma("P", d_w1bf.ap()[i * 32:(i + 1) * 32], d_w1.ap()[i * 32:(i + 1) * 32], rw1d)
        dma("P", d_w2bf.ap()[i * 32:(i + 1) * 32], d_w2.ap()[i * 32:(i + 1) * 32], rw2d)

    def p1_block(c):
        par = c % 2
        t0 = c * CH
        for kc in range(8):
            cp("S", xbfr[:, (kc * 2 + par) * CH:(kc * 2 + par + 1) * CH],
               X[:, kc * S + t0: kc * S + t0 + CH], (rX[kc][c],), (rxbfr[kc][par],))
            yield
        for cc in range(16):
            ws = cc % 2
            if ws == 0:
                for w2_ in range(2):
                    dma("Y", wqbuf[:, w2_ * 8 * P:(w2_ + 1) * 8 * P].rearrange("p (a c) -> p a c", c=P),
                        d_wqkv_bf.ap()[(cc + w2_) * 8:(cc + w2_ + 1) * 8].rearrange("a p c -> p a c"),
                        rwq[w2_], (rwqd,))
                    yield
            bank = cc % 2
            for k in range(8):
                mm(RB[bank], pbk[bank][:, 0:CH],
                   wqbuf[:, (ws * 8 + k) * P:(ws * 8 + k + 1) * P],
                   xbfr[:, (k * 2 + par) * CH:(k * 2 + par + 1) * CH],
                   k == 0, k == 7, (rwq[0], rwq[1], rxbfr[k][par]))
                yield
            mxs = mixsb[:, par * CH:(par + 1) * CH]
            cp("V", mxs, pbk[bank][:, 0:CH], (RB[bank],), (rmix[par], RB[bank]))
            yield
            cb = convbuf[:, par * CH:(par + 1) * CH]

            def wbc(j, wdt, cc=cc):
                return bass.AP(convw, cc * 4 + j, [[64, P], [0, wdt]])
            if cc % 2 == 0:
                tt("P", cb, mxs, wbc(3, CH), ALU.mult, (rmix[par], rconsts), (rconvb[par],))
                yield
                for j in (2, 1, 0):
                    sh = 3 - j
                    tt("P", convtmp[:, 0:CH - sh], mxs[:, 0:CH - sh], wbc(j, CH - sh), ALU.mult,
                       (rmix[par], rconsts), (rcvt,))
                    yield
                    tt("P", cb[:, sh:CH], cb[:, sh:CH], convtmp[:, 0:CH - sh], ALU.add,
                       (rcvt,), (rconvb[par],))
                    yield
                    if c > 0:
                        stt("V", cb[:, 0:sh], mixtail[:, cc * 4 + 3 - sh:cc * 4 + 3],
                            convw[:, cc * 4 + j:cc * 4 + j + 1], cb[:, 0:sh],
                            ALU.mult, ALU.add, (rmt[cc], rconsts), (rconvb[par],))
                        yield
                cp("P", mixtail[:, cc * 4:cc * 4 + 3], mxs[:, CH - 3:CH], (rmix[par],), (rmt[cc],))
                yield
            else:
                ts1("V", cb, mxs, convw[:, cc * 4 + 3:cc * 4 + 4], ALU.mult, (rmix[par], rconsts), (rconvb[par],))
                yield
                for j in (2, 1, 0):
                    sh = 3 - j
                    wj = convw[:, cc * 4 + j:cc * 4 + j + 1]
                    stt("V", cb[:, sh:CH], mxs[:, 0:CH - sh], wj, cb[:, sh:CH], ALU.mult, ALU.add,
                        (rmix[par], rconsts), (rconvb[par],))
                    yield
                    if c > 0:
                        stt("V", cb[:, 0:sh], mixtail[:, cc * 4 + 3 - sh:cc * 4 + 3], wj, cb[:, 0:sh],
                            ALU.mult, ALU.add, (rmt[cc], rconsts), (rconvb[par],))
                        yield
                cp("V", mixtail[:, cc * 4:cc * 4 + 3], mxs[:, CH - 3:CH], (rmix[par],), (rmt[cc],))
                yield
            thb = thbuf[:, par * CH:(par + 1) * CH]
            act("S", thb, cb, AF.Tanh, 0.0, 0.5, (rconvb[par],), (rth[par],))
            yield
            ts2("V", thb, thb, 1.0, 0.5, ALU.add, ALU.mult, (rth[par],), (rth[par],))
            yield
            tt("P", qkv[:, (cc * 2 + par) * CH:(cc * 2 + par + 1) * CH], cb, thb, ALU.mult,
               (rconvb[par], rth[par]), (rqkv[cc][par],))
            yield
        for zc in range(8):
            ws = zc % 2
            if ws == 0:
                for w2_ in range(2):
                    dma("Y", wzbuf[:, w2_ * 8 * P:(w2_ + 1) * 8 * P].rearrange("p (a c) -> p a c", c=P),
                        d_wz_bf.ap()[(zc + w2_) * 8:(zc + w2_ + 1) * 8].rearrange("a p c -> p a c"),
                        rwz[w2_], (rwzd,))
                    yield
            bank = zc % 2
            for k in range(8):
                mm(RB[bank], pbk[bank][:, 0:CH],
                   wzbuf[:, (ws * 8 + k) * P:(ws * 8 + k + 1) * P],
                   xbfr[:, (k * 2 + par) * CH:(k * 2 + par + 1) * CH],
                   k == 0, k == 7, (rwz[0], rwz[1], rxbfr[k][par]))
                yield
            thb = thbuf[:, par * CH:(par + 1) * CH]
            act("S", thb, pbk[bank][:, 0:CH], AF.Tanh, 0.0, 0.5, (RB[bank],), (rth[par],))
            yield
            ts2("V", thb, thb, 1.0, 0.5, ALU.add, ALU.mult, (rth[par],), (rth[par],))
            yield
            tt("V", szr[:, (zc * 2 + par) * CH:(zc * 2 + par + 1) * CH], pbk[bank][:, 0:CH], thb,
               ALU.mult, (RB[bank], rth[par]), (rszr[zc][par], RB[bank]))
            yield

    def p1_gh(c):
        par = c % 2
        for k in range(8):
            mm(RB[6], pbk[6][0:16, 0:CH], wbT[:, k * NVH:(k + 1) * NVH],
               xbfr[:, (k * 2 + par) * CH:(k * 2 + par + 1) * CH],
               k == 0, k == 7, (rconsts, rxbfr[k][par]))
        act("S", gsB[:], pbk[6][0:16, 0:CH], AF.Abs, 0.0, 1.0, (RB[6],), (rsB,))
        ts1("V", gsA[:], pbk[6][0:16, 0:CH], 0.0, ALU.min, (RB[6],), (rsA, RB[6]))
        act("S", gsB[:], gsB[:], AF.Exp, 0.0, -1.0, (rsB,), (rsB,))
        act("S", gsC[:], gsB[:], AF.Ln, 1.0, 1.0, (rsB,), (rsC,))
        tt("V", gsC[:], gsA[:], gsC[:], ALU.subtract, (rsA, rsC), (rsC,))
        for k in range(8):
            mm(RB[6], pbk[6][0:16, 0:CH], waT[:, k * NVH:(k + 1) * NVH],
               xbfr[:, (k * 2 + par) * CH:(k * 2 + par + 1) * CH],
               k == 0, k == 7, (rconsts, rxbfr[k][par]))
        ub = ubuf[:, par * CH:(par + 1) * CH]
        gb = gbuf[:, par * CH:(par + 1) * CH]
        hb = hbuf[:, par * CH:(par + 1) * CH]
        ts1("V", ub[0:16, :], pbk[6][0:16, 0:CH], dtcol[:], ALU.add, (RB[6], rconsts), (rub[par], RB[6]))
        act("S", gsA[:], ub[0:16, :], AF.Abs, 0.0, 1.0, (rub[par],), (rsA,))
        act("S", gsA[:], gsA[:], AF.Exp, 0.0, -1.0, (rsA,), (rsA,))
        act("S", gl1p[:], gsA[:], AF.Ln, 1.0, 1.0, (rsA,), (rl1p,))
        ts1("V", gsA[:], ub[0:16, :], 0.0, ALU.max, (rub[par],), (rsA,))
        tt("V", gsA[:], gsA[:], gl1p[:], ALU.add, (rsA, rl1p), (rsA,))
        ts1("V", gb[0:16, :], gsA[:], negea[:], ALU.mult, (rsA, rconsts), (rgb[par],))
        srcv, srcr = gb[0:16, :], rgb[par]
        dstv, dstr = gsA[:], rsA
        s = 1
        while s < CH:
            tt("V", dstv[:, s:CH], srcv[:, s:CH], srcv[:, 0:CH - s], ALU.add, (srcr,), (dstr,))
            cp("V", dstv[:, 0:s], srcv[:, 0:s], (srcr,), (dstr,))
            srcv, dstv = dstv, srcv
            srcr, dstr = dstr, srcr
            s *= 2
        cp("V", gb[0:16, :], gsA[:], (rsA,), (rgb[par],))
        tt("V", hb[0:16, :], gb[0:16, :], gsC[:], ALU.subtract, (rgb[par], rsC), (rhb[par],))
        trans(RB[6], pbk[6][:, 0:16], gb[0:16, :], ident16[:], (rgb[par], rconsts))
        cp("V", gt[:, par * 16:(par + 1) * 16], pbk[6][:, 0:16], (RB[6],), (rgt[par], RB[6]))
        trans(RB[6], pbk[6][:, 16:32], hb[0:16, :], ident16[:], (rhb[par], rconsts))
        cp("V", ht[:, par * 16:(par + 1) * 16], pbk[6][:, 16:32], (RB[6],), (rht[par], RB[6]))
        act("S", expgt[:, par * 16:(par + 1) * 16], gt[:, par * 16:(par + 1) * 16], AF.Exp, 0.0, 1.0,
            (rgt[par],), (rexpgt[par],))
        ts1("V", negbt[:, par * 16:(par + 1) * 16], expgt[:, par * 16:(par + 1) * 16], -1.0, ALU.mult,
            (rexpgt[par],), (rnegbt[par],))


    def scan_chunk(c, bg_gen):
        par = c % 2

        def kv_trans():
            for i in range(4):
                cc = 4 + i
                trans(RB[7], pbf[:, 0:P], qkv[:, (cc * 2 + par) * CH:(cc * 2 + par + 1) * CH], identbf[:],
                      (rqkv[cc][par], rmask))
                cp("V", kcm[:, par * 512 + i * P: par * 512 + (i + 1) * P], pbf[:, 0:P], (RB[7],), (rkcm[par], RB[7]))
                yield
            for i in range(8):
                cc = 8 + i
                trans(RB[7], pbf[:, 0:P], qkv[:, (cc * 2 + par) * CH:(cc * 2 + par + 1) * CH], identbf[:],
                      (rqkv[cc][par], rmask))
                cp("S" if i % 2 else "V", vcm[:, par * 1024 + i * P: par * 1024 + (i + 1) * P], pbf[:, 0:P],
                   (RB[7],), (rvcm[par], RB[7]))
                yield

        kvg = [kv_trans()]

        def scan_head(h, par):
            hp = h % 2
            h4 = h % 4
            kvp = (h // 2) % 2
            qq = (h // 4) % 2
            kh = h // 2
            gcol = gt[:, par * 16 + h:par * 16 + h + 1]
            hcol = ht[:, par * 16 + h:par * 16 + h + 1]
            Q = qkv[:, ((kh // 2) * 2 + par) * CH:((kh // 2) * 2 + par + 1) * CH][(kh % 2) * 64:(kh % 2) * 64 + 64, :]
            K = qkv[:, ((4 + kh // 2) * 2 + par) * CH:((4 + kh // 2) * 2 + par + 1) * CH][(kh % 2) * 64:(kh % 2) * 64 + 64, :]
            rq = rqkv[kh // 2][par]
            rk = rqkv[4 + kh // 2][par]
            sb_ = pbk[4 + hp]
            sbR = RB[4 + hp]
            B0 = (h4 // 2) * 256
            nkkLs = nkkL[:, kvp * P:(kvp + 1) * P]
            nkkUs = nkkU[:, kvp * P:(kvp + 1) * P]
            kqIs = kqI[:, kvp * P:(kvp + 1) * P]
            if h % 2 == 0:
                kks = pbk[2][:, h4 * P:(h4 + 1) * P]
                kqs = pbk[3][:, h4 * P:(h4 + 1) * P]
                mm(RB[2], kks, K, K, True, True, (rk,))
                yield
                mm(RB[3], kqs, K, Q, True, True, (rk, rq))
                yield
                stt("V", nkkLs, kks, -1.0, maskL[:], ALU.mult, ALU.mult,
                    (RB[2], rmask), (rkkm[kvp][0],))
                yield
                stt("V", nkkUs, kks, -1.0, maskUs[:], ALU.mult, ALU.mult,
                    (RB[2], rmask), (rkkm[kvp][1], RB[2]))
                yield
                stt("V", kqIs, kqs, 1.0, maskUi[:], ALU.mult, ALU.mult,
                    (RB[3], rmask), (rkkm[kvp][2], RB[3]))
                yield
            t1s = tmpc[:, h4 * 2 * P: h4 * 2 * P + P]
            t2s = tmpc[:, h4 * 2 * P + P: h4 * 2 * P + 2 * P]
            g1 = gam1[:, h4 * P:(h4 + 1) * P]
            g2 = gam2[:, h4 * P:(h4 + 1) * P]
            g3 = gam3[:, h4 * P:(h4 + 1) * P]
            crepH = creps[:, h4 * P:(h4 + 1) * P]
            bc1 = pbk[2][:, h4 * P:(h4 + 1) * P]
            bc2 = pbk[3][:, h4 * P:(h4 + 1) * P]
            cp("P", crepH, bass.AP(ht, par * 16 + h, [[32, P], [0, P]]), (rht[par],), (rtmpc[h4][0],))
            yield
            trans(RB[2], bc1, crepH, identf[:], (rtmpc[h4][0], rconsts))
            yield
            ts2("V", t1s, bc1, gcol, -30.0, ALU.subtract, ALU.max,
                (RB[2], rgt[par]), (rtmpc[h4][0], RB[2]))
            yield
            act("S", g1, t1s, AF.Exp, 0.0, -1.0, (rtmpc[h4][0],), (rgam[h4][0],))
            yield
            cp("P", crepH, bass.AP(gt, par * 16 + h, [[32, P], [0, P]]), (rgt[par], rtmpc[h4][0]), (rtmpc[h4][1],))
            yield
            trans(RB[3], bc2, crepH, identf[:], (rtmpc[h4][1], rconsts))
            yield
            act("S", bccol[:, h4:h4 + 1], bc2[0:DK, P - 1:P], AF.Exp, 0.0, 1.0, (RB[3],), (rbccol[h4],))
            yield
            ts2("V", t2s, bc2, hcol, 30.0, ALU.subtract, ALU.min,
                (RB[3], rht[par]), (rtmpc[h4][1], RB[3]))
            yield
            act("S", g2, t2s, AF.Exp, 0.0, 1.0, (rtmpc[h4][1],), (rgam[h4][1],))
            yield
            Up0 = Upow[:, h4 * 2 * P: h4 * 2 * P + P]
            Up1 = Upow[:, h4 * 2 * P + P: h4 * 2 * P + 2 * P]
            Wp = [Wpow[:, (h4 * 7 + lv) * P:(h4 * 7 + lv + 1) * P] for lv in range(7)]
            tt("P", Up0, g1, nkkLs, ALU.mult, (rgam[h4][0], rkkm[kvp][0]), (rUp[h4][0],))
            yield
            tt("P", Wp[0], g2, nkkUs, ALU.mult, (rgam[h4][1], rkkm[kvp][1]), (rWp[h4][0],))
            yield
            tt("P", g3, g2, kqIs, ALU.mult, (rgam[h4][1], rkkm[kvp][2]), (rgam3[h4],))
            yield
            sb0 = (kh % 2) * 64
            mm(sbR, sb_[:, B0:B0 + DV], K, Sbf[sb0:sb0 + DK, h * DV:(h + 1) * DV], True, True, (rk, rSbf[h]))
            yield
            RpS = Rp[:, h4 * DV:(h4 + 1) * DV]
            stt("V", RpS, sb_[:, B0:B0 + DV], negbt[:, par * 16 + h:par * 16 + h + 1],
                vcm[:, par * 1024 + h * DV: par * 1024 + (h + 1) * DV], ALU.mult, ALU.add,
                (sbR, rnegbt[par], rvcm[par]), (rRp[h4], sbR))
            yield
            xsrc, xres = RpS, rRp[h4]
            for lev in range(7):
                mm(sbR, sb_[:, B0 + 64:B0 + 64 + DV], Wp[lev], xsrc, True, True, (rWp[h4][lev], xres))
                yield
                dst = Xch[:, (h4 * 2 + lev % 2) * DV:(h4 * 2 + lev % 2 + 1) * DV]
                tt("V", dst, sb_[:, B0 + 64:B0 + 64 + DV], xsrc, ALU.add,
                   (sbR, xres), (rXch[h4][lev % 2], sbR))
                yield
                xsrc, xres = dst, rXch[h4][lev % 2]
                if lev < 6:
                    sp_ = Up0 if lev % 2 == 0 else Up1
                    dp_ = Up1 if lev % 2 == 0 else Up0
                    rsp, rdp = rUp[h4][lev % 2], rUp[h4][(lev + 1) % 2]
                    sq2 = pbk[2][:, h4 * P:(h4 + 1) * P]
                    sq3 = pbk[3][:, h4 * P:(h4 + 1) * P]
                    if lev < 5:
                        mm(RB[2], sq2, Wp[lev], sp_, True, True, (rsp, rWp[h4][lev]))
                        yield
                        cp("V" if lev % 2 else "S", dp_, sq2, (RB[2],), (rdp, RB[2]))
                        yield
                    mm(RB[3], sq3, sp_, Wp[lev], True, True, (rsp, rWp[h4][lev]))
                    yield
                    cp("S" if lev % 2 else "V", Wp[lev + 1], sq3, (RB[3],), (rWp[h4][lev + 1], RB[3]))
                    yield
            act("S", scol[:, h4:h4 + 1], t2s[:, P - 1:P], AF.Exp, 0.0, 1.0, (rtmpc[h4][1],), (rscol[h4],))
            yield
            DtlS = Dtl[:, h4 * DV:(h4 + 1) * DV]
            ts1("V", DtlS, xsrc, scol[:, h4:h4 + 1], ALU.mult, (xres, rscol[h4]), (rDtl[h4],))
            yield
            mm(sbR, sb_[:, B0 + 128:B0 + 128 + DV], Q, Sbf[sb0:sb0 + DK, h * DV:(h + 1) * DV], True, True,
               (rq, rSbf[h]))
            yield
            t3S = t3[:, h4 * DV:(h4 + 1) * DV]
            ts2("V", t3S, sb_[:, B0 + 128:B0 + 128 + DV], expgt[:, par * 16 + h:par * 16 + h + 1], 0.125,
                ALU.mult, ALU.mult, (sbR, rexpgt[par]), (rt3[h4], sbR))
            yield
            mm(sbR, sb_[:, B0 + 192:B0 + 192 + DV], g3, xsrc, True, True, (rgam3[h4], xres))
            yield
            stt("V", obuf[:, par * 1024 + h * DV: par * 1024 + (h + 1) * DV], sb_[:, B0 + 192:B0 + 192 + DV], 0.125,
                t3S, ALU.mult, ALU.add, (sbR, rt3[h4]), (robuf[par], sbR))
            yield
            mm(sbR, sb_[0:DK, B0:B0 + DV], kcm[:, par * 512 + kh * DK: par * 512 + (kh + 1) * DK],
               DtlS, True, True, (rkcm[par], rDtl[h4]))
            yield
            stt("V", Sf[:, h * DV:(h + 1) * DV], Sf[:, h * DV:(h + 1) * DV], bccol[:, h4:h4 + 1],
                sb_[0:DK, B0:B0 + DV], ALU.mult, ALU.add, (rSf[h], rbccol[h4], sbR), (rSf[h], sbR))
            yield
            cp("P", Sbf[0:DK, h * DV:(h + 1) * DV], Sf[:, h * DV:(h + 1) * DV], (rSf[h],), (rSbf[h],))
            yield
            cp("P", Sbf[DK:P, h * DV:(h + 1) * DV], Sf[:, h * DV:(h + 1) * DV], (rSf[h],), (rSbf[h],))
            yield


        pending = list(range(16))
        first = [scan_head(pending.pop(0), par), scan_head(pending.pop(0), par)]
        for g_ in first:
            for _ in range(8):
                next(g_)
        live = first + [scan_head(pending.pop(0), par), scan_head(pending.pop(0), par)]
        while live:
            for g_ in list(live):
                try:
                    next(g_)
                except StopIteration:
                    live.remove(g_)
                    if pending:
                        live.append(scan_head(pending.pop(0), par))
            if kvg[0] is not None:
                try:
                    next(kvg[0])
                except StopIteration:
                    kvg[0] = None
            if bg_gen[0] is not None:
                try:
                    next(bg_gen[0])
                except StopIteration:
                    bg_gen[0] = None

    def p45_chunk(c):
        par = c % 2
        t0 = c * CH
        for half in range(2):
            for fc4 in range(4):
                fc = half * 4 + fc4
                trans(RB[7], pbf[:, 0:P], obuf[:, par * 1024 + fc * P: par * 1024 + (fc + 1) * P],
                      identbf[:], (robuf[par], rmask))
                tt("V", htile[:, fc * P:(fc + 1) * P], pbf[:, 0:P],
                   szr[:, (fc * 2 + par) * CH:(fc * 2 + par + 1) * CH], ALU.mult,
                   (RB[7], rszr[fc][par]), (rhtile, RB[7]))
                act("S", sqt[:, fc * P:(fc + 1) * P], htile[:, fc * P:(fc + 1) * P], AF.Square, 0.0, 1.0,
                    (rhtile,), (rsqt,))
                mm(RB[6], pbk[6][0:2, fc4 * P:(fc4 + 1) * P], hm2[:], sqt[:, fc * P:(fc + 1) * P],
                   True, True, (rmask, rsqt))
            cp("V", rsq[0:2, 0:512], pbk[6][0:2, 0:512], (RB[6],), (rrsq, RB[6]))
            act("S", rsq[0:2, :], rsq[0:2, :], AF.Ln, 1e-6, 1.0 / DV, (rrsq,), (rrstd,))
            act("S", rsq[0:2, :], rsq[0:2, :], AF.Exp, 0.0, -0.5, (rrstd,), (rrstd,))
            cp("V", rstd2b[0:2, :], rsq[0:2, :], (rrstd,), (rrstd,))
            for fc4 in range(4):
                fc = half * 4 + fc4
                bank = fc % 2
                mm(RB[bank], pbk[bank][:, 0:P], ex2[:], rstd2b[0:2, fc4 * P:(fc4 + 1) * P], True, True,
                   (rmask, rrstd))
                tt("V", hnbf[:, fc * P:(fc + 1) * P], htile[:, fc * P:(fc + 1) * P], pbk[bank][:, 0:P],
                   ALU.mult, (rhtile, RB[bank]), (rhn, RB[bank]))
        for hc in range(8):
            bank = hc % 2
            for kc in range(8):
                mm(RB[bank], pbk[bank][:, 0:P], wout[:, kc * D + hc * P: kc * D + (hc + 1) * P],
                   hnbf[:, kc * P:(kc + 1) * P], kc == 0, kc == 7, (rwout, rhn))
            tt("V", X[:, hc * S + t0: hc * S + t0 + CH], X[:, hc * S + t0: hc * S + t0 + CH],
               pbk[bank][:, 0:P], ALU.add, (rX[hc][c], RB[bank]), (rX[hc][c], RB[bank]))
            cp("V", sqt[:, hc * P:(hc + 1) * P],
               X[:, hc * S + t0: hc * S + t0 + CH], (rX[hc][c],), (rsqt,))
        for hc in range(8):
            eng_ = "S" if hc % 2 else "V"
            if eng_ == "S":
                act("S", hnbf[:, hc * P:(hc + 1) * P], sqt[:, hc * P:(hc + 1) * P], AF.Square, 0.0, 1.0,
                    (rsqt,), (rhn,))
            else:
                tt("V", hnbf[:, hc * P:(hc + 1) * P], sqt[:, hc * P:(hc + 1) * P],
                   sqt[:, hc * P:(hc + 1) * P], ALU.mult, (rsqt,), (rhn,))
        for hc in range(8):
            mm(RB[6], pbk[6][0:1, 0:P], onesbf[:], sqt[:, hc * P:(hc + 1) * P], hc == 0, hc == 7,
               (rsqt,))
        ts1("V", m_row[0:1, 0:P], pbk[6][0:1, 0:P], 1.0 / D, ALU.mult, (RB[6],), (rrows, RB[6]))
        for hc in range(8):
            mm(RB[6], pbk[6][0:1, 0:P], onesbf[:], hnbf[:, hc * P:(hc + 1) * P], hc == 0, hc == 7,
               (rhn,))
        ts1("V", v_row[0:1, 0:P], pbk[6][0:1, 0:P], 1.0 / D, ALU.mult, (RB[6],), (rrows, RB[6]))
        tt("V", r_row[0:1, 0:P], m_row[0:1, 0:P], m_row[0:1, 0:P], ALU.mult, (rrows,), (rrows,))
        tt("V", v_row[0:1, 0:P], v_row[0:1, 0:P], r_row[0:1, 0:P], ALU.subtract, (rrows,), (rrows,))
        act("S", r_row[0:1, 0:P], v_row[0:1, 0:P], AF.Ln, 1e-5, 1.0, (rrows,), (rrows,))
        act("S", r_row[0:1, 0:P], r_row[0:1, 0:P], AF.Exp, 0.0, -0.5, (rrows,), (rrows,))
        tt("V", r_row[0:1, P:2 * P], m_row[0:1, 0:P], r_row[0:1, 0:P], ALU.mult, (rrows,), (rrows,))
        mm(RB[6], pbk[6][:, 0:2 * P], onesf1[:], r_row[0:1, 0:2 * P], True, True, (rconsts, rrows))
        for hc in range(8):
            tt("V", t4[:, 0:P], X[:, hc * S + t0: hc * S + t0 + CH], pbk[6][:, 0:P], ALU.mult,
               (rX[hc][c], RB[6]), (rt4,))
            tt("V", X[:, hc * S + t0: hc * S + t0 + CH], t4[:, 0:P], pbk[6][:, P:2 * P], ALU.subtract,
               (rt4, RB[6]), (rX[hc][c],))
            cp("S", x1bf[:, hc * S + t0: hc * S + t0 + CH], X[:, hc * S + t0: hc * S + t0 + CH],
               (rX[hc][c],), (rx1bf[hc][c],))

    def router():
        allx = [rX[kc][c] for kc in range(8) for c in range(nchunks)]
        for kc in range(8):
            pr.emit("V", lambda e, kc=kc: e.reduce_sum(srep[:, kc:kc + 1], X[:, kc * S:(kc + 1) * S],
                                                       axis=mybir.AxisListType.X),
                    reads=tuple(rX[kc]), writes=(rrouter,))
        ts1("V", srep[:], srep[:], 1.0 / S, ALU.mult, (rrouter,), (rrouter,))
        cp("V", srepbf[:, 0:8], srep[:], (rrouter,), (rrouter,))
        for kc in range(8):
            mm(RB[6], pbk[6][0:NE, 0:1], gatesb[:, kc * NE:(kc + 1) * NE], srepbf[:, kc:kc + 1],
               kc == 0, kc == 7, (rconsts, rrouter))
        cp("V", lcol[:], pbk[6][0:NE, 0:1], (RB[6],), (rrouter, RB[6]))
        trans(RB[6], pbk[6][0:1, 0:NE], lcol[:], ident16[0:NE, 0:NE], (rrouter, rconsts))
        cp("V", lrow[:], pbk[6][0:1, 0:NE], (RB[6],), (rrouter, RB[6]))
        pr.emit("V", lambda e: e.reduce_max(sc1[0:1, 0:1], lrow[:], axis=mybir.AxisListType.X),
                reads=(rrouter,), writes=(rrouter,))
        ts1("V", sc1[0:1, 1:2], sc1[0:1, 0:1], -1.0, ALU.mult, (rrouter,), (rrouter,))
        act("S", prow[:], lrow[:], AF.Exp, sc1[0:1, 1:2], 1.0, (rrouter,), (rrouter,))
        pr.emit("V", lambda e: e.reduce_sum(sc1[0:1, 2:3], prow[:], axis=mybir.AxisListType.X),
                reads=(rrouter,), writes=(rrouter,))
        pr.emit("V", lambda e: e.reciprocal(sc1[0:1, 3:4], sc1[0:1, 2:3]),
                reads=(rrouter,), writes=(rrouter,))
        ts1("V", prow[:], prow[:], sc1[0:1, 3:4], ALU.mult, (rrouter,), (rrouter,))
        pr.emit("V", lambda e: e.reduce_max(sc1[0:1, 4:5], prow[:], axis=mybir.AxisListType.X),
                reads=(rrouter,), writes=(rrouter,))
        ts1("V", sc1[0:1, 8:9], sc1[0:1, 4:5], -1.0, ALU.mult, (rrouter,), (rrouter,))
        ts1("V", p2row[:], prow[:], sc1[0:1, 8:9], ALU.add, (rrouter,), (rrouter,))
        pr.emit("S", lambda e: e.sign(p2row[:], p2row[:]), reads=(rrouter,), writes=(rrouter,))
        ts1("V", p2row[:], p2row[:], 1.0, ALU.add, (rrouter,), (rrouter,))
        stt("V", crow[:], p2row[:], -2.0, prow[:], ALU.mult, ALU.add, (rrouter,), (rrouter,))
        pr.emit("V", lambda e: e.reduce_max(sc1[0:1, 5:6], crow[:], axis=mybir.AxisListType.X),
                reads=(rrouter,), writes=(rrouter,))
        ts1("V", sc1[0:1, 9:10], sc1[0:1, 5:6], -1.0, ALU.mult, (rrouter,), (rrouter,))
        ts1("V", crow[:], crow[:], sc1[0:1, 9:10], ALU.add, (rrouter,), (rrouter,))
        pr.emit("S", lambda e: e.sign(crow[:], crow[:]), reads=(rrouter,), writes=(rrouter,))
        ts1("V", crow[:], crow[:], 1.0, ALU.add, (rrouter,), (rrouter,))
        tt("V", p2row[:], p2row[:], crow[:], ALU.add, (rrouter,), (rrouter,))
        tt("V", crow[:], prow[:], p2row[:], ALU.mult, (rrouter,), (rrouter,))
        pr.emit("V", lambda e: e.reduce_sum(sc1[0:1, 6:7], crow[:], axis=mybir.AxisListType.X),
                reads=(rrouter,), writes=(rrouter,))
        pr.emit("V", lambda e: e.reciprocal(sc1[0:1, 7:8], sc1[0:1, 6:7]),
                reads=(rrouter,), writes=(rrouter,))
        ts1("V", crow[:], crow[:], sc1[0:1, 7:8], ALU.mult, (rrouter,), (rrouter,))
        mm(RB[6], pbk[6][:, 0:NE], onesf1[:], crow[:], True, True, (rconsts, rrouter))
        cp("V", cbc[:], pbk[6][:, 0:NE], (RB[6],), (rrouter, RB[6]))
        trans(RB[6], pbk[6][0:NE, 32:33], crow[:], ident16[0:1, 0:1], (rrouter, rconsts))
        cp("V", cvec_c[:], pbk[6][0:NE, 32:33], (RB[6],), (rrouter, RB[6]))
        ts1("V", lrow[:], prow[:], sc1[0:1, 8:9], ALU.add, (rrouter,), (rrouter,))
        pr.emit("S", lambda e: e.sign(lrow[:], lrow[:]), reads=(rrouter,), writes=(rrouter,))
        ts1("V", lrow[:], lrow[:], 1.0, ALU.add, (rrouter,), (rrouter,))
        tt("V", p2row[:], p2row[:], lrow[:], ALU.subtract, (rrouter,), (rrouter,))
        trans(RB[6], pbk[6][0:1, 40:48], iotac[0:NE, 0:1], ident16[0:NE, 0:NE], (rconsts,))
        cp("V", iorow[:], pbk[6][0:1, 40:48], (RB[6],), (rrouter, RB[6]))
        tt("V", lrow[:], lrow[:], iorow[:], ALU.mult, (rrouter,), (rrouter,))
        pr.emit("V", lambda e: e.reduce_sum(sc1[0:1, 10:11], lrow[:], axis=mybir.AxisListType.X),
                reads=(rrouter,), writes=(rrouter,))
        tt("V", p2row[:], p2row[:], iorow[:], ALU.mult, (rrouter,), (rrouter,))
        pr.emit("V", lambda e: e.reduce_sum(sc1[0:1, 11:12], p2row[:], axis=mybir.AxisListType.X),
                reads=(rrouter,), writes=(rrouter,))
        tt("V", sc1[0:1, 12:13], sc1[0:1, 4:5], sc1[0:1, 7:8], ALU.mult, (rrouter,), (rrouter,))
        tt("V", sc1[0:1, 13:14], sc1[0:1, 5:6], sc1[0:1, 7:8], ALU.mult, (rrouter,), (rrouter,))
        mm(RB[6], pbk[6][:, 48:52], onesf1[:], sc1[0:1, 10:14], True, True, (rconsts, rrouter))
        cp("V", cbs[:], pbk[6][:, 50:52], (RB[6],), (rrouter,))
        for j_ in range(2):
            stt("V", idxf[:, j_:j_ + 1], pbk[6][:, 48 + j_:49 + j_], 4096.0, iotac[:],
                ALU.mult, ALU.add, (RB[6], rconsts), (rrouter,))
            stt("V", ebase[:, j_:j_ + 1], pbk[6][:, 48 + j_:49 + j_], 128.0, iotac[:],
                ALU.mult, ALU.add, (RB[6], rconsts), (rrouter, RB[6]))
        cp("V", idxi[:, 0:2], ebase[:, 0:2], (rrouter,), (rrouter,))
        for j_ in range(2):
            pr.emit("P", lambda e, j=j_: e.indirect_dma_start(
                b1g[:, j * 32:(j + 1) * 32], None,
                d_b1.ap().rearrange("a p c -> (a p) c"),
                IndirectOffsetOnAxis(ap=idxi[:, j:j + 1], axis=0)),
                reads=(rrouter,), writes=(rrouter,), dma=True)
        pr.emit("V", lambda e: e.memset(b2comb[:], 0.0), writes=(rrouter,))
        for e_ in range(NE):
            for hc in range(8):
                stt("V", b2comb[:, hc:hc + 1], b2sb[:, e_ * 8 + hc:e_ * 8 + hc + 1],
                    cbc[:, e_:e_ + 1], b2comb[:, hc:hc + 1], ALU.mult, ALU.add,
                    (rconsts, rrouter), (rrouter,))

    def moe():
        w1rows = d_w1bf.ap().rearrange("a p c -> (a p) c")
        w2rows = d_w2bf.ap().rearrange("a p c -> (a p) c")
        idxq = [0]

        def idx8(es, fcg):
            for f8 in range(8):
                fc = fcg * 8 + f8
                ts1("V", idxt[:, f8:f8 + 1], idxf[:, es:es + 1], float(fc * P), ALU.add,
                    (rrouter,), (ridx[f8],))
                cp("V", idxi[:, f8:f8 + 1], idxt[:, f8:f8 + 1], (ridx[f8],), (ridx[f8],))

        def w1g1(es, fc):
            ws = fc % 2
            q = fc % 8
            pr.emit("P", lambda e, ws=ws, q=q: e.indirect_dma_start(
                w1buf[:, ws * 8 * P:(ws + 1) * 8 * P], None, w1rows,
                IndirectOffsetOnAxis(ap=idxi[:, q:q + 1], axis=0)),
                reads=(ridx[q],), writes=(rw1[ws],), dma=True)

        def w2g(es, fcg):
            for f8 in range(8):
                pr.emit("P", lambda e, f8=f8: e.indirect_dma_start(
                    w2buf[:, f8 * D:(f8 + 1) * D], None, w2rows,
                    IndirectOffsetOnAxis(ap=idxi[:, f8:f8 + 1], axis=0)),
                    reads=(ridx[f8],), writes=(rw2[f8],), dma=True)
                ts1("V", w2buf[:, f8 * D:(f8 + 1) * D], w2buf[:, f8 * D:(f8 + 1) * D],
                    cbs[:, es:es + 1], ALU.mult, (rrouter,), (rw2[f8],))

        blocks = [(es, fcg) for es in range(2) for fcg in range(4)]
        for bi, (es, fcg) in enumerate(blocks):
            idx8(es, fcg)
            for f8 in range(8):
                fc = fcg * 8 + f8
                ws = f8 % 2
                w1g1(es, fc)
                for mb in range(nmb):
                    bank = mb % 2
                    for k in range(8):
                        mm(RB[bank], pbk[bank][:, 0:MW],
                           w1buf[:, (ws * 8 + k) * P:(ws * 8 + k + 1) * P],
                           x1bf[:, k * S + mb * MW: k * S + (mb + 1) * MW],
                           k == 0, k == 7,
                           (rw1[ws],) + tuple(rx1bf[k][mb * MW // CH + j] for j in range(MW // CH)))
                    act("S", hmid[:, f8 * S + mb * MW: f8 * S + (mb + 1) * MW], pbk[bank][:, 0:MW],
                        AF.Gelu, b1g[:, es * 32 + (fc % 32):es * 32 + (fc % 32) + 1], 1.0,
                        (RB[bank], rrouter), (rhmid[f8][mb], RB[bank]))
            w2g(es, fcg)
            for mb in range(nmb):
                for hc in range(8):
                    bank = 2 + (mb * 8 + hc) % 2
                    for f8 in range(8):
                        mm(RB[bank], pbk[bank][:, 0:MW],
                           w2buf[:, f8 * D + hc * P: f8 * D + (hc + 1) * P],
                           hmid[:, f8 * S + mb * MW: f8 * S + (mb + 1) * MW],
                           f8 == 0, f8 == 7, (rw2[f8], rhmid[f8][mb]))
                    rxs = tuple(rX[hc][mb * MW // CH + j] for j in range(MW // CH))
                    tt("V", X[:, hc * S + mb * MW: hc * S + (mb + 1) * MW],
                       X[:, hc * S + mb * MW: hc * S + (mb + 1) * MW],
                       pbk[bank][:, 0:MW], ALU.add, rxs + (RB[bank],), rxs + (RB[bank],))

    def ln2_out():
        for mb in range(nmb):
            rxs = [tuple(rX[hc][mb * MW // CH + j] for j in range(MW // CH)) for hc in range(8)]
            for hc in range(8):
                ts1("V", X[:, hc * S + mb * MW: hc * S + (mb + 1) * MW],
                    X[:, hc * S + mb * MW: hc * S + (mb + 1) * MW],
                    b2comb[:, hc:hc + 1], ALU.add, rxs[hc] + (rrouter,), rxs[hc])
            for hc in range(8):
                cp("V", ybft[:], X[:, hc * S + mb * MW: hc * S + (mb + 1) * MW], rxs[hc], (rybft,))
                act("S", ln2sq[:], ybft[:], AF.Square, 0.0, 1.0, (rybft,), (rln2sq,))
                mm(RB[4], pbk[4][0:1, 0:MW], onesbf[:], ybft[:], hc == 0, hc == 7, (rybft,))
                mm(RB[5], pbk[5][0:1, 0:MW], onesbf[:], ln2sq[:], hc == 0, hc == 7, (rln2sq,))
            ts1("V", m_row[:], pbk[4][0:1, 0:MW], 1.0 / D, ALU.mult, (RB[4],), (rrows, RB[4]))
            ts1("V", v_row[:], pbk[5][0:1, 0:MW], 1.0 / D, ALU.mult, (RB[5],), (rrows, RB[5]))
            tt("V", r_row[0:1, 0:MW], m_row[:], m_row[:], ALU.mult, (rrows,), (rrows,))
            tt("V", v_row[:], v_row[:], r_row[0:1, 0:MW], ALU.subtract, (rrows,), (rrows,))
            act("S", r_row[0:1, 0:MW], v_row[:], AF.Ln, 1e-5, 1.0, (rrows,), (rrows,))
            act("S", r_row[0:1, 0:MW], r_row[0:1, 0:MW], AF.Exp, 0.0, -0.5, (rrows,), (rrows,))
            tt("V", r_row[0:1, MW:2 * MW], m_row[:], r_row[0:1, 0:MW], ALU.mult, (rrows,), (rrows,))
            mm(RB[6], pbk[6][:, 0:MW], onesf1[:], r_row[0:1, 0:MW], True, True, (rconsts, rrows))
            bank2 = 4
            mm(RB[bank2], pbk[bank2][:, 0:MW], onesf1[:], r_row[0:1, MW:2 * MW], True, True,
               (rconsts, rrows))
            for hc in range(8):
                ost = outst[:, 0:MW]
                tt("V", t4[:, 0:MW], X[:, hc * S + mb * MW: hc * S + (mb + 1) * MW], pbk[6][:, 0:MW],
                   ALU.mult, rxs[hc] + (RB[6],), (rt4,))
                tt("V", ost, t4[:, 0:MW], pbk[bank2][:, 0:MW], ALU.subtract,
                   (rt4, RB[bank2]), (routst[0],))
                dma("Y", d_out.ap()[hc * P:(hc + 1) * P, mb * MW:(mb + 1) * MW], ost,
                    routst[0], (routst[0],))

    for _ in p1_block(0):
        pass
    p1_gh(0)
    for c in range(nchunks):
        bg = [p1_block(c + 1)] if c + 1 < nchunks else [None]
        scan_chunk(c, bg)
        while bg[0] is not None:
            try:
                next(bg[0])
            except StopIteration:
                bg[0] = None
        p45_chunk(c)
        if c + 1 < nchunks:
            p1_gh(c + 1)
    router()
    pr.barrier()
    moe()
    ln2_out()

    DMAQ = {"Y": 4, "S": 5, "P": 6}
    waited = {4: set(), 5: set(), 6: set()}
    for eng in ENGS:
        for waits, fn, sem, amt in pr.q[eng]:
            for s, v in waits:
                if s >= 4:
                    waited[s].add(v)
    for eng, semidx in DMAQ.items():
        wl = sorted(waited[semidx])
        tot = 0
        last = 0
        q2 = []
        import bisect
        for waits, fn, sem, amt in pr.q[eng]:
            if sem == semidx:
                i = bisect.bisect_right(wl, tot) - 1
                if i >= 0 and wl[i] > last:
                    waits = [(semidx, wl[i])] + [x for x in waits if x != (semidx, wl[i])]
                    last = wl[i]
                tot += amt
            q2.append((waits, fn, sem, amt))
        pr.q[eng] = q2

    sems = {}
    with (
        nc.semaphore("semT") as sT,
        nc.semaphore("semV") as sV,
        nc.semaphore("semS") as sS,
        nc.semaphore("semP") as sP,
        nc.semaphore("dmaY") as dY,
        nc.semaphore("dmaA") as dA,
        nc.semaphore("dmaP") as dP,
        nc.Block() as block,
    ):
        semh = [sT, sV, sS, sP, dY, dA, dP]

        def run_queue(engobj, lst):
            for waits, fn, sem, amt in lst:
                for s, v in waits:
                    engobj.wait_ge(semh[s], v)
                inst = fn(engobj)
                if amt:
                    inst.then_inc(semh[sem], amt)

        @block.tensor
        def _(e):
            run_queue(e, pr.q["T"])

        @block.vector
        def _(e):
            run_queue(e, pr.q["V"])

        @block.scalar
        def _(e):
            run_queue(e, pr.q["S"])

        @block.gpsimd
        def _(e):
            run_queue(e, pr.q["P"])

        @block.sync
        def _(e):
            run_queue(e, pr.q["Y"])
            e.wait_ge(dY, pr.cnt[4])
            e.wait_ge(dA, pr.cnt[5])
            e.wait_ge(dP, pr.cnt[6])

    return nc


def host_prep(inputs, S=2048):
    f = np.float32
    W_qkv = np.asarray(inputs["L0_attn_W_qkv"], f)
    W_z = np.asarray(inputs["L0_attn_W_z"], f)
    W_b = np.asarray(inputs["L0_attn_W_b"], f)
    W_a = np.asarray(inputs["L0_attn_W_a"], f)
    W_out = np.asarray(inputs["L0_attn_W_out"], f)
    conv_w = np.asarray(inputs["L0_attn_conv_w"], f)
    base = {
        "wqkvT": np.ascontiguousarray(
            W_qkv.T.reshape(8, 128, 16, 128).transpose(2, 0, 1, 3).reshape(16 * 8, 128, 128)),
        "wzT": np.ascontiguousarray(
            W_z.T.reshape(8, 128, 8, 128).transpose(2, 0, 1, 3).reshape(8 * 8, 128, 128)),
        "woutT": np.ascontiguousarray(W_out.T.reshape(8, 128, 1024)),
        "wbT": np.ascontiguousarray(W_b.T.reshape(8, 128, 16)),
        "waT": np.ascontiguousarray(W_a.T.reshape(8, 128, 16)),
        "convw": np.ascontiguousarray(
            conv_w.reshape(2048, 4).reshape(16, 128, 4).transpose(1, 0, 2).reshape(128, 64)),
        "dtb": np.asarray(inputs["L0_attn_dt_bias"], f).reshape(16, 1),
        "alog": np.asarray(inputs["L0_attn_A_log"], f).reshape(16, 1),
        "normw": np.ascontiguousarray(np.tile(np.asarray(inputs["L0_attn_norm_weight"], f), 2).reshape(128, 1)),
        "gatew": np.ascontiguousarray(np.asarray(inputs["L0_ffn_gate_w"], f).reshape(8, 128, 8)),
        "w1p": np.ascontiguousarray(
            np.asarray(inputs["L0_ffn_w1"], f).reshape(8, 8, 128, 32, 128).transpose(0, 3, 2, 1, 4)
            .reshape(8 * 32, 128, 8 * 128)),
        "b1p": np.ascontiguousarray(np.asarray(inputs["L0_ffn_b1"], f).reshape(8, 32, 128).transpose(0, 2, 1)),
        "w2p": np.ascontiguousarray(np.asarray(inputs["L0_ffn_w2"], f).reshape(8 * 32, 128, 1024)),
        "b2p": np.ascontiguousarray(np.asarray(inputs["L0_ffn_b2"], f).reshape(8, 8, 128).transpose(2, 0, 1).reshape(128, 64)),
        "identf": np.eye(128, dtype=f),
        "ident16": np.eye(16, dtype=f),
        "maskL": np.tril(np.ones((128, 128), f), -1),
        "maskUs": np.triu(np.ones((128, 128), f), 1),
        "maskUi": np.triu(np.ones((128, 128), f), 0),
        "hm2": (np.arange(128)[:, None] // 64 == np.arange(2)[None, :]).astype(f),
        "ex2": (np.arange(128)[None, :] // 64 == np.arange(2)[:, None]).astype(f),
        "onesin": np.ones((128, 128), f),
        "cvals": np.broadcast_to(np.array([1.0, 1e-6, 1e-5, 0, 0, 0, 0, 0], f), (128, 8)).copy(),
        "onescol": np.ones((128, 1), f),
        "iotac": np.arange(128, dtype=f).reshape(128, 1),
    }
    return base


def kernel(**inputs):
    S = 2048
    x_all = np.asarray(inputs["L0_attn_hidden_states"], np.float32)
    base = host_prep(inputs, S)
    nc = build_program(S)
    in_maps = []
    for b in range(8):
        m = dict(base)
        m["xT"] = np.ascontiguousarray(x_all[b].T)
        in_maps.append(m)
    res = run_bass_kernel_spmd(nc, in_maps, core_ids=list(range(8)))
    out = np.stack([np.asarray(r["outT"]).T for r in res.results])
    return out.astype(np.float32)
